# version 08
# speedup vs baseline: 1.5302x; 1.0083x over previous
import math
from contextlib import ExitStack

import numpy as np
import ml_dtypes
import concourse.bass as bass
import concourse.mybir as mybir
from concourse.bass_isa import ReduceOp
from concourse.bass_utils import run_bass_kernel_spmd

F32 = mybir.dt.float32
BF16 = mybir.dt.bfloat16
F16 = mybir.dt.float16
I32 = mybir.dt.int32
AF = mybir.ActivationFunctionType
ALU = mybir.AluOpType
AX = mybir.AxisListType

C = 8
D = 2048
H = 16
DH = 128
HID = 5632
GU = 2 * HID
P = 128
LN2 = math.log(2.0)
C1 = math.log2(3.0) - 1.5


class Sem:
    def __init__(self, nc, name):
        self.h = nc.alloc_semaphore(name)
        self.n = 0

    def inc(self, instr, k=1):
        instr.then_inc(self.h, k)
        self.n += k
        return instr



def _rope_tables(pos):
    i = np.arange(0, DH, 2)
    freqs = np.exp(-math.log(10000.0) * i / DH).astype(np.float32)
    ang = (freqs[:, None] * pos.astype(np.float32)[None, :]).astype(np.float32)
    ang64 = ang.astype(np.float64)
    sin2 = np.repeat(np.sin(ang64), 2, axis=0).astype(np.float32)
    cos2 = np.repeat(np.cos(ang64), 2, axis=0).astype(np.float32)
    return np.ascontiguousarray(sin2), np.ascontiguousarray(cos2)


def _masks(c):
    out = np.zeros((8, P, P), dtype=np.float32)
    kl = np.arange(P)[:, None]
    ql = np.arange(P)[None, :]
    for m in range(8):
        if m < c:
            out[m] = 1.0
        elif m == c:
            out[m] = (kl <= ql).astype(np.float32)
    return out.astype(ml_dtypes.bfloat16)


def _rotmat():
    M = np.zeros((DH, DH), dtype=np.float32)
    for i in range(0, DH, 2):
        M[i, i + 1] = -1.0
        M[i + 1, i] = 1.0
    return np.ascontiguousarray(M.T).astype(ml_dtypes.bfloat16)



def build(J=8, depth=2, tap=None):
    nc = bass.Bass("TRN2", target_bir_lowering=False)
    L = J * P
    NKC = 8 * J

    sems = {}
    _uid = [0]

    def U(name):
        _uid[0] += 1
        return f"{name}_{_uid[0]}"

    def sem(name):
        if name not in sems:
            sems[name] = Sem(nc, name)
        return sems[name]

    x_in = nc.dram_tensor("x_in", [L, D], F32, kind="ExternalInput")
    sin_in = nc.dram_tensor("sin_in", [DH, L], F32, kind="ExternalInput")
    cos_in = nc.dram_tensor("cos_in", [DH, L], F32, kind="ExternalInput")
    mask_in = nc.dram_tensor("mask_in", [8, P, P], BF16, kind="ExternalInput")
    wq0_s = nc.dram_tensor("wq0_s", [3 * D // C, D], F32, kind="ExternalInput")
    wo0_s = nc.dram_tensor("wo0_s", [D // C, D], F32, kind="ExternalInput")
    w1_s = nc.dram_tensor("w1_s", [HID // C, D], F32, kind="ExternalInput")
    w3_s = nc.dram_tensor("w3_s", [HID // C, D], F32, kind="ExternalInput")
    w2_s = nc.dram_tensor("w2_s", [D // C, HID], F32, kind="ExternalInput")
    wq1_s = nc.dram_tensor("wq1_s", [3 * D // C, D], F32, kind="ExternalInput")
    wo1_s = nc.dram_tensor("wo1_s", [D // C, D], F32, kind="ExternalInput")
    gup_s = nc.dram_tensor("gup_s", [GU // C, D // 8], I32, kind="ExternalInput")
    gus_s = nc.dram_tensor("gus_s", [D // P, GU // C], F16, kind="ExternalInput")
    dnp_s = nc.dram_tensor("dnp_s", [D // C, HID // 8], I32, kind="ExternalInput")
    dns_s = nc.dram_tensor("dns_s", [HID // P, D // C], F16, kind="ExternalInput")
    y_out = nc.dram_tensor("y_out", [L, D], F32, kind="ExternalOutput")

    ag = {}
    ag_shapes = (
        ("wq0", 3 * D // C, D), ("wo0", D // C, D), ("w1", HID // C, D),
        ("w3", HID // C, D), ("w2", D // C, HID), ("wq1", 3 * D // C, D),
        ("wo1", D // C, D), ("wgu", GU // C, D), ("wdn", D // C, HID))
    for name, rows, cols in ag_shapes:
        i = nc.dram_tensor(f"agi_{name}", [rows, cols], BF16, kind="Internal")
        o = nc.dram_tensor(f"ago_{name}", [C * rows, cols], BF16, kind="Internal")
        ag[name] = (i, o)

    tsum_i = nc.dram_tensor("tsum_i", [1, 4], F32, kind="Internal")
    tsum_o = nc.dram_tensor("tsum_o", [1, 4], F32, kind="Internal")
    psum_scr = nc.dram_tensor("psum_scr", [P, 4], F32, kind="Internal")
    rep_scr = nc.dram_tensor("rep_scr", [P, 4], F32, kind="Internal")

    kvt = []
    for l in range(depth):
        ki = nc.dram_tensor(f"kvKi{l}", [D, L], BF16, kind="Internal")
        ko = nc.dram_tensor(f"kvKo{l}", [C, D, L], BF16, kind="Internal")
        vi = nc.dram_tensor(f"kvVi{l}", [16, P, L], BF16, kind="Internal")
        vo = nc.dram_tensor(f"kvVo{l}", [C, 16, P, L], BF16, kind="Internal")
        kvt.append((ki, ko, vi, vo))

    q_stash = nc.dram_tensor("q_stash", [P, 16 * L], BF16, kind="Internal")
    xn_stash = nc.dram_tensor("xn_stash", [L, D], BF16, kind="Internal")
    xnT_stash = nc.dram_tensor("xnT_stash", [P, 16 * L], BF16, kind="Internal")
    hT_stash = nc.dram_tensor("hT_stash", [P, (HID // P) * L], BF16, kind="Internal")

    ident_c = nc.inline_tensor(np.eye(P, dtype=ml_dtypes.bfloat16), "ident_c")
    onec_c = nc.inline_tensor(np.ones((P, 1), dtype=ml_dtypes.bfloat16), "onec_c")
    oner_c = nc.inline_tensor(np.ones((1, P), dtype=ml_dtypes.bfloat16), "oner_c")
    onerf_c = nc.inline_tensor(np.ones((1, P), dtype=np.float32), "onerf_c")
    rot_cst = nc.inline_tensor(_rotmat(), "rot_cst")

    with ExitStack() as ctx:
        e = ctx.enter_context
        block = e(nc.Block())

        x_res = e(nc.sbuf_tensor("x_res", [P, J * D], F32))
        id_s = e(nc.sbuf_tensor("id_s", [P, P], BF16))
        rot_s = e(nc.sbuf_tensor("rot_s", [P, P], BF16))
        mask_s = e(nc.sbuf_tensor("mask_s", [P, 8 * P], BF16))
        onec_s = e(nc.sbuf_tensor("onec_s", [P, 1], BF16))
        oner_s = e(nc.sbuf_tensor("oner_s", [1, P], BF16))
        onerf_s = e(nc.sbuf_tensor("onerf_s", [1, P], F32))
        rope_s = e(nc.sbuf_tensor("rope_s", [P, 2 * L], F32))
        sin_s = rope_s[:, 0:L]
        cos_s = rope_s[:, L:2 * L]
        sml = e(nc.sbuf_tensor("sml", [P, 64], F32))

        s_ld = sem("ld")
        s_g = sem("gd")
        s_t = sem("t")
        s_v = sem("v")
        s_a = sem("a")
        s_w = sem("w")
        s_kv = sem("kvd")
        s_st = sem("st")
        s_wp = [sem("wpa"), sem("wpb")]
        s_sp = [sem("spa"), sem("spb")]
        s_hp = [sem("hpa"), sem("hpb")]
        s_o = sem("o")
        bar = nc.alloc_semaphore("bar")
        barn = [0]

        def barrier():
            barn[0] += 5
            tgt = barn[0]
            for name in ("sync", "scalar", "tensor", "vector", "gpsimd"):
                def f(eng, tgt=tgt, name=name):
                    if name in ("scalar", "vector"):
                        eng.drain()
                    eng.sem_inc(bar, 1)
                    eng.wait_ge(bar, tgt)
                getattr(block, name)(f)

        def tap_out(src_ap, ncols, wait_sems=()):
            nt = ncols // D
            def f(gpsimd):
                for ss, vv in wait_sems:
                    gpsimd.wait_ge(ss.h, vv)
                dst = y_out.ap().rearrange("(t p) d -> p t d", p=P)
                s_o.inc(gpsimd.dma_start(
                    out=dst[:, 0:nt, :],
                    in_=src_ap.rearrange("p (t d) -> p t d", d=D)), 16)
                gpsimd.wait_ge(s_o.h, s_o.n)
            block.gpsimd(f)

        s_c0 = sem("ldc")

        def ph0(sync):
            xr = x_in.ap().rearrange("(t p) d -> p t d", p=P)
            s_c0.inc(sync.dma_start(
                out=x_res.ap().rearrange("p (t d) -> p t d", d=D), in_=xr), 16)
            s_c0.inc(sync.dma_start(out=id_s[:, :], in_=ident_c[:, :]), 16)
            s_c0.inc(sync.dma_start(out=rot_s[:, :], in_=rot_cst[:, :]), 16)
            for m in range(8):
                s_c0.inc(sync.dma_start(out=mask_s[:, m * P:(m + 1) * P],
                                        in_=mask_in[m, :, :]), 16)
            s_c0.inc(sync.dma_start(out=onec_s[:, :], in_=onec_c[:, :]), 16)
            s_c0.inc(sync.dma_start(out=oner_s[:, :], in_=oner_c[:, :]), 16)
            s_c0.inc(sync.dma_start(out=onerf_s[:, :], in_=onerf_c[:, :]), 16)
            s_c0.inc(sync.dma_start(out=sin_s, in_=sin_in[:, :]), 16)
            s_c0.inc(sync.dma_start(out=cos_s, in_=cos_in[:, :]), 16)
        block.sync(ph0)
        consts_done = s_c0.n

        ag_sem = {name: sem(f"ag_{name}") for name, _, _ in ag_shapes}
        ar_sem = sem("ar_tern")

        def ph1a(gpsimd):
            g0 = s_g.n
            for name, src in (("wq0", wq0_s), ("wo0", wo0_s),
                              ("wq1", wq1_s), ("wo1", wo1_s)):
                s_g.inc(gpsimd.dma_start(out=ag[name][0][:, :], in_=src[:, :]), 16)
            gpsimd.wait_ge(s_g.h, s_g.n)
            for name in ("wq0", "wo0", "wq1", "wo1"):
                ag_sem[name].inc(gpsimd.collective_compute(
                    "AllGather", ALU.bypass, replica_groups=[list(range(C))],
                    ins=[ag[name][0].ap().opt()], outs=[ag[name][1].ap().opt()]))
        block.gpsimd(ph1a)

        with nc.sbuf_tensor("pL", [P, HID], F32) as pL, \
             nc.sbuf_tensor("pI", [P, HID], I32) as pI, \
             nc.sbuf_tensor("pOT", [P, HID], BF16) as pOT, \
             nc.sbuf_tensor("pF", [P, 6 * 2048], F32) as pF, \
             nc.sbuf_tensor("pOb", [P, 2 * 2048], BF16) as pOb, \
             nc.sbuf_tensor("sc16", [64, P], F16) as sc16, \
             nc.sbuf_tensor("sc16b", [64, P], BF16) as sc16b, \
             nc.sbuf_tensor("scT_all", [P, 264], F32) as scT_all, \
             nc.sbuf_tensor("pSt", [4, P], F32) as pSt, \
             nc.sbuf_tensor("pSr", [4, 1], F32) as pSr, \
             nc.psum_tensor("psT", [P, 64], BF16) as psT:

            Fb = [pF[:, i * 2048:(i + 1) * 2048] for i in range(6)]
            Ob = [pOb[:, i * 2048:(i + 1) * 2048] for i in range(2)]

            tern = [("w1", w1_s, HID // C, D), ("w3", w3_s, HID // C, D),
                    ("w2", w2_s, D // C, HID)]
            tiles_t = []
            for mi, (name, src, rows, cols) in enumerate(tern):
                r0 = 0
                while r0 < rows:
                    nr = min(P, rows - r0)
                    tiles_t.append((mi, name, src, r0, nr, cols))
                    r0 += nr
            nts = [sum(1 for tt in tiles_t if tt[0] == mi) for mi in range(3)]
            offs = [0, nts[0], nts[0] + nts[1], nts[0] + nts[1] + nts[2]]

            s_lt1 = sem("lt1")
            v0 = s_v.n

            def t_loads(sync):
                for k, (mi, name, src, r0, nr, cols) in enumerate(tiles_t):
                    if k >= 1:
                        sync.wait_ge(s_v.h, v0 + k + 1)
                    s_lt1.inc(sync.dma_start(out=pL[0:nr, 0:cols],
                                             in_=src[r0:r0 + nr, :]), 16)

            def t_abs(vector):
                vector.memset(sml[:, 0:len(tiles_t)], 0.0)
                vector.drain()
                s_v.inc(vector.engine_nop())
                for k, (mi, name, src, r0, nr, cols) in enumerate(tiles_t):
                    vector.wait_ge(s_lt1.h, 16 * (k + 1))
                    s_v.inc(vector.tensor_reduce(
                        sml[0:nr, k:k + 1], pL[0:nr, 0:cols], AX.X, ALU.add,
                        apply_absolute_value=True))
                vector.drain()
                for mi in range(3):
                    vector.tensor_reduce(sml[:, 32 + mi:33 + mi],
                                         sml[:, offs[mi]:offs[mi + 1]],
                                         AX.X, ALU.add)
                s_v.inc(vector.memset(sml[:, 35:36], 0.0))
            block.sync(t_loads)
            block.vector(t_abs)
            absums_done = s_v.n

            s_vr = sem("vred")

            def t_pred1(gpsimd):
                gpsimd.wait_ge(s_v.h, absums_done)
                s_g.inc(gpsimd.dma_start(out=psum_scr[:, :], in_=sml[:, 32:36]), 16)
                gpsimd.wait_ge(s_g.h, s_g.n)
                s_g.inc(gpsimd.dma_start(
                    out=pSt[:, :], in_=psum_scr.ap().rearrange("p f -> f p"),
                    allow_slow_non_contiguous=True), 16)
                gpsimd.wait_ge(s_g.h, s_g.n)
            block.gpsimd(t_pred1)
            g_tr = s_g.n

            def t_red(vector):
                vector.wait_ge(s_g.h, g_tr)
                s_vr.inc(vector.tensor_reduce(pSr[:, :], pSt[:, :], AX.X, ALU.add))
            block.vector(t_red)

            def t_pred2(gpsimd):
                gpsimd.wait_ge(s_vr.h, s_vr.n)
                s_g.inc(gpsimd.dma_start(
                    out=tsum_i.ap()[0:1, 0:4].rearrange("o f -> f o"),
                    in_=pSr[:, 0:1]), 16)
                gpsimd.wait_ge(s_g.h, s_g.n)
                ar_sem.inc(gpsimd.collective_compute(
                    "AllReduce", ALU.add, replica_groups=[list(range(C))],
                    ins=[tsum_i.ap().opt()], outs=[tsum_o.ap().opt()]))
                gpsimd.wait_ge(ar_sem.h, ar_sem.n)
                repf = rep_scr.ap().rearrange("p f -> (p f)")
                s_g.inc(gpsimd.dma_start(out=repf[0:4], in_=tsum_o.ap().rearrange("o f -> (o f)")), 16)
                gpsimd.wait_ge(s_g.h, s_g.n)
                n = 4
                while n < 512:
                    s_g.inc(gpsimd.dma_start(out=repf[n:2 * n], in_=repf[0:n]), 16)
                    gpsimd.wait_ge(s_g.h, s_g.n)
                    n *= 2
                s_g.inc(gpsimd.dma_start(out=sml[:, 48:52], in_=rep_scr[:, :]), 16)
                gpsimd.wait_ge(s_g.h, s_g.n)
                s_g.inc(gpsimd.dma_start(out=sml[0:1, 44:48], in_=tsum_o[:, :]), 16)
                gpsimd.wait_ge(s_g.h, s_g.n)
            block.gpsimd(t_pred2)
            pred_done = s_g.n

            def t_scale(vector):
                vector.wait_ge(s_g.h, pred_done)
                for mi, (name, src, rows, cols) in enumerate(tern):
                    n_el = float(rows * C * cols)
                    vector.tensor_scalar(sml[:, 52 + mi:53 + mi],
                                         sml[:, 48 + mi:49 + mi],
                                         1.0 / n_el, 1e-5, ALU.mult, ALU.max)
                vector.drain()
                s_v.inc(vector.reciprocal(sml[:, 56:59], sml[:, 52:55]))
            block.vector(t_scale)
            scale_ready = s_v.n

            s_lt2 = sem("lt2")
            s_gs = sem("gs")
            v1 = s_v.n

            def t2_loads(sync):
                sync.wait_ge(s_v.h, scale_ready)
                for k, (mi, name, src, r0, nr, cols) in enumerate(tiles_t):
                    if k >= 1:
                        sync.wait_ge(s_v.h, v1 + k)
                    s_lt2.inc(sync.dma_start(out=pL[0:nr, 0:cols],
                                             in_=src[r0:r0 + nr, :]), 16)

            def t2_q(vector):
                for k, (mi, name, src, r0, nr, cols) in enumerate(tiles_t):
                    vector.wait_ge(s_lt2.h, 16 * (k + 1))
                    if k >= 1:
                        vector.wait_ge(s_gs.h, 16 * k)
                    li = None
                    for c0 in range(0, cols, 2048):
                        cw = min(2048, cols - c0)
                        vector.tensor_scalar(Fb[0][0:nr, 0:cw], pL[0:nr, c0:c0 + cw],
                                             sml[0:nr, 56 + mi:57 + mi], None,
                                             ALU.mult)
                        vector.drain()
                        vector.tensor_copy(pI[0:nr, c0:c0 + cw], Fb[0][0:nr, 0:cw])
                        vector.drain()
                        li = vector.tensor_scalar(pOT[0:nr, c0:c0 + cw], pI[0:nr, c0:c0 + cw],
                                             1, -1, ALU.min, ALU.max)
                    s_v.inc(li)

            def t2_st(gpsimd):
                for k, (mi, name, src, r0, nr, cols) in enumerate(tiles_t):
                    gpsimd.wait_ge(s_v.h, v1 + k + 1)
                    s_gs.inc(gpsimd.dma_start(
                        out=ag[name][0][r0:r0 + nr, :],
                        in_=pOT[0:nr, 0:cols]), 16)
            block.sync(t2_loads)
            block.vector(t2_q)
            block.gpsimd(t2_st)
            tern_stores = s_gs.n

            def t_ag(gpsimd):
                gpsimd.wait_ge(s_gs.h, tern_stores)
                for name in ("w1", "w3", "w2"):
                    ag_sem[name].inc(gpsimd.collective_compute(
                        "AllGather", ALU.bypass, replica_groups=[list(range(C))],
                        ins=[ag[name][0].ap().opt()], outs=[ag[name][1].ap().opt()]))
            block.gpsimd(t_ag)

            fp4 = [("wgu", gup_s, gus_s, GU // C, D),
                   ("wdn", dnp_s, dns_s, D // C, HID)]
            tiles_f = [(name, pk, sc, t, cols)
                       for name, pk, sc, rows, cols in fp4
                       for t in range(rows // P)]
            scT_off = []
            off = 0
            for name, pk, sc, t, cols in tiles_f:
                scT_off.append(off)
                off += cols // P
            chunks_per = [(cols + 2047) // 2048 for _, _, _, _, cols in tiles_f]

            s_lf = sem("lf")
            s_gf = sem("gf")
            t2c = s_t.n
            a2 = s_a.n

            v_base = s_v.n
            v_cvt, v_f, v_chunk_done, v_tile_done = [], [], [], []
            vv = v_base
            ci_total = 0
            for k in range(len(tiles_f)):
                vv += 1
                v_cvt.append(vv)
                vv += 1
                fs, ds = [], []
                for c in range(chunks_per[k]):
                    vv += 1
                    fs.append(vv)
                    vv += 1
                    ds.append(vv)
                v_f.append(fs)
                v_chunk_done.append(ds)
                v_tile_done.append(vv)
                ci_total += chunks_per[k]

            def f_loads(sync):
                sync.wait_ge(s_v.h, v1 + len(tiles_t))
                for k, (name, pk, sc, t, cols) in enumerate(tiles_f):
                    if k >= 1:
                        sync.wait_ge(s_gf.h, 16 * k)
                    s_lf.inc(sync.dma_start(
                        out=pI[:, 0:cols // 8],
                        in_=pk[t * P:(t + 1) * P, :]), 16)
                    s_lf.inc(sync.dma_start(
                        out=sc16[0:cols // P, :],
                        in_=sc[:, t * P:(t + 1) * P]), 16)
            block.sync(f_loads)

            def f_sct(tensor):
                tensor.wait_ge(s_c0.h, consts_done)
                for k, (name, pk, sc, t, cols) in enumerate(tiles_f):
                    ng = cols // P
                    tensor.wait_ge(s_v.h, v_cvt[k])
                    s_t.inc(tensor.transpose(psT[:, 0:ng], sc16b[0:ng, :],
                                             id_s[0:ng, 0:ng]))
            block.tensor(f_sct)

            def f_dec_v(vector):
                vector.wait_ge(s_gs.h, tern_stores)
                for k, (name, pk, sc, t, cols) in enumerate(tiles_f):
                    ng = cols // P
                    vector.wait_ge(s_lf.h, 32 * (k + 1))
                    if k >= 1:
                        vector.wait_ge(s_gf.h, 16 * k)
                    s_v.inc(vector.tensor_copy(sc16b[0:ng, :], sc16[0:ng, :]))
                    vector.wait_ge(s_t.h, t2c + k + 1)
                    s_v.inc(vector.tensor_copy(
                        scT_all[:, scT_off[k]:scT_off[k] + ng], psT[:, 0:ng]))
                    for cidx, c0 in enumerate(range(0, cols, 2048)):
                        cw = min(2048, cols - c0)
                        cw8 = cw // 8
                        i0 = c0 // 8
                        ia, ib = 2048, 2048 + 1024
                        for nk in range(8):
                            vector.tensor_scalar(
                                pI[:, ia:ia + cw8], pI[:, i0:i0 + cw8],
                                4 * nk, 15, ALU.logical_shift_right,
                                ALU.bitwise_and)
                            vector.drain()
                            vector.tensor_copy(Fb[0][:, nk:cw:8],
                                               pI[:, ia:ia + cw8])
                            vector.tensor_scalar(
                                pI[:, ib:ib + cw8], pI[:, ia:ia + cw8],
                                1, None, ALU.bitwise_and)
                            vector.drain()
                            vector.tensor_copy(Fb[1][:, nk:cw:8],
                                               pI[:, ib:ib + cw8])
                        An, Bm = Fb[0][:, 0:cw], Fb[1][:, 0:cw]
                        F2, F3, F4, F5 = (Fb[2][:, 0:cw], Fb[3][:, 0:cw],
                                          Fb[4][:, 0:cw], Fb[5][:, 0:cw])
                        Sg = Ob[0][:, 0:cw]
                        vector.tensor_scalar(F2, An, 8.0, None, ALU.is_ge)
                        vector.drain()
                        vector.tensor_scalar(Sg, F2, -2.0, 1.0, ALU.mult, ALU.add)
                        vector.tensor_scalar(F3, An, 0.5, -1.0, ALU.mult, ALU.add)
                        vector.drain()
                        vector.scalar_tensor_tensor(F4, F2, -4.0, F3,
                                                    ALU.mult, ALU.add)
                        vector.drain()
                        vector.scalar_tensor_tensor(F5, Bm, C1, F4,
                                                    ALU.mult, ALU.add)
                        vector.tensor_scalar(F2, F4, -0.25, None, ALU.is_le)
                        vector.tensor_scalar(F3, F4, 396.83008, 197.83008,
                                             ALU.mult, ALU.add)
                        vector.drain()
                        vector.tensor_tensor(F4, F2, F3, ALU.mult)
                        vector.drain()
                        s_v.inc(vector.tensor_tensor(F2, F5, F4, ALU.add))
                        vector.wait_ge(s_a.h, a2 + sum(chunks_per[:k]) + cidx + 1)
                        vector.tensor_tensor(Ob[1][:, 0:cw], F3, Sg, ALU.mult)
                        vector.drain()
                        ng0 = scT_off[k] + c0 // P
                        li = None
                        for g in range(cw // P):
                            li = vector.tensor_scalar(
                                pOT[:, c0 + g * P:c0 + (g + 1) * P],
                                Ob[1][:, g * P:(g + 1) * P],
                                scT_all[:, ng0 + g:ng0 + g + 1], None, ALU.mult)
                        s_v.inc(li)
            block.vector(f_dec_v)

            def f_dec_a(scalar):
                for k, (name, pk, sc, t, cols) in enumerate(tiles_f):
                    for cidx, c0 in enumerate(range(0, cols, 2048)):
                        cw = min(2048, cols - c0)
                        scalar.wait_ge(s_v.h, v_f[k][cidx])
                        s_a.inc(scalar.activation(Fb[3][:, 0:cw], Fb[2][:, 0:cw],
                                                  AF.Exp, scale=LN2))
            block.scalar(f_dec_a)

            def f_dec_st(gpsimd):
                for k, (name, pk, sc, t, cols) in enumerate(tiles_f):
                    gpsimd.wait_ge(s_v.h, v_tile_done[k])
                    s_gf.inc(gpsimd.dma_start(
                        out=ag[name][0][t * P:(t + 1) * P, :],
                        in_=pOT[:, 0:cols]), 16)
            block.gpsimd(f_dec_st)
            fp4_stores = s_gf.n

            def f_ag(gpsimd):
                gpsimd.wait_ge(s_gf.h, fp4_stores)
                for name in ("wgu", "wdn"):
                    ag_sem[name].inc(gpsimd.collective_compute(
                        "AllGather", ALU.bypass, replica_groups=[list(range(C))],
                        ins=[ag[name][0].ap().opt()], outs=[ag[name][1].ap().opt()]))
            block.gpsimd(f_ag)

            if tap is not None and tap.startswith("prep"):
                wname = tap.split(":")[1] if ":" in tap else "w1"
                wcols = dict((n, c) for n, r, c in ag_shapes)[wname]
                wcols = min(wcols, J * D)
                with nc.sbuf_tensor("dmp", [P, wcols], BF16) as dmp, \
                     nc.sbuf_tensor("dmpf", [P, wcols], F32) as dmpf:
                    s_dmp = sem("dmp")

                    def d1(sync):
                        sync.wait_ge(ag_sem[wname].h, 1)
                        s_dmp.inc(sync.dma_start(out=dmp[:, :],
                                                 in_=ag[wname][1][0:P, 0:wcols]), 16)

                    def d2(vector):
                        vector.wait_ge(s_dmp.h, s_dmp.n)
                        s_v.inc(vector.tensor_copy(dmpf[:, :], dmp[:, :]))
                    block.sync(d1)
                    block.vector(d2)
                    tap_out(dmpf[:, :], wcols, [(s_v, s_v.n)])
                    return nc

        barrier()

        NS = max(1, L // 512)
        SW = min(512, L)

        def spans(lo, hi):
            out = []
            a = lo
            while a < hi:
                b = min(hi, (a // 512 + 1) * 512)
                out.append((a, b))
                a = b
            return out

        agk_sem = [sem(f"agk{l}") for l in range(depth)]
        agv_sem = [sem(f"agv{l}") for l in range(depth)]
        tapret = {"stop": False}

        def do_tap(name, src_ap, ncols, wait_sems):
            if tap == name:
                tap_out(src_ap, ncols, wait_sems)
                tapret["stop"] = True

        def ph_ln(l, xn, lnscr):
            a0, v0 = s_a.n, s_v.n

            def vsum(vector):
                vector.wait_ge(s_c0.h, consts_done)
                if l > 0:
                    vector.wait_ge(bar, barn[0])
                vector.drain()
                for t in range(J):
                    s_v.inc(vector.tensor_reduce(
                        sml[:, t:t + 1], x_res[:, t * D:(t + 1) * D],
                        AX.X, ALU.add))
            block.vector(vsum)

            def asq(scalar):
                scalar.wait_ge(s_c0.h, consts_done)
                if l > 0:
                    scalar.wait_ge(bar, barn[0])
                for t in range(J):
                    s_a.inc(scalar.activation(
                        lnscr[:, :], x_res[:, t * D:(t + 1) * D], AF.Square,
                        accum_out=sml[:, 8 + t:9 + t]))
                    scalar.drain()
            block.scalar(asq)

            def vstats(vector):
                vector.wait_ge(s_a.h, a0 + J)
                vector.drain()
                vector.tensor_scalar(sml[:, 16:16 + J], sml[:, 0:J],
                                     1.0 / D, None, ALU.mult)
                vector.drain()
                vector.tensor_scalar(sml[:, 24:24 + J], sml[:, 8:8 + J],
                                     1.0 / D, None, ALU.mult)
                vector.tensor_tensor(sml[:, 32:32 + J], sml[:, 16:16 + J],
                                     sml[:, 16:16 + J], ALU.mult)
                vector.drain()
                vector.tensor_tensor(sml[:, 8:8 + J], sml[:, 24:24 + J],
                                     sml[:, 32:32 + J], ALU.subtract)
                vector.drain()
                s_v.inc(vector.tensor_scalar(sml[:, 24:24 + J], sml[:, 8:8 + J],
                                             1e-5, None, ALU.add))
            block.vector(vstats)

            def asqrt(scalar):
                scalar.wait_ge(s_v.h, v0 + J + 1)
                s_a.inc(scalar.sqrt(sml[:, 32:32 + J], sml[:, 24:24 + J]))
            block.scalar(asqrt)

            def vrecip(vector):
                vector.wait_ge(s_a.h, a0 + J + 1)
                vector.reciprocal(sml[:, 0:J], sml[:, 32:32 + J])
                vector.drain()
                s_v.inc(vector.scalar_tensor_tensor(
                    sml[:, 8:8 + J], sml[:, 16:16 + J],
                    -1.0, sml[:, 0:J], ALU.mult, ALU.mult))
            block.vector(vrecip)

            def anorm(vector):
                vector.wait_ge(s_v.h, v0 + J + 2)
                li = None
                for t in range(J):
                    li = vector.tensor_scalar(
                        xn[:, t * D:(t + 1) * D], x_res[:, t * D:(t + 1) * D],
                        sml[:, t:t + 1], sml[:, 8 + t:9 + t],
                        ALU.mult, ALU.add)
                s_v.inc(li)
            block.vector(anorm)
            return s_v.n

        def ph_transpose(src, dst, ntile, ready, src_cols=D):
            nblk = src_cols // P
            t0, v0 = s_t.n, s_v.n
            with nc.psum_tensor(U("psTTa"), [P, P], BF16) as psTTa, \
                 nc.psum_tensor(U("psTTb"), [P, P], BF16) as psTTb:
                psTT = [psTTa, psTTb]
                def tt_(tensor):
                    tensor.wait_ge(ready[0].h, ready[1])
                    i = 0
                    for dk in range(nblk):
                        for t in range(ntile):
                            par = i % 2
                            if i >= 2:
                                tensor.wait_ge(s_v.h, v0 + i - 1)
                            s_t.inc(tensor.transpose(
                                psTT[par][:, :],
                                src[:, t * src_cols + dk * P:
                                    t * src_cols + (dk + 1) * P],
                                id_s[:, :]))
                            i += 1
                block.tensor(tt_)

                def tv_(vector):
                    i = 0
                    for dk in range(nblk):
                        for t in range(ntile):
                            par = i % 2
                            vector.wait_ge(s_t.h, t0 + i + 1)
                            s_v.inc(vector.tensor_copy(
                                dst[:, dk * L + t * P:dk * L + (t + 1) * P],
                                psTT[par][:, :]))
                            i += 1
                block.vector(tv_)
            return s_v.n


        def ph_qkv(l, xn, xnT, xnT_ready_v):
            wname = "wq0" if l == 0 else "wq1"
            wfull = ag[wname][1]
            rope = (l == 1)
            ki, ko, vi, vo = kvt[l]
            w0, st0 = s_w.n, s_st.n
            sp_base = [s_sp[0].n, s_sp[1].n]
            NOB = (3 * D) // 512
            tiles = []
            tt, vv = s_t.n, s_v.n
            for ob in range(NOB):
                for ot in range(4):
                    o = ob * 4 + ot
                    for sh in range(NS):
                        d = {"ob": ob, "o": o, "sh": sh}
                        tt += 1
                        d["t_grp"] = tt
                        vv += 1
                        d["v_evac"] = vv
                        if rope and o < 32:
                            tt += 1
                            d["t_rot"] = tt
                            vv += 1
                            d["v_rot"] = vv
                        elif o >= 32:
                            nss = SW // P
                            d["t_vt"] = []
                            d["v_vt"] = []
                            for ss in range(nss):
                                tt += 1
                                d["t_vt"].append(tt)
                                vv += 1
                                d["v_vt"].append(vv)
                        d["t_done"] = tt
                        d["v_done"] = vv
                        tiles.append(d)
            with nc.sbuf_tensor(U("wbufA"), [P, 16 * 512], BF16) as wbufA, \
                 nc.sbuf_tensor(U("wbufB"), [P, 16 * 512], BF16) as wbufB, \
                 nc.sbuf_tensor(U("qs0"), [P, 2 * SW], BF16) as qs0, \
                 nc.sbuf_tensor(U("qs1"), [P, 2 * SW], BF16) as qs1, \
                 nc.sbuf_tensor(U("vst"), [P, 2 * SW], BF16) as vst, \
                 nc.sbuf_tensor(U("rm1"), [P, SW], F32) as rm1, \
                 nc.sbuf_tensor(U("rm2"), [P, SW], F32) as rm2, \
                 nc.psum_tensor(U("psQ"), [P, 2 * 512], F32) as psQ, \
                 nc.psum_tensor(U("psR"), [P, 512], F32) as psR, \
                 nc.psum_tensor(U("psVTa"), [P, P], BF16) as psVTa, \
                 nc.psum_tensor(U("psVTb"), [P, P], BF16) as psVTb:
                psVT = [psVTa, psVTb]
                wbufs = [wbufA, wbufB]

                wq_tgt = [s_wp[0].n + 256 * (ob // 2 + 1) if ob % 2 == 0
                          else s_wp[1].n + 256 * (ob // 2 + 1)
                          for ob in range(NOB)]

                def qkv_w(scalar):
                    scalar.wait_ge(ag_sem[wname].h, 1)
                    scalar.wait_ge(s_v.h, xnT_ready_v)
                    for ob in range(NOB):
                        if ob >= 2:
                            scalar.wait_ge(s_t.h,
                                           tiles[(ob - 1) * 4 * NS - 1]["t_done"])
                        for dk in range(16):
                            s_wp[ob % 2].inc(scalar.dma_start(
                                out=wbufs[ob % 2][:, dk * 512:(dk + 1) * 512],
                                in_=wfull[ob * 512:(ob + 1) * 512,
                                          dk * P:(dk + 1) * P],
                                transpose=True), 16)
                block.scalar(qkv_w)

                def qkv_t(tensor):
                    tensor.wait_ge(s_v.h, xnT_ready_v)
                    for i, d in enumerate(tiles):
                        ob, o, sh = d["ob"], d["o"], d["sh"]
                        pp = i % 2
                        if sh == 0 and d["o"] % 4 == 0:
                            tensor.wait_ge(s_wp[ob % 2].h, wq_tgt[ob])
                        if i >= 2:
                            tensor.wait_ge(s_v.h, tiles[i - 2]["v_done"])
                        for dk in range(16):
                            mm = tensor.matmul(
                                psQ[:, pp * 512:pp * 512 + SW],
                                wbufs[ob % 2][:, dk * 512 + ot_col(o):
                                              dk * 512 + ot_col(o) + P],
                                xnT[:, dk * L + sh * 512:dk * L + sh * 512 + SW],
                                start=(dk == 0), stop=(dk == 15))
                        s_t.inc(mm)
                        if "t_rot" in d:
                            tensor.wait_ge(s_v.h, d["v_evac"])
                            s_t.inc(tensor.matmul(
                                psR[:, 0:SW], rot_s[:, :],
                                qs0[:, pp * SW:pp * SW + SW],
                                start=True, stop=True))
                        elif "t_vt" in d:
                            tensor.wait_ge(s_v.h, d["v_evac"])
                            for ss in range(len(d["t_vt"])):
                                if ss >= 2:
                                    tensor.wait_ge(s_v.h, d["v_vt"][ss - 2])
                                s_t.inc(tensor.transpose(
                                    psVT[ss % 2][:, :],
                                    qs0[:, pp * SW + ss * P:pp * SW + (ss + 1) * P],
                                    id_s[:, :]))
                block.tensor(qkv_t)

                def qkv_v(vector):
                    for i, d in enumerate(tiles):
                        o, sh = d["o"], d["sh"]
                        pp = i % 2
                        vector.wait_ge(s_t.h, d["t_grp"])
                        if i >= 2:
                            vector.wait_ge(s_sp[i % 2].h,
                                           sp_base[i % 2] + 16 * (i // 2))
                        s_v.inc(vector.tensor_copy(
                            qs0[:, pp * SW:pp * SW + SW],
                            psQ[:, pp * 512:pp * 512 + SW]))
                        if "t_rot" in d:
                            vector.drain()
                        if "t_rot" in d:
                            vector.wait_ge(s_t.h, d["t_rot"])
                            cs = sh * 512
                            vector.tensor_tensor(
                                rm1[:, 0:SW], qs0[:, pp * SW:pp * SW + SW],
                                cos_s[:, cs:cs + SW], ALU.mult)
                            vector.tensor_tensor(
                                rm2[:, 0:SW], psR[:, 0:SW],
                                sin_s[:, cs:cs + SW], ALU.mult)
                            vector.drain()
                            s_v.inc(vector.tensor_tensor(
                                qs1[:, pp * SW:pp * SW + SW],
                                rm1[:, 0:SW], rm2[:, 0:SW], ALU.add))
                        elif "t_vt" in d:
                            for ss in range(len(d["t_vt"])):
                                vector.wait_ge(s_t.h, d["t_vt"][ss])
                                s_v.inc(vector.tensor_copy(
                                    vst[:, pp * SW + ss * P:pp * SW + (ss + 1) * P],
                                    psVT[ss % 2][:, :]))
                block.vector(qkv_v)

                def qkv_st(gpsimd):
                    for i, d in enumerate(tiles):
                        o, sh = d["o"], d["sh"]
                        pp = i % 2
                        gpsimd.wait_ge(s_v.h, d["v_done"])
                        if o < 16:
                            srcb = qs1 if rope else qs0
                            s_sp[i % 2].inc(gpsimd.dma_start(
                                out=q_stash[:, o * L + sh * 512:
                                            o * L + sh * 512 + SW],
                                in_=srcb[:, pp * SW:pp * SW + SW]), 16)
                        elif o < 32:
                            srcb = qs1 if rope else qs0
                            s_sp[i % 2].inc(gpsimd.dma_start(
                                out=ki[(o - 16) * P:(o - 15) * P,
                                       sh * 512:sh * 512 + SW],
                                in_=srcb[:, pp * SW:pp * SW + SW]), 16)
                        else:
                            s_sp[i % 2].inc(gpsimd.dma_start(
                                out=vi.ap()[o - 32, :,
                                            sh * 512:sh * 512 + SW],
                                in_=vst[:, pp * SW:pp * SW + SW]), 16)
                    if l == 0:
                        xr = xn_stash.ap().rearrange("(t p) d -> p t d", p=P)
                        s_st.inc(gpsimd.dma_start(
                            out=xr,
                            in_=xn.ap().rearrange("p (t d) -> p t d", d=D)), 16)
                    else:
                        s_st.inc(gpsimd.dma_start(
                            out=xnT_stash[:, :], in_=xnT[:, :]), 16)
                    gpsimd.wait_ge(s_sp[0].h, s_sp[0].n)
                    gpsimd.wait_ge(s_sp[1].h, s_sp[1].n)
                    gpsimd.wait_ge(s_st.h, s_st.n)
                    agk_sem[l].inc(gpsimd.collective_compute(
                        "AllGather", ALU.bypass, replica_groups=[list(range(C))],
                        ins=[ki.ap().opt()], outs=[ko.ap().opt()]))
                    agv_sem[l].inc(gpsimd.collective_compute(
                        "AllGather", ALU.bypass, replica_groups=[list(range(C))],
                        ins=[vi.ap().opt()], outs=[vo.ap().opt()]))
                block.gpsimd(qkv_st)

        def ot_col(o):
            return (o % 4) * P

        def ph_attn(l):
            ki, ko, vi, vo = kvt[l]
            with nc.sbuf_tensor(U("QT"), [P, 2 * L], BF16) as QT, \
                 nc.sbuf_tensor(U("aoT"), [P, 16 * L], BF16) as aoT:
              with nc.sbuf_tensor(U("KTb"), [P, 2 * NKC * P], BF16) as KTb, \
                 nc.sbuf_tensor(U("Vb"), [P, 2 * NKC * P], BF16) as Vb, \
                 nc.sbuf_tensor(U("exb"), [P, 2 * L], BF16) as exb, \
                 nc.sbuf_tensor(U("dsc"), [P, 2 * P], BF16) as dsc, \
                 nc.sbuf_tensor(U("zsb"), [1, 2 * 512], F32) as zsb, \
                 nc.sbuf_tensor(U("rzb"), [1, 2 * 512], F32) as rzb, \
                 nc.sbuf_tensor(U("bbs"), [P, 2 * 512], F32) as bbs, \
                 nc.psum_tensor(U("Sps0"), [P, L], F32) as Sps0, \
                 nc.psum_tensor(U("Sps1"), [P, L], F32) as Sps1, \
                 nc.psum_tensor(U("Ops"), [P, L], F32) as Ops, \
                 nc.psum_tensor(U("Zps0"), [1, 512], F32) as Zps0, \
                 nc.psum_tensor(U("Zps1"), [1, 512], F32) as Zps1:
                Sps = [Sps0, Sps1]
                Zps = [Zps0, Zps1]
                kv0 = s_kv.n
                s_kp = [sem(U("kpa")), sem(U("kpb"))]
                kp0 = [s_kp[0].n, s_kp[1].n]

                heads_plan = []
                tt, aa, vv = s_t.n, s_a.n, s_v.n
                for h in range(H):
                    plan = {"qk": [0] * NKC, "zav": [0] * NKC,
                            "exp": [], "mask": []}
                    for kc in range(NKC):
                        tt += 1
                        plan["qk"][kc] = tt
                        if kc >= 1:
                            tt += 1
                            plan["zav"][kc - 1] = tt
                    tt += 1
                    plan["zav"][NKC - 1] = tt
                    for kc in range(NKC):
                        aa += 1
                        plan["exp"].append(aa)
                        vv += 1
                        plan["mask"].append(vv)
                    tt += NS
                    plan["bmm"] = tt
                    vv += 2 * NS
                    plan["vdone"] = vv
                    heads_plan.append(plan)

                def kvload(sync):
                    sync.wait_ge(agk_sem[l].h, agk_sem[l].n)
                    sync.wait_ge(agv_sem[l].h, agv_sem[l].n)
                    for h in range(H):
                        hb = (h % 2) * NKC * P
                        if h >= 2:
                            sync.wait_ge(s_t.h, heads_plan[h - 2]["zav"][-1])
                        s_kp[h % 2].inc(sync.dma_start(
                            out=QT[:, (h % 2) * L:(h % 2 + 1) * L],
                            in_=q_stash[:, h * L:(h + 1) * L]), 16)
                        for r in range(C):
                            s_kp[h % 2].inc(sync.dma_start(
                                out=KTb[:, hb + r * J * P:hb + (r + 1) * J * P],
                                in_=ko.ap()[r, h * P:(h + 1) * P, :]), 16)
                            s_kp[h % 2].inc(sync.dma_start(
                                out=Vb[:, hb + r * J * P:hb + (r + 1) * J * P],
                                in_=vo.ap()[r, h, :, :]), 16)
                block.sync(kvload)

                def attn_t(tensor):
                    def emit_qk(plan, h, kc):
                        jmin = kc // 8
                        qlo = jmin * P
                        slot = kc % 2
                        if kc >= 2:
                            tensor.wait_ge(s_a.h, plan["exp"][kc - 2])
                        kb = (h % 2) * NKC * P + ((kc % 8) * J + kc // 8) * P
                        qb = (h % 2) * L
                        for (pa, pb) in spans(qlo, L):
                            mm = tensor.matmul(
                                Sps[slot][:, pa:pb],
                                KTb[:, kb:kb + P],
                                QT[:, qb + pa:qb + pb],
                                start=True, stop=True)
                        s_t.inc(mm)

                    def emit_zav(plan, h, kc):
                        jmin = kc // 8
                        qlo = jmin * P
                        slot = kc % 2
                        tensor.wait_ge(s_v.h, plan["mask"][kc])
                        kb = (h % 2) * NKC * P + ((kc % 8) * J + kc // 8) * P
                        for (pa, pb) in spans(qlo, L):
                            half = pa // 512
                            kc_max = 8 * min(J, (half + 1) * 4)
                            tensor.matmul(
                                Zps[half][:, pa - half * 512:pb - half * 512],
                                onec_s[:, :],
                                exb[:, slot * L + pa:slot * L + pb],
                                start=(kc == 0), stop=(kc == kc_max - 1))
                            mm = tensor.matmul(
                                Ops[:, pa:pb],
                                Vb[:, kb:kb + P],
                                exb[:, slot * L + pa:slot * L + pb],
                                start=(kc == 0), stop=(kc == kc_max - 1))
                        s_t.inc(mm)

                    for h in range(H):
                        plan = heads_plan[h]
                        tensor.wait_ge(s_kp[h % 2].h,
                                       kp0[h % 2] + 272 * (h // 2 + 1))
                        if h >= 1:
                            tensor.wait_ge(s_v.h, heads_plan[h - 1]["vdone"])
                        for kc in range(NKC):
                            emit_qk(plan, h, kc)
                            if kc >= 1:
                                emit_zav(plan, h, kc - 1)
                        emit_zav(plan, h, NKC - 1)
                        tensor.wait_ge(s_a.h, plan["exp"][NKC - 1])
                        tensor.wait_ge(s_v.h, plan["vdone"] - NS)
                        for half in range(NS):
                            s_t.inc(tensor.matmul(
                                Sps[half][:, 0:SW], onerf_s[:, :],
                                rzb[0:1, half * 512:half * 512 + SW],
                                start=True, stop=True))
                block.tensor(attn_t)

                def attn_a(scalar):
                    for h in range(H):
                        plan = heads_plan[h]
                        for kc in range(NKC):
                            jmin = kc // 8
                            qlo = jmin * P
                            slot = kc % 2
                            scalar.wait_ge(s_t.h, plan["qk"][kc])
                            if kc >= 2:
                                scalar.wait_ge(s_t.h, plan["zav"][kc - 2])
                            ai = scalar.activation(
                                dsc[:, slot * P:(slot + 1) * P],
                                Sps[slot][:, qlo:qlo + P], AF.Exp,
                                scale=1.0 / math.sqrt(DH))
                            if qlo + P < L:
                                ai = scalar.activation(
                                    exb[:, slot * L + qlo + P:slot * L + L],
                                    Sps[slot][:, qlo + P:L], AF.Exp,
                                    scale=1.0 / math.sqrt(DH))
                            s_a.inc(ai)
                block.scalar(attn_a)

                def attn_v(vector):
                    for h in range(H):
                        plan = heads_plan[h]
                        for kc in range(NKC):
                            jmin = kc // 8
                            qlo = jmin * P
                            slot = kc % 2
                            m = kc % 8
                            vector.wait_ge(s_a.h, plan["exp"][kc])
                            s_v.inc(vector.tensor_tensor(
                                exb[:, slot * L + qlo:slot * L + qlo + P],
                                dsc[:, slot * P:(slot + 1) * P],
                                mask_s[:, m * P:(m + 1) * P], ALU.mult))
                        for half in range(NS):
                            kc_max = 8 * min(J, (half + 1) * 4)
                            vector.wait_ge(s_t.h, plan["zav"][kc_max - 1])
                            vector.tensor_copy(zsb[0:1, half * 512:half * 512 + SW],
                                               Zps[half][0:1, 0:SW])
                            vector.drain()
                            s_v.inc(vector.reciprocal(
                                rzb[0:1, half * 512:half * 512 + SW],
                                zsb[0:1, half * 512:half * 512 + SW]))
                        for half in range(NS):
                            vector.wait_ge(s_t.h, plan["bmm"] - NS + half + 1)
                            vector.tensor_copy(bbs[:, half * 512:half * 512 + SW],
                                               Sps[half][:, 0:SW])
                            vector.drain()
                            s_v.inc(vector.tensor_tensor(
                                aoT[:, h * L + half * 512:h * L + half * 512 + SW],
                                Ops[:, half * 512:half * 512 + SW],
                                bbs[:, half * 512:half * 512 + SW], ALU.mult))
                block.vector(attn_v)
              do_tap(f"attn{l}", aoT[:, 0:min(16 * L, J * D)],
                     min(16 * L, J * D), [(s_v, s_v.n)])
              if tapret["stop"]:
                  return
              barrier()

              if True:
                woname = "wo0" if l == 0 else "wo1"
                wofull = ag[woname][1]
                t0, v0 = s_t.n, s_v.n
                wo_tgt = [s_wp[ob % 2].n + 256 * (ob // 2 + 1)
                          for ob in range(D // 512)]
                with nc.sbuf_tensor(U("wbufO"), [P, 16 * 512], BF16) as wbufO, \
                     nc.sbuf_tensor(U("wbufO2"), [P, 16 * 512], BF16) as wbufO2, \
                     nc.sbuf_tensor(U("tmpO"), [P, 2 * 512], F32) as tmpO, \
                     nc.psum_tensor(U("psO"), [P, 2 * 512], F32) as psO:
                    wobufs = [wbufO, wbufO2]
                    NOB2 = D // 512

                    def wo_w(scalar):
                        scalar.wait_ge(ag_sem[woname].h, 1)
                        scalar.wait_ge(bar, barn[0])
                        for ob in range(NOB2):
                            if ob >= 2:
                                scalar.wait_ge(s_t.h, t0 + (ob - 1) * J)
                            for dk in range(16):
                                s_wp[ob % 2].inc(scalar.dma_start(
                                    out=wobufs[ob % 2][:, dk * 512:(dk + 1) * 512],
                                    in_=wofull[ob * 512:(ob + 1) * 512,
                                               dk * P:(dk + 1) * P],
                                    transpose=True), 16)
                    block.scalar(wo_w)

                    def wo_t(tensor):
                        i = 0
                        for ob in range(NOB2):
                            tensor.wait_ge(s_wp[ob % 2].h, wo_tgt[ob])
                            for st in range(J):
                                pp = i % 2
                                if i >= 2:
                                    tensor.wait_ge(s_v.h, v0 + i - 1)
                                for dk in range(16):
                                    mm = tensor.matmul(
                                        psO[:, pp * 512:(pp + 1) * 512],
                                        aoT[:, dk * L + st * P:dk * L + (st + 1) * P],
                                        wobufs[ob % 2][:, dk * 512:(dk + 1) * 512],
                                        start=(dk == 0), stop=(dk == 15))
                                s_t.inc(mm)
                                i += 1
                    block.tensor(wo_t)

                    def wo_v(vector):
                        vector.drain()
                        i = 0
                        for ob in range(NOB2):
                            for st in range(J):
                                pp = i % 2
                                vector.wait_ge(s_t.h, t0 + i + 1)
                                xsl = x_res[:, st * D + ob * 512:
                                            st * D + (ob + 1) * 512]
                                tsl = tmpO[:, pp * 512:(pp + 1) * 512]
                                vector.tensor_tensor(
                                    tsl, psO[:, pp * 512:(pp + 1) * 512],
                                    xsl, ALU.add)
                                vector.drain()
                                s_v.inc(vector.tensor_copy(xsl, tsl))
                                i += 1
                    block.vector(wo_v)

        def ph_ffn0():
            NH = HID // 512
            NHK = HID // P
            HF = min(2, J)
            nhalves = (J + HF - 1) // HF
            if True:
                s_xn = sem("ffn0xn")
                with nc.sbuf_tensor(U("xqT"), [P, 16 * L], BF16) as xqT:
                    with nc.sbuf_tensor(U("xnt_in"), [P, D], BF16) as xnt_in, \
                         nc.sbuf_tensor(U("xqt"), [P, D], BF16) as xqt, \
                         nc.sbuf_tensor(U("qsf"), [P, 512], F32) as qsf, \
                         nc.sbuf_tensor(U("qsi"), [P, 512], I32) as qsi, \
                         nc.psum_tensor(U("psTqa"), [P, P], BF16) as psTqa, \
                         nc.psum_tensor(U("psTqb"), [P, P], BF16) as psTqb:
                        psTq = [psTqa, psTqb]
                        tt, vv = s_t.n, s_v.n
                        q_plan = []
                        for t in range(J):
                            vv += 1
                            d = {"vq": vv, "t_vt": [], "v_vt": []}
                            for dk in range(16):
                                tt += 1
                                d["t_vt"].append(tt)
                                vv += 1
                                d["v_vt"].append(vv)
                            q_plan.append(d)

                        def xload(sync):
                            sync.wait_ge(bar, barn[0])
                            for t in range(J):
                                if t >= 1:
                                    sync.wait_ge(s_v.h, q_plan[t - 1]["vq"])
                                s_xn.inc(sync.dma_start(
                                    out=xnt_in[:, :],
                                    in_=xn_stash[t * P:(t + 1) * P, :]), 16)
                        block.sync(xload)

                        def q_v(vector):
                            for t in range(J):
                                vector.wait_ge(s_xn.h, 16 * (t + 1))
                                vector.tensor_reduce(
                                    sml[:, 48:49], xnt_in[:, :], AX.X, ALU.max,
                                    apply_absolute_value=True)
                                vector.drain()
                                vector.tensor_scalar(sml[:, 49:50], sml[:, 48:49],
                                                     1e-5, None, ALU.max)
                                vector.drain()
                                vector.reciprocal(sml[:, 50:51], sml[:, 49:50])
                                vector.drain()
                                vector.tensor_scalar(sml[:, 51:52], sml[:, 50:51],
                                                     127.0, None, ALU.mult)
                                vector.tensor_scalar(sml[:, 24 + t:25 + t],
                                                     sml[:, 49:50],
                                                     1.0 / 127.0, None, ALU.mult)
                                vector.drain()
                                vector.tensor_scalar(sml[:, 32 + t:33 + t],
                                                     sml[:, 24 + t:25 + t],
                                                     sml[:, 52:53], None, ALU.mult)
                                vector.tensor_scalar(sml[:, 40 + t:41 + t],
                                                     sml[:, 24 + t:25 + t],
                                                     sml[:, 53:54], None, ALU.mult)
                                vector.drain()
                                li = None
                                for ch in range(4):
                                    sl = slice(ch * 512, (ch + 1) * 512)
                                    vector.tensor_scalar(qsf[:, :], xnt_in[:, sl],
                                                         sml[:, 51:52], None,
                                                         ALU.mult)
                                    vector.drain()
                                    vector.tensor_copy(qsi[:, :], qsf[:, :])
                                    vector.drain()
                                    li = vector.tensor_scalar(xqt[:, sl], qsi[:, :],
                                                              127, -128,
                                                              ALU.min, ALU.max)
                                s_v.inc(li)
                                for dk in range(16):
                                    vector.wait_ge(s_t.h, q_plan[t]["t_vt"][dk])
                                    s_v.inc(vector.tensor_copy(
                                        xqT[:, dk * L + t * P:dk * L + (t + 1) * P],
                                        psTq[dk % 2][:, :]))
                        block.vector(q_v)

                        def q_t(tensor):
                            for t in range(J):
                                tensor.wait_ge(s_v.h, q_plan[t]["vq"])
                                for dk in range(16):
                                    if dk >= 2:
                                        tensor.wait_ge(s_v.h,
                                                       q_plan[t]["v_vt"][dk - 2])
                                    s_t.inc(tensor.transpose(
                                        psTq[dk % 2][:, :],
                                        xqt[:, dk * P:(dk + 1) * P],
                                        id_s[:, :]))
                        block.tensor(q_t)
                    xqT_ready_v = s_v.n

                    do_tap("xq0", xqT[:, 0:min(16 * L, J * D)],
                           min(16 * L, J * D), [(s_v, xqT_ready_v)])
                    if tapret["stop"]:
                        return

                    with nc.sbuf_tensor(U("wb1"), [P, 2 * 16 * 512], BF16) as wb1, \
                         nc.sbuf_tensor(U("wb3"), [P, 16 * 512], BF16) as wb3, \
                         nc.sbuf_tensor(U("hbuf"), [P, HF * HID], BF16) as hbuf, \
                         nc.sbuf_tensor(U("tm1"), [P, 512], F32) as tm1, \
                         nc.sbuf_tensor(U("hqf"), [P, 512], F32) as hqf, \
                         nc.sbuf_tensor(U("hqi"), [P, 512], I32) as hqi, \
                         nc.sbuf_tensor(U("hq"), [P, HID], BF16) as hq, \
                         nc.sbuf_tensor(U("hstg"), [P, NHK * P], BF16) as hstg, \
                         nc.psum_tensor(U("psB1"), [P, 512], F32) as psB1, \
                         nc.psum_tensor(U("psB3"), [P, 512], F32) as psB3, \
                         nc.psum_tensor(U("psHTa"), [P, P], BF16) as psHTa, \
                         nc.psum_tensor(U("psHTb"), [P, P], BF16) as psHTb:
                        psHT = [psHTa, psHTb]
                        w0, st0 = s_w.n, s_st.n
                        s_b3 = sem("fb3")
                        spb = [s_wp[0].n, s_wp[1].n]
                        tt, aa, vv = s_t.n, s_a.n, s_v.n
                        plan_mm = {}
                        plan_q = {}
                        for hf in range(nhalves):
                            sts = list(range(hf * HF, min(J, (hf + 1) * HF)))
                            for hg in range(NH):
                                for si, st in enumerate(sts):
                                    tt += 1
                                    aa += 1
                                    vv += 1
                                    plan_mm[(hf, hg, si)] = {
                                        "t": tt, "a": aa, "v": vv}
                            for si, st in enumerate(sts):
                                vv += 1
                                q = {"vq": vv, "t_vt": [], "v_vt": []}
                                for hk in range(NHK):
                                    tt += 1
                                    q["t_vt"].append(tt)
                                    vv += 1
                                    q["v_vt"].append(vv)
                                plan_q[(hf, si)] = q

                        groups_f = [(hf, hg) for hf in range(nhalves)
                                    for hg in range(NH)]
                        b3_0 = s_b3.n

                        def f_w(scalar):
                            scalar.wait_ge(ag_sem["w1"].h, 1)
                            scalar.wait_ge(ag_sem["w3"].h, 1)
                            scalar.wait_ge(s_v.h, xqT_ready_v)
                            for g, (hf, hg) in enumerate(groups_f):
                                pg = (g % 2) * 16 * 512
                                if g >= 2:
                                    phf, phg = groups_f[g - 2]
                                    nsi = len(list(range(
                                        phf * HF, min(J, (phf + 1) * HF))))
                                    scalar.wait_ge(
                                        s_t.h,
                                        plan_mm[(phf, phg, nsi - 1)]["t"])
                                for dk in range(16):
                                    s_wp[g % 2].inc(scalar.dma_start(
                                        out=wb1[:, pg + dk * 512:
                                                pg + (dk + 1) * 512],
                                        in_=ag["w1"][1][hg * 512:(hg + 1) * 512,
                                                        dk * P:(dk + 1) * P],
                                        transpose=True), 16)
                                if g >= 1:
                                    phf, phg = groups_f[g - 1]
                                    nsi = len(list(range(
                                        phf * HF, min(J, (phf + 1) * HF))))
                                    scalar.wait_ge(
                                        s_t.h,
                                        plan_mm[(phf, phg, nsi - 1)]["t"])
                                for dk in range(16):
                                    s_b3.inc(scalar.dma_start(
                                        out=wb3[:, dk * 512:(dk + 1) * 512],
                                        in_=ag["w3"][1][hg * 512:(hg + 1) * 512,
                                                        dk * P:(dk + 1) * P],
                                        transpose=True), 16)
                        block.sync(f_w)

                        def f_t(tensor):
                            i = 0
                            for hf in range(nhalves):
                                sts = list(range(hf * HF, min(J, (hf + 1) * HF)))
                                for hg in range(NH):
                                    pg = (i % 2) * 16 * 512
                                    tensor.wait_ge(
                                        s_wp[i % 2].h,
                                        spb[i % 2] + 256 * (i // 2 + 1))
                                    i += 1
                                    tensor.wait_ge(s_b3.h, b3_0 + 256 * i)
                                    for si, st in enumerate(sts):
                                        key = (hf, hg, si)
                                        if not (hf == 0 and hg == 0 and si == 0):
                                            tensor.wait_ge(
                                                s_v.h, plan_mm[key]["v"] - 1)
                                        for dk in range(16):
                                            tensor.matmul(
                                                psB1[:, :],
                                                xqT[:, dk * L + st * P:
                                                    dk * L + (st + 1) * P],
                                                wb1[:, pg + dk * 512:
                                                    pg + (dk + 1) * 512],
                                                start=(dk == 0), stop=(dk == 15))
                                        for dk in range(16):
                                            mm = tensor.matmul(
                                                psB3[:, :],
                                                xqT[:, dk * L + st * P:
                                                    dk * L + (st + 1) * P],
                                                wb3[:, dk * 512:(dk + 1) * 512],
                                                start=(dk == 0), stop=(dk == 15))
                                        s_t.inc(mm)
                                for si, st in enumerate(sts):
                                    q = plan_q[(hf, si)]
                                    tensor.wait_ge(s_v.h, q["vq"])
                                    for hk in range(NHK):
                                        if hk >= 2:
                                            tensor.wait_ge(s_v.h,
                                                           q["v_vt"][hk - 2])
                                        s_t.inc(tensor.transpose(
                                            psHT[hk % 2][:, :],
                                            hq[:, hk * P:(hk + 1) * P],
                                            id_s[:, :]))
                        block.tensor(f_t)

                        def f_a(scalar):
                            for hf in range(nhalves):
                                sts = list(range(hf * HF, min(J, (hf + 1) * HF)))
                                for hg in range(NH):
                                    for si, st in enumerate(sts):
                                        key = (hf, hg, si)
                                        scalar.wait_ge(s_t.h, plan_mm[key]["t"])
                                        ai = scalar.activation(
                                            tm1[:, :], psB1[:, :], AF.Silu,
                                            scale=sml[:, 32 + st:33 + st])
                                        s_a.inc(ai)
                        block.scalar(f_a)

                        def f_v(vector):
                            for hf in range(nhalves):
                                sts = list(range(hf * HF, min(J, (hf + 1) * HF)))
                                for hg in range(NH):
                                    for si, st in enumerate(sts):
                                        key = (hf, hg, si)
                                        vector.wait_ge(s_a.h, plan_mm[key]["a"])
                                        s_v.inc(vector.scalar_tensor_tensor(
                                            hbuf[:, si * HID + hg * 512:
                                                 si * HID + (hg + 1) * 512],
                                            psB3[:, :],
                                            sml[:, 40 + st:41 + st],
                                            tm1[:, :], ALU.mult, ALU.mult))
                                for si, st in enumerate(sts):
                                    q = plan_q[(hf, si)]
                                    if not (hf == 0 and si == 0):
                                        nprev = hf * HF + si
                                        vector.wait_ge(s_st.h,
                                                       st0 + 16 * nprev)
                                    hrow = hbuf[:, si * HID:(si + 1) * HID]
                                    vector.drain()
                                    vector.tensor_reduce(
                                        sml[:, 48:49], hrow, AX.X, ALU.max,
                                        apply_absolute_value=True)
                                    vector.drain()
                                    vector.tensor_scalar(sml[:, 49:50],
                                                         sml[:, 48:49],
                                                         1e-5, None, ALU.max)
                                    vector.drain()
                                    vector.reciprocal(sml[:, 50:51],
                                                      sml[:, 49:50])
                                    vector.drain()
                                    vector.tensor_scalar(sml[:, 51:52],
                                                         sml[:, 50:51],
                                                         127.0, None, ALU.mult)
                                    vector.tensor_scalar(sml[:, 55:56],
                                                         sml[:, 49:50],
                                                         1.0 / 127.0, None,
                                                         ALU.mult)
                                    vector.drain()
                                    vector.tensor_scalar(sml[:, 56 + st:57 + st],
                                                         sml[:, 55:56],
                                                         sml[:, 54:55], None,
                                                         ALU.mult)
                                    li = None
                                    for hg2 in range(NH):
                                        vector.tensor_scalar(
                                            hqf[:, :],
                                            hrow[:, hg2 * 512:(hg2 + 1) * 512],
                                            sml[:, 51:52], None, ALU.mult)
                                        vector.drain()
                                        vector.tensor_copy(hqi[:, :], hqf[:, :])
                                        vector.drain()
                                        li = vector.tensor_scalar(
                                            hq[:, hg2 * 512:(hg2 + 1) * 512],
                                            hqi[:, :], 127, -128,
                                            ALU.min, ALU.max)
                                    s_v.inc(li)
                                    for hk in range(NHK):
                                        vector.wait_ge(s_t.h, q["t_vt"][hk])
                                        s_v.inc(vector.tensor_copy(
                                            hstg[:, hk * P:(hk + 1) * P],
                                            psHT[hk % 2][:, :]))
                        block.vector(f_v)

                        def f_st(gpsimd):
                            for hf in range(nhalves):
                                sts = list(range(hf * HF, min(J, (hf + 1) * HF)))
                                for si, st in enumerate(sts):
                                    q = plan_q[(hf, si)]
                                    gpsimd.wait_ge(s_v.h, q["v_vt"][-1])
                                    s_st.inc(gpsimd.dma_start(
                                        out=hT_stash[:, st * NHK * P:
                                                     (st + 1) * NHK * P],
                                        in_=hstg[:, :]), 16)
                        block.gpsimd(f_st)
                        hstores = s_st.n

                with nc.sbuf_tensor(U("w2buf"), [P, 2 * NHK * 512], BF16) as w2buf, \
                     nc.sbuf_tensor(U("hq1"), [P, 2 * NHK * P], BF16) as hq1, \
                     nc.sbuf_tensor(U("tmp2b"), [P, 2 * 512], F32) as tmp2b, \
                     nc.psum_tensor(U("psW2"), [P, 2 * 512], F32) as psW2:
                    w0, t0, v0 = s_w.n, s_t.n, s_v.n
                    s_w2p = [sem("w2pa"), sem("w2pb")]
                    w2p0 = [s_w2p[0].n, s_w2p[1].n]
                    h2_tgt = [s_hp[i % 2].n + 16 * (i // 2 + 1)
                              for i in range(4 * J)]

                    def w2_w(scalar):
                        scalar.wait_ge(ag_sem["w2"].h, 1)
                        scalar.wait_ge(s_st.h, hstores)
                        for og in range(4):
                            po = (og % 2) * NHK * 512
                            if og >= 2:
                                scalar.wait_ge(s_t.h, t0 + (og - 1) * J)
                            for hk in range(NHK):
                                s_w2p[og % 2].inc(scalar.dma_start(
                                    out=w2buf[:, po + hk * 512:
                                              po + (hk + 1) * 512],
                                    in_=ag["w2"][1][og * 512:(og + 1) * 512,
                                                    hk * P:(hk + 1) * P],
                                    transpose=True), 16)
                    block.scalar(w2_w)

                    def w2_h(sync):
                        sync.wait_ge(s_st.h, hstores)
                        i = 0
                        for og in range(4):
                            for st in range(J):
                                if i >= 2:
                                    sync.wait_ge(s_t.h, t0 + i - 1)
                                s_hp[i % 2].inc(sync.dma_start(
                                    out=hq1[:, (i % 2) * NHK * P:
                                            (i % 2 + 1) * NHK * P],
                                    in_=hT_stash[:, st * NHK * P:
                                                 (st + 1) * NHK * P]), 16)
                                i += 1
                    block.sync(w2_h)

                    def w2_t(tensor):
                        i = 0
                        for og in range(4):
                            po = (og % 2) * NHK * 512
                            tensor.wait_ge(s_w2p[og % 2].h,
                                           w2p0[og % 2] + 16 * NHK * (og // 2 + 1))
                            for st in range(J):
                                pp = i % 2
                                tensor.wait_ge(s_hp[i % 2].h, h2_tgt[i])
                                if i >= 2:
                                    tensor.wait_ge(s_v.h, v0 + i - 1)
                                for hk in range(NHK):
                                    mm = tensor.matmul(
                                        psW2[:, pp * 512:(pp + 1) * 512],
                                        hq1[:, pp * NHK * P + hk * P:
                                            pp * NHK * P + (hk + 1) * P],
                                        w2buf[:, po + hk * 512:
                                              po + (hk + 1) * 512],
                                        start=(hk == 0), stop=(hk == NHK - 1))
                                s_t.inc(mm)
                                i += 1
                    block.tensor(w2_t)

                    def w2_v(vector):
                        vector.drain()
                        i = 0
                        for og in range(4):
                            for st in range(J):
                                pp = i % 2
                                vector.wait_ge(s_t.h, t0 + i + 1)
                                xsl = x_res[:, st * D + og * 512:
                                            st * D + (og + 1) * 512]
                                tsl = tmp2b[:, pp * 512:(pp + 1) * 512]
                                vector.scalar_tensor_tensor(
                                    tsl, psW2[:, pp * 512:(pp + 1) * 512],
                                    sml[:, 56 + st:57 + st], xsl,
                                    ALU.mult, ALU.add)
                                vector.drain()
                                s_v.inc(vector.tensor_copy(xsl, tsl))
                                i += 1
                    block.vector(w2_v)

        def ph_ffn1():
            NH = HID // 512
            NHK = HID // P
            HF = min(2, J)
            nhalves = (J + HF - 1) // HF
            s_x1 = sem("ffn1xn")
            with nc.sbuf_tensor(U("xnT1"), [P, 16 * L], BF16) as xnT1:
                def x1load(sync):
                    sync.wait_ge(bar, barn[0])
                    s_x1.inc(sync.dma_start(
                        out=xnT1[:, :], in_=xnT_stash[:, :]), 16)
                block.sync(x1load)

                with nc.sbuf_tensor(U("wbg"), [P, 2 * 16 * 512], BF16) as wbg, \
                     nc.sbuf_tensor(U("wbu"), [P, 16 * 512], BF16) as wbu, \
                     nc.sbuf_tensor(U("h1buf"), [P, HF * HID], BF16) as h1buf, \
                     nc.sbuf_tensor(U("tg1"), [P, 512], F32) as tg1, \
                     nc.sbuf_tensor(U("h1stg"), [P, NHK * P], BF16) as h1stg, \
                     nc.psum_tensor(U("psG"), [P, 512], F32) as psG, \
                     nc.psum_tensor(U("psU"), [P, 512], F32) as psU, \
                     nc.psum_tensor(U("psHT1a"), [P, P], BF16) as psHT1a, \
                     nc.psum_tensor(U("psHT1b"), [P, P], BF16) as psHT1b:
                    psHT1 = [psHT1a, psHT1b]
                    w0, st0 = s_w.n, s_st.n
                    s_u1 = sem("gup")
                    spb = [s_wp[0].n, s_wp[1].n]
                    tt, aa, vv = s_t.n, s_a.n, s_v.n
                    plan_mm = {}
                    plan_q = {}
                    for hf in range(nhalves):
                        sts = list(range(hf * HF, min(J, (hf + 1) * HF)))
                        for hg in range(NH):
                            for si, st in enumerate(sts):
                                tt += 1
                                aa += 1
                                vv += 1
                                plan_mm[(hf, hg, si)] = {"t": tt, "a": aa, "v": vv}
                        for si, st in enumerate(sts):
                            q = {"t_vt": [], "v_vt": []}
                            for hk in range(NHK):
                                tt += 1
                                q["t_vt"].append(tt)
                                vv += 1
                                q["v_vt"].append(vv)
                            plan_q[(hf, si)] = q

                    groups_g = [(hf, hg) for hf in range(nhalves)
                                for hg in range(NH)]
                    u1_0 = s_u1.n

                    def g_w(scalar):
                        scalar.wait_ge(ag_sem["wgu"].h, 1)
                        scalar.wait_ge(s_x1.h, s_x1.n)
                        for g, (hf, hg) in enumerate(groups_g):
                            pg = (g % 2) * 16 * 512
                            if g >= 2:
                                phf, phg = groups_g[g - 2]
                                nsi = len(list(range(
                                    phf * HF, min(J, (phf + 1) * HF))))
                                scalar.wait_ge(
                                    s_t.h,
                                    plan_mm[(phf, phg, nsi - 1)]["t"])
                            for dk in range(16):
                                s_wp[g % 2].inc(scalar.dma_start(
                                    out=wbg[:, pg + dk * 512:pg + (dk + 1) * 512],
                                    in_=ag["wgu"][1][hg * 512:(hg + 1) * 512,
                                                     dk * P:(dk + 1) * P],
                                    transpose=True), 16)
                            if g >= 1:
                                phf, phg = groups_g[g - 1]
                                nsi = len(list(range(
                                    phf * HF, min(J, (phf + 1) * HF))))
                                scalar.wait_ge(
                                    s_t.h,
                                    plan_mm[(phf, phg, nsi - 1)]["t"])
                            for dk in range(16):
                                s_u1.inc(scalar.dma_start(
                                    out=wbu[:, dk * 512:(dk + 1) * 512],
                                    in_=ag["wgu"][1][HID + hg * 512:
                                                     HID + (hg + 1) * 512,
                                                     dk * P:(dk + 1) * P],
                                    transpose=True), 16)
                    block.sync(g_w)

                    def g_t(tensor):
                        tensor.wait_ge(s_x1.h, s_x1.n)
                        i = 0
                        for hf in range(nhalves):
                            sts = list(range(hf * HF, min(J, (hf + 1) * HF)))
                            for hg in range(NH):
                                pg = (i % 2) * 16 * 512
                                tensor.wait_ge(
                                    s_wp[i % 2].h,
                                    spb[i % 2] + 256 * (i // 2 + 1))
                                i += 1
                                tensor.wait_ge(s_u1.h, u1_0 + 256 * i)
                                for si, st in enumerate(sts):
                                    key = (hf, hg, si)
                                    if not (hf == 0 and hg == 0 and si == 0):
                                        tensor.wait_ge(s_v.h,
                                                       plan_mm[key]["v"] - 1)
                                    for dk in range(16):
                                        tensor.matmul(
                                            psG[:, :],
                                            xnT1[:, dk * L + st * P:
                                                 dk * L + (st + 1) * P],
                                            wbg[:, pg + dk * 512:
                                                pg + (dk + 1) * 512],
                                            start=(dk == 0), stop=(dk == 15))
                                    for dk in range(16):
                                        mm = tensor.matmul(
                                            psU[:, :],
                                            xnT1[:, dk * L + st * P:
                                                 dk * L + (st + 1) * P],
                                            wbu[:, dk * 512:(dk + 1) * 512],
                                            start=(dk == 0), stop=(dk == 15))
                                    s_t.inc(mm)
                            for si, st in enumerate(sts):
                                q = plan_q[(hf, si)]
                                tensor.wait_ge(s_v.h,
                                               plan_mm[(hf, NH - 1, si)]["v"])
                                for hk in range(NHK):
                                    if hk >= 2:
                                        tensor.wait_ge(s_v.h, q["v_vt"][hk - 2])
                                    elif si > 0:
                                        tensor.wait_ge(
                                            s_v.h,
                                            plan_q[(hf, si - 1)]["v_vt"][
                                                NHK - 2 + hk])
                                    s_t.inc(tensor.transpose(
                                        psHT1[hk % 2][:, :],
                                        h1buf[:, si * HID + hk * P:
                                              si * HID + (hk + 1) * P],
                                        id_s[:, :]))
                    block.tensor(g_t)

                    def g_a(scalar):
                        for hf in range(nhalves):
                            sts = list(range(hf * HF, min(J, (hf + 1) * HF)))
                            for hg in range(NH):
                                for si, st in enumerate(sts):
                                    key = (hf, hg, si)
                                    scalar.wait_ge(s_t.h, plan_mm[key]["t"])
                                    ai = scalar.activation(tg1[:, :], psG[:, :],
                                                           AF.Silu)
                                    s_a.inc(ai)
                    block.scalar(g_a)

                    def g_v(vector):
                        for hf in range(nhalves):
                            sts = list(range(hf * HF, min(J, (hf + 1) * HF)))
                            for hg in range(NH):
                                for si, st in enumerate(sts):
                                    key = (hf, hg, si)
                                    vector.wait_ge(s_a.h, plan_mm[key]["a"])
                                    s_v.inc(vector.scalar_tensor_tensor(
                                        h1buf[:, si * HID + hg * 512:
                                              si * HID + (hg + 1) * 512],
                                        psU[:, :], 1.0,
                                        tg1[:, :], ALU.mult, ALU.mult))
                            for si, st in enumerate(sts):
                                q = plan_q[(hf, si)]
                                if not (hf == 0 and si == 0):
                                    nprev = hf * HF + si
                                    vector.wait_ge(s_st.h, st0 + 16 * nprev)
                                for hk in range(NHK):
                                    vector.wait_ge(s_t.h, q["t_vt"][hk])
                                    s_v.inc(vector.tensor_copy(
                                        h1stg[:, hk * P:(hk + 1) * P],
                                        psHT1[hk % 2][:, :]))
                    block.vector(g_v)

                    def g_st(gpsimd):
                        for hf in range(nhalves):
                            sts = list(range(hf * HF, min(J, (hf + 1) * HF)))
                            for si, st in enumerate(sts):
                                q = plan_q[(hf, si)]
                                gpsimd.wait_ge(s_v.h, q["v_vt"][-1])
                                s_st.inc(gpsimd.dma_start(
                                    out=hT_stash[:, st * NHK * P:
                                                 (st + 1) * NHK * P],
                                    in_=h1stg[:, :]), 16)
                    block.gpsimd(g_st)
                    h1stores = s_st.n

            with nc.sbuf_tensor(U("wdnb"), [P, (HID // P) * 512], BF16) as wdnb, \
                 nc.sbuf_tensor(U("hq2"), [P, 2 * (HID // P) * P], BF16) as hq2, \
                 nc.sbuf_tensor(U("y_all"), [P, J * D], BF16) as y_all, \
                 nc.sbuf_tensor(U("rscr"), [P, D], F32) as rscr, \
                 nc.sbuf_tensor(U("tmp3"), [P, 2 * 512], F32) as tmp3, \
                 nc.psum_tensor(U("psDN"), [P, 2 * 512], F32) as psDN:
                NHK = HID // P
                w0, t0, v0, a0 = s_w.n, s_t.n, s_v.n, s_a.n
                dn_tgt = [s_hp[i % 2].n + 16 * (i // 2 + 1)
                          for i in range(4 * J)]

                def dn_w(scalar):
                    scalar.wait_ge(ag_sem["wdn"].h, 1)
                    scalar.wait_ge(s_st.h, h1stores)
                    for og in range(4):
                        if og >= 1:
                            scalar.wait_ge(s_t.h, t0 + og * J)
                        for hk in range(NHK):
                            s_w.inc(scalar.dma_start(
                                out=wdnb[:, hk * 512:(hk + 1) * 512],
                                in_=ag["wdn"][1][og * 512:(og + 1) * 512,
                                                 hk * P:(hk + 1) * P],
                                transpose=True), 16)
                block.scalar(dn_w)

                def dn_h(sync):
                    sync.wait_ge(s_st.h, h1stores)
                    i = 0
                    for og in range(4):
                        for st in range(J):
                            if i >= 2:
                                sync.wait_ge(s_t.h, t0 + i - 1)
                            s_hp[i % 2].inc(sync.dma_start(
                                out=hq2[:, (i % 2) * NHK * P:
                                        (i % 2 + 1) * NHK * P],
                                in_=hT_stash[:, st * NHK * P:
                                             (st + 1) * NHK * P]), 16)
                            i += 1
                block.sync(dn_h)

                def dn_t(tensor):
                    i = 0
                    for og in range(4):
                        tensor.wait_ge(s_w.h, w0 + 16 * NHK * (og + 1))
                        for st in range(J):
                            pp = i % 2
                            tensor.wait_ge(s_hp[i % 2].h, dn_tgt[i])
                            if i >= 2:
                                tensor.wait_ge(s_v.h, v0 + i - 1)
                            for hk in range(NHK):
                                mm = tensor.matmul(
                                    psDN[:, pp * 512:(pp + 1) * 512],
                                    hq2[:, pp * NHK * P + hk * P:
                                        pp * NHK * P + (hk + 1) * P],
                                    wdnb[:, hk * 512:(hk + 1) * 512],
                                    start=(hk == 0), stop=(hk == NHK - 1))
                            s_t.inc(mm)
                            i += 1
                block.tensor(dn_t)

                def dn_v(vector):
                    i = 0
                    for og in range(4):
                        for st in range(J):
                            pp = i % 2
                            vector.wait_ge(s_t.h, t0 + i + 1)
                            s_v.inc(vector.tensor_copy(
                                y_all[:, st * D + og * 512:st * D + (og + 1) * 512],
                                psDN[:, pp * 512:(pp + 1) * 512]))
                            i += 1
                block.vector(dn_v)
                yall_done_v = s_v.n

                def rms_a(scalar):
                    scalar.wait_ge(s_v.h, yall_done_v)
                    for st in range(J):
                        s_a.inc(scalar.activation(
                            rscr[:, :], y_all[:, st * D:(st + 1) * D], AF.Square,
                            accum_out=sml[:, st:st + 1]))
                        scalar.drain()
                block.scalar(rms_a)

                def rms_v(vector):
                    vector.wait_ge(s_a.h, a0 + J)
                    s_v.inc(vector.tensor_scalar(sml[:, 8:8 + J], sml[:, 0:J],
                                                 1.0 / D, 1e-6, ALU.mult, ALU.add))
                block.vector(rms_v)

                def rms_a2(scalar):
                    scalar.wait_ge(s_v.h, s_v.n)
                    s_a.inc(scalar.sqrt(sml[:, 16:16 + J], sml[:, 8:8 + J]))
                block.scalar(rms_a2)

                def rms_v2(vector):
                    vector.wait_ge(s_a.h, s_a.n)
                    vector.reciprocal(sml[:, 24:24 + J], sml[:, 16:16 + J])
                    vector.drain()
                    ii = 0
                    for st in range(J):
                        for og in range(4):
                            xsl = x_res[:, st * D + og * 512:
                                        st * D + (og + 1) * 512]
                            t3 = tmp3[:, (ii % 2) * 512:(ii % 2 + 1) * 512]
                            vector.scalar_tensor_tensor(
                                t3,
                                y_all[:, st * D + og * 512:st * D + (og + 1) * 512],
                                sml[:, 24 + st:25 + st], xsl, ALU.mult, ALU.add)
                            vector.drain()
                            li = vector.tensor_copy(xsl, t3)
                            ii += 1
                    s_v.inc(li)
                block.vector(rms_v2)

        if tap is None or not tap.startswith("prep"):
            for l in range(depth):
                with nc.sbuf_tensor(f"xn{l}", [P, J * D], BF16) as xn, \
                     nc.sbuf_tensor(f"lnscr{l}", [P, D], F32) as lnscr:
                    xn_ready = ph_ln(l, xn, lnscr)
                    with nc.sbuf_tensor(f"xnT{l}", [P, 16 * L], BF16) as xnT:
                        xnT_ready = ph_transpose(xn, xnT, J, (s_v, xn_ready))
                        ph_qkv(l, xn, xnT, xnT_ready)
                barrier()
                if l == 0:
                    ph_attn(0)
                    if tapret["stop"]:
                        return nc
                    barrier()
                    if tap != "L0":
                        ph_ffn0()
                        if tapret["stop"]:
                            return nc
                        barrier()
                else:
                    ph_ffn1()
                    barrier()
                    ph_attn(1)
                    if tapret["stop"]:
                        return nc
                    barrier()
                if tap == "L0" and l == 0:
                    break
                if tap == "L0a" and l == 0:
                    break

        def ph_out(sync):
            sync.wait_ge(bar, barn[0])
            dst = y_out.ap().rearrange("(t p) d -> p t d", p=P)
            s_o.inc(sync.dma_start(
                out=dst,
                in_=x_res.ap().rearrange("p (t d) -> p t d", d=D)), 16)
            sync.wait_ge(s_o.h, s_o.n)
        block.sync(ph_out)

    return nc



def make_inputs(inputs, J=8):
    L = J * P
    x = np.asarray(inputs["L0_attn_x"])[0]
    in_maps = []
    for c in range(C):
        tiles = [x[(c + 8 * j) * P:(c + 8 * j + 1) * P] for j in range(J)]
        xs = np.ascontiguousarray(np.concatenate(tiles, axis=0))
        pos = np.concatenate([np.arange((c + 8 * j) * P, (c + 8 * j + 1) * P)
                              for j in range(J)])
        sin2, cos2 = _rope_tables(pos)
        m = {
            "x_in": xs.astype(np.float32),
            "sin_in": sin2, "cos_in": cos2,
            "mask_in": _masks(c),
            "wq0_s": np.ascontiguousarray(inputs["L0_attn_w_project"][c * (3 * D // C):(c + 1) * (3 * D // C)]),
            "wo0_s": np.ascontiguousarray(inputs["L0_attn_w_out"][c * (D // C):(c + 1) * (D // C)]),
            "w1_s": np.ascontiguousarray(inputs["L0_ffn_w1"][c * (HID // C):(c + 1) * (HID // C)]),
            "w3_s": np.ascontiguousarray(inputs["L0_ffn_w3"][c * (HID // C):(c + 1) * (HID // C)]),
            "w2_s": np.ascontiguousarray(inputs["L0_ffn_w2"][c * (D // C):(c + 1) * (D // C)]),
            "wq1_s": np.ascontiguousarray(inputs["L1_attn_Wqkv"][c * (3 * D // C):(c + 1) * (3 * D // C)]),
            "wo1_s": np.ascontiguousarray(inputs["L1_attn_WO"][c * (D // C):(c + 1) * (D // C)]),
            "gup_s": np.ascontiguousarray(inputs["L1_ffn_gate_up_packed"][c * (GU // C):(c + 1) * (GU // C)]),
            "gus_s": np.ascontiguousarray(np.asarray(inputs["L1_ffn_gate_up_scales"])[:, c * (GU // C):(c + 1) * (GU // C)]),
            "dnp_s": np.ascontiguousarray(inputs["L1_ffn_down_packed"][c * (D // C):(c + 1) * (D // C)]),
            "dns_s": np.ascontiguousarray(np.asarray(inputs["L1_ffn_down_scales"])[:, c * (D // C):(c + 1) * (D // C)]),
        }
        in_maps.append(m)
    return in_maps


def assemble(results, J=8):
    S = C * J * P
    out = np.empty((1, S, D), dtype=np.float32)
    for c in range(C):
        y = results[c]["y_out"]
        for j in range(J):
            out[0, (c + 8 * j) * P:(c + 8 * j + 1) * P] = y[j * P:(j + 1) * P]
    return out


def kernel(**inputs):
    nc = build(J=8, depth=2)
    in_maps = make_inputs(inputs, J=8)
    res = run_bass_kernel_spmd(nc, in_maps, core_ids=list(range(C)))
    return assemble(res.results, J=8)


# version 09
# speedup vs baseline: 1.5761x; 1.0300x over previous
import math
from contextlib import ExitStack

import numpy as np
import ml_dtypes
import concourse.bass as bass
import concourse.mybir as mybir
from concourse.bass_isa import ReduceOp
from concourse.bass_utils import run_bass_kernel_spmd

F32 = mybir.dt.float32
BF16 = mybir.dt.bfloat16
F16 = mybir.dt.float16
I32 = mybir.dt.int32
AF = mybir.ActivationFunctionType
ALU = mybir.AluOpType
AX = mybir.AxisListType

C = 8
D = 2048
H = 16
DH = 128
HID = 5632
GU = 2 * HID
P = 128
LN2 = math.log(2.0)
C1 = math.log2(3.0) - 1.5


class Sem:
    def __init__(self, nc, name):
        self.h = nc.alloc_semaphore(name)
        self.n = 0

    def inc(self, instr, k=1):
        instr.then_inc(self.h, k)
        self.n += k
        return instr



def _rope_tables(pos):
    i = np.arange(0, DH, 2)
    freqs = np.exp(-math.log(10000.0) * i / DH).astype(np.float32)
    ang = (freqs[:, None] * pos.astype(np.float32)[None, :]).astype(np.float32)
    ang64 = ang.astype(np.float64)
    sin2 = np.repeat(np.sin(ang64), 2, axis=0).astype(np.float32)
    cos2 = np.repeat(np.cos(ang64), 2, axis=0).astype(np.float32)
    return np.ascontiguousarray(sin2), np.ascontiguousarray(cos2)


def _masks(c):
    out = np.zeros((8, P, P), dtype=np.float32)
    kl = np.arange(P)[:, None]
    ql = np.arange(P)[None, :]
    for m in range(8):
        if m < c:
            out[m] = 1.0
        elif m == c:
            out[m] = (kl <= ql).astype(np.float32)
    return out.astype(ml_dtypes.bfloat16)


def _rotmat():
    M = np.zeros((DH, DH), dtype=np.float32)
    for i in range(0, DH, 2):
        M[i, i + 1] = -1.0
        M[i + 1, i] = 1.0
    return np.ascontiguousarray(M.T).astype(ml_dtypes.bfloat16)



def build(J=8, depth=2, tap=None):
    nc = bass.Bass("TRN2", target_bir_lowering=False)
    L = J * P
    NKC = 8 * J

    sems = {}
    _uid = [0]

    def U(name):
        _uid[0] += 1
        return f"{name}_{_uid[0]}"

    def sem(name):
        if name not in sems:
            sems[name] = Sem(nc, name)
        return sems[name]

    x_in = nc.dram_tensor("x_in", [L, D], F32, kind="ExternalInput")
    sin_in = nc.dram_tensor("sin_in", [DH, L], F32, kind="ExternalInput")
    cos_in = nc.dram_tensor("cos_in", [DH, L], F32, kind="ExternalInput")
    mask_in = nc.dram_tensor("mask_in", [8, P, P], BF16, kind="ExternalInput")
    wq0_s = nc.dram_tensor("wq0_s", [3 * D // C, D], F32, kind="ExternalInput")
    wo0_s = nc.dram_tensor("wo0_s", [D // C, D], F32, kind="ExternalInput")
    w1_s = nc.dram_tensor("w1_s", [HID // C, D], F32, kind="ExternalInput")
    w3_s = nc.dram_tensor("w3_s", [HID // C, D], F32, kind="ExternalInput")
    w2_s = nc.dram_tensor("w2_s", [D // C, HID], F32, kind="ExternalInput")
    wq1_s = nc.dram_tensor("wq1_s", [3 * D // C, D], F32, kind="ExternalInput")
    wo1_s = nc.dram_tensor("wo1_s", [D // C, D], F32, kind="ExternalInput")
    gup_s = nc.dram_tensor("gup_s", [GU // C, D // 8], I32, kind="ExternalInput")
    gus_s = nc.dram_tensor("gus_s", [D // P, GU // C], F16, kind="ExternalInput")
    dnp_s = nc.dram_tensor("dnp_s", [D // C, HID // 8], I32, kind="ExternalInput")
    dns_s = nc.dram_tensor("dns_s", [HID // P, D // C], F16, kind="ExternalInput")
    y_out = nc.dram_tensor("y_out", [L, D], F32, kind="ExternalOutput")

    ag = {}
    ag_shapes = (
        ("wq0", 3 * D // C, D), ("wo0", D // C, D), ("w1", HID // C, D),
        ("w3", HID // C, D), ("w2", D // C, HID), ("wq1", 3 * D // C, D),
        ("wo1", D // C, D), ("wgu", GU // C, D), ("wdn", D // C, HID))
    for name, rows, cols in ag_shapes:
        i = nc.dram_tensor(f"agi_{name}", [rows, cols], BF16, kind="Internal")
        o = nc.dram_tensor(f"ago_{name}", [C * rows, cols], BF16, kind="Internal")
        ag[name] = (i, o)

    tsum_i = nc.dram_tensor("tsum_i", [1, 4], F32, kind="Internal")
    tsum_o = nc.dram_tensor("tsum_o", [1, 4], F32, kind="Internal")
    psum_scr = nc.dram_tensor("psum_scr", [P, 4], F32, kind="Internal")
    rep_scr = nc.dram_tensor("rep_scr", [P, 4], F32, kind="Internal")

    kvt = []
    for l in range(depth):
        ki = nc.dram_tensor(f"kvKi{l}", [D, L], BF16, kind="Internal")
        ko = nc.dram_tensor(f"kvKo{l}", [C, D, L], BF16, kind="Internal")
        vi = nc.dram_tensor(f"kvVi{l}", [16, P, L], BF16, kind="Internal")
        vo = nc.dram_tensor(f"kvVo{l}", [C, 16, P, L], BF16, kind="Internal")
        kvt.append((ki, ko, vi, vo))

    q_stash = nc.dram_tensor("q_stash", [P, 16 * L], BF16, kind="Internal")
    xn_stash = nc.dram_tensor("xn_stash", [L, D], BF16, kind="Internal")
    xnT_stash = nc.dram_tensor("xnT_stash", [P, 16 * L], BF16, kind="Internal")
    hT_stash = nc.dram_tensor("hT_stash", [P, (HID // P) * L], BF16, kind="Internal")

    ident_c = nc.inline_tensor(np.eye(P, dtype=ml_dtypes.bfloat16), "ident_c")
    onec_c = nc.inline_tensor(np.ones((P, 1), dtype=ml_dtypes.bfloat16), "onec_c")
    oner_c = nc.inline_tensor(np.ones((1, P), dtype=ml_dtypes.bfloat16), "oner_c")
    onerf_c = nc.inline_tensor(np.ones((1, P), dtype=np.float32), "onerf_c")
    rot_cst = nc.inline_tensor(_rotmat(), "rot_cst")

    with ExitStack() as ctx:
        e = ctx.enter_context
        block = e(nc.Block())

        x_res = e(nc.sbuf_tensor("x_res", [P, J * D], F32))
        id_s = e(nc.sbuf_tensor("id_s", [P, P], BF16))
        rot_s = e(nc.sbuf_tensor("rot_s", [P, P], BF16))
        mask_s = e(nc.sbuf_tensor("mask_s", [P, 8 * P], BF16))
        onec_s = e(nc.sbuf_tensor("onec_s", [P, 1], BF16))
        oner_s = e(nc.sbuf_tensor("oner_s", [1, P], BF16))
        onerf_s = e(nc.sbuf_tensor("onerf_s", [1, P], F32))
        rope_s = e(nc.sbuf_tensor("rope_s", [P, 2 * L], F32))
        sin_s = rope_s[:, 0:L]
        cos_s = rope_s[:, L:2 * L]
        sml = e(nc.sbuf_tensor("sml", [P, 64], F32))

        s_ld = sem("ld")
        s_g = sem("gd")
        s_t = sem("t")
        s_v = sem("v")
        s_a = sem("a")
        s_w = sem("w")
        s_kv = sem("kvd")
        s_st = sem("st")
        s_wp = [sem("wpa"), sem("wpb")]
        s_sp = [sem("spa"), sem("spb")]
        s_hp = [sem("hpa"), sem("hpb")]
        s_o = sem("o")
        bar = nc.alloc_semaphore("bar")
        barn = [0]

        def barrier():
            barn[0] += 5
            tgt = barn[0]
            for name in ("sync", "scalar", "tensor", "vector", "gpsimd"):
                def f(eng, tgt=tgt, name=name):
                    if name in ("scalar", "vector"):
                        eng.drain()
                    eng.sem_inc(bar, 1)
                    eng.wait_ge(bar, tgt)
                getattr(block, name)(f)

        def tap_out(src_ap, ncols, wait_sems=()):
            nt = ncols // D
            def f(gpsimd):
                for ss, vv in wait_sems:
                    gpsimd.wait_ge(ss.h, vv)
                dst = y_out.ap().rearrange("(t p) d -> p t d", p=P)
                s_o.inc(gpsimd.dma_start(
                    out=dst[:, 0:nt, :],
                    in_=src_ap.rearrange("p (t d) -> p t d", d=D)), 16)
                gpsimd.wait_ge(s_o.h, s_o.n)
            block.gpsimd(f)

        s_c0 = sem("ldc")

        def ph0(sync):
            xr = x_in.ap().rearrange("(t p) d -> p t d", p=P)
            s_c0.inc(sync.dma_start(
                out=x_res.ap().rearrange("p (t d) -> p t d", d=D), in_=xr), 16)
            s_c0.inc(sync.dma_start(out=id_s[:, :], in_=ident_c[:, :]), 16)
            s_c0.inc(sync.dma_start(out=rot_s[:, :], in_=rot_cst[:, :]), 16)
            for m in range(8):
                s_c0.inc(sync.dma_start(out=mask_s[:, m * P:(m + 1) * P],
                                        in_=mask_in[m, :, :]), 16)
            s_c0.inc(sync.dma_start(out=onec_s[:, :], in_=onec_c[:, :]), 16)
            s_c0.inc(sync.dma_start(out=oner_s[:, :], in_=oner_c[:, :]), 16)
            s_c0.inc(sync.dma_start(out=onerf_s[:, :], in_=onerf_c[:, :]), 16)
            s_c0.inc(sync.dma_start(out=sin_s, in_=sin_in[:, :]), 16)
            s_c0.inc(sync.dma_start(out=cos_s, in_=cos_in[:, :]), 16)
        block.sync(ph0)
        consts_done = s_c0.n

        ag_sem = {name: sem(f"ag_{name}") for name, _, _ in ag_shapes}
        ar_sem = sem("ar_tern")

        def ph1a(gpsimd):
            g0 = s_g.n
            for name, src in (("wq0", wq0_s), ("wo0", wo0_s),
                              ("wq1", wq1_s), ("wo1", wo1_s)):
                s_g.inc(gpsimd.dma_start(out=ag[name][0][:, :], in_=src[:, :]), 16)
            gpsimd.wait_ge(s_g.h, s_g.n)
            for name in ("wq0", "wo0", "wq1", "wo1"):
                ag_sem[name].inc(gpsimd.collective_compute(
                    "AllGather", ALU.bypass, replica_groups=[list(range(C))],
                    ins=[ag[name][0].ap().opt()], outs=[ag[name][1].ap().opt()]))
        block.gpsimd(ph1a)

        with nc.sbuf_tensor("pL", [P, HID], F32) as pL, \
             nc.sbuf_tensor("pI", [P, HID], I32) as pI, \
             nc.sbuf_tensor("pOT", [P, HID], BF16) as pOT, \
             nc.sbuf_tensor("pF", [P, 6 * 2048], F32) as pF, \
             nc.sbuf_tensor("pOb", [P, 2 * 2048], BF16) as pOb, \
             nc.sbuf_tensor("sc16", [64, P], F16) as sc16, \
             nc.sbuf_tensor("sc16b", [64, P], BF16) as sc16b, \
             nc.sbuf_tensor("scT_all", [P, 264], F32) as scT_all, \
             nc.sbuf_tensor("pSt", [4, P], F32) as pSt, \
             nc.sbuf_tensor("pSr", [4, 1], F32) as pSr, \
             nc.psum_tensor("psT", [P, 64], BF16) as psT:

            Fb = [pF[:, i * 2048:(i + 1) * 2048] for i in range(6)]
            Ob = [pOb[:, i * 2048:(i + 1) * 2048] for i in range(2)]

            tern = [("w1", w1_s, HID // C, D), ("w3", w3_s, HID // C, D),
                    ("w2", w2_s, D // C, HID)]
            tiles_t = []
            for mi, (name, src, rows, cols) in enumerate(tern):
                r0 = 0
                while r0 < rows:
                    nr = min(P, rows - r0)
                    tiles_t.append((mi, name, src, r0, nr, cols))
                    r0 += nr
            nts = [sum(1 for tt in tiles_t if tt[0] == mi) for mi in range(3)]
            offs = [0, nts[0], nts[0] + nts[1], nts[0] + nts[1] + nts[2]]

            s_lt1 = sem("lt1")
            v0 = s_v.n

            def t_loads(sync):
                for k, (mi, name, src, r0, nr, cols) in enumerate(tiles_t):
                    if k >= 1:
                        sync.wait_ge(s_v.h, v0 + k + 1)
                    s_lt1.inc(sync.dma_start(out=pL[0:nr, 0:cols],
                                             in_=src[r0:r0 + nr, :]), 16)

            def t_abs(vector):
                vector.memset(sml[:, 0:len(tiles_t)], 0.0)
                vector.drain()
                s_v.inc(vector.engine_nop())
                for k, (mi, name, src, r0, nr, cols) in enumerate(tiles_t):
                    vector.wait_ge(s_lt1.h, 16 * (k + 1))
                    s_v.inc(vector.tensor_reduce(
                        sml[0:nr, k:k + 1], pL[0:nr, 0:cols], AX.X, ALU.add,
                        apply_absolute_value=True))
                vector.drain()
                for mi in range(3):
                    vector.tensor_reduce(sml[:, 32 + mi:33 + mi],
                                         sml[:, offs[mi]:offs[mi + 1]],
                                         AX.X, ALU.add)
                s_v.inc(vector.memset(sml[:, 35:36], 0.0))
            block.sync(t_loads)
            block.vector(t_abs)
            absums_done = s_v.n

            s_vr = sem("vred")

            def t_pred1(gpsimd):
                gpsimd.wait_ge(s_v.h, absums_done)
                s_g.inc(gpsimd.dma_start(out=psum_scr[:, :], in_=sml[:, 32:36]), 16)
                gpsimd.wait_ge(s_g.h, s_g.n)
                s_g.inc(gpsimd.dma_start(
                    out=pSt[:, :], in_=psum_scr.ap().rearrange("p f -> f p"),
                    allow_slow_non_contiguous=True), 16)
                gpsimd.wait_ge(s_g.h, s_g.n)
            block.gpsimd(t_pred1)
            g_tr = s_g.n

            def t_red(vector):
                vector.wait_ge(s_g.h, g_tr)
                s_vr.inc(vector.tensor_reduce(pSr[:, :], pSt[:, :], AX.X, ALU.add))
            block.vector(t_red)

            def t_pred2(gpsimd):
                gpsimd.wait_ge(s_vr.h, s_vr.n)
                s_g.inc(gpsimd.dma_start(
                    out=tsum_i.ap()[0:1, 0:4].rearrange("o f -> f o"),
                    in_=pSr[:, 0:1]), 16)
                gpsimd.wait_ge(s_g.h, s_g.n)
                ar_sem.inc(gpsimd.collective_compute(
                    "AllReduce", ALU.add, replica_groups=[list(range(C))],
                    ins=[tsum_i.ap().opt()], outs=[tsum_o.ap().opt()]))
                gpsimd.wait_ge(ar_sem.h, ar_sem.n)
                repf = rep_scr.ap().rearrange("p f -> (p f)")
                s_g.inc(gpsimd.dma_start(out=repf[0:4], in_=tsum_o.ap().rearrange("o f -> (o f)")), 16)
                gpsimd.wait_ge(s_g.h, s_g.n)
                n = 4
                while n < 512:
                    s_g.inc(gpsimd.dma_start(out=repf[n:2 * n], in_=repf[0:n]), 16)
                    gpsimd.wait_ge(s_g.h, s_g.n)
                    n *= 2
                s_g.inc(gpsimd.dma_start(out=sml[:, 48:52], in_=rep_scr[:, :]), 16)
                gpsimd.wait_ge(s_g.h, s_g.n)
                s_g.inc(gpsimd.dma_start(out=sml[0:1, 44:48], in_=tsum_o[:, :]), 16)
                gpsimd.wait_ge(s_g.h, s_g.n)
            block.gpsimd(t_pred2)
            pred_done = s_g.n

            def t_scale(vector):
                vector.wait_ge(s_g.h, pred_done)
                for mi, (name, src, rows, cols) in enumerate(tern):
                    n_el = float(rows * C * cols)
                    vector.tensor_scalar(sml[:, 52 + mi:53 + mi],
                                         sml[:, 48 + mi:49 + mi],
                                         1.0 / n_el, 1e-5, ALU.mult, ALU.max)
                vector.drain()
                s_v.inc(vector.reciprocal(sml[:, 56:59], sml[:, 52:55]))
            block.vector(t_scale)
            scale_ready = s_v.n

            s_lt2 = sem("lt2")
            s_gs = sem("gs")
            v1 = s_v.n

            def t2_loads(sync):
                sync.wait_ge(s_v.h, scale_ready)
                for k, (mi, name, src, r0, nr, cols) in enumerate(tiles_t):
                    if k >= 1:
                        sync.wait_ge(s_v.h, v1 + k)
                    s_lt2.inc(sync.dma_start(out=pL[0:nr, 0:cols],
                                             in_=src[r0:r0 + nr, :]), 16)

            def t2_q(vector):
                for k, (mi, name, src, r0, nr, cols) in enumerate(tiles_t):
                    vector.wait_ge(s_lt2.h, 16 * (k + 1))
                    if k >= 1:
                        vector.wait_ge(s_gs.h, 16 * k)
                    li = None
                    for c0 in range(0, cols, 2048):
                        cw = min(2048, cols - c0)
                        vector.tensor_scalar(Fb[0][0:nr, 0:cw], pL[0:nr, c0:c0 + cw],
                                             sml[0:nr, 56 + mi:57 + mi], None,
                                             ALU.mult)
                        vector.drain()
                        vector.tensor_copy(pI[0:nr, c0:c0 + cw], Fb[0][0:nr, 0:cw])
                        vector.drain()
                        li = vector.tensor_scalar(pOT[0:nr, c0:c0 + cw], pI[0:nr, c0:c0 + cw],
                                             1, -1, ALU.min, ALU.max)
                    s_v.inc(li)

            def t2_st(gpsimd):
                for k, (mi, name, src, r0, nr, cols) in enumerate(tiles_t):
                    gpsimd.wait_ge(s_v.h, v1 + k + 1)
                    s_gs.inc(gpsimd.dma_start(
                        out=ag[name][0][r0:r0 + nr, :],
                        in_=pOT[0:nr, 0:cols]), 16)
            block.sync(t2_loads)
            block.vector(t2_q)
            block.gpsimd(t2_st)
            tern_stores = s_gs.n

            def t_ag(gpsimd):
                gpsimd.wait_ge(s_gs.h, tern_stores)
                for name in ("w1", "w3", "w2"):
                    ag_sem[name].inc(gpsimd.collective_compute(
                        "AllGather", ALU.bypass, replica_groups=[list(range(C))],
                        ins=[ag[name][0].ap().opt()], outs=[ag[name][1].ap().opt()]))
            block.gpsimd(t_ag)

            fp4 = [("wgu", gup_s, gus_s, GU // C, D),
                   ("wdn", dnp_s, dns_s, D // C, HID)]
            tiles_f = [(name, pk, sc, t, cols)
                       for name, pk, sc, rows, cols in fp4
                       for t in range(rows // P)]
            scT_off = []
            off = 0
            for name, pk, sc, t, cols in tiles_f:
                scT_off.append(off)
                off += cols // P
            chunks_per = [(cols + 2047) // 2048 for _, _, _, _, cols in tiles_f]

            s_lf = sem("lf")
            s_gf = sem("gf")
            t2c = s_t.n
            a2 = s_a.n

            v_base = s_v.n
            v_cvt, v_f, v_chunk_done, v_tile_done = [], [], [], []
            vv = v_base
            ci_total = 0
            for k in range(len(tiles_f)):
                vv += 1
                v_cvt.append(vv)
                vv += 1
                fs, ds = [], []
                for c in range(chunks_per[k]):
                    vv += 1
                    fs.append(vv)
                    vv += 1
                    ds.append(vv)
                v_f.append(fs)
                v_chunk_done.append(ds)
                v_tile_done.append(vv)
                ci_total += chunks_per[k]

            def f_loads(sync):
                sync.wait_ge(s_v.h, v1 + len(tiles_t))
                for k, (name, pk, sc, t, cols) in enumerate(tiles_f):
                    if k >= 1:
                        sync.wait_ge(s_gf.h, 16 * k)
                    s_lf.inc(sync.dma_start(
                        out=pI[:, 0:cols // 8],
                        in_=pk[t * P:(t + 1) * P, :]), 16)
                    s_lf.inc(sync.dma_start(
                        out=sc16[0:cols // P, :],
                        in_=sc[:, t * P:(t + 1) * P]), 16)
            block.sync(f_loads)

            def f_sct(tensor):
                tensor.wait_ge(s_c0.h, consts_done)
                for k, (name, pk, sc, t, cols) in enumerate(tiles_f):
                    ng = cols // P
                    tensor.wait_ge(s_v.h, v_cvt[k])
                    s_t.inc(tensor.transpose(psT[:, 0:ng], sc16b[0:ng, :],
                                             id_s[0:ng, 0:ng]))
            block.tensor(f_sct)

            def f_dec_v(vector):
                vector.wait_ge(s_gs.h, tern_stores)
                for k, (name, pk, sc, t, cols) in enumerate(tiles_f):
                    ng = cols // P
                    vector.wait_ge(s_lf.h, 32 * (k + 1))
                    if k >= 1:
                        vector.wait_ge(s_gf.h, 16 * k)
                    s_v.inc(vector.tensor_copy(sc16b[0:ng, :], sc16[0:ng, :]))
                    vector.wait_ge(s_t.h, t2c + k + 1)
                    s_v.inc(vector.tensor_copy(
                        scT_all[:, scT_off[k]:scT_off[k] + ng], psT[:, 0:ng]))
                    for cidx, c0 in enumerate(range(0, cols, 2048)):
                        cw = min(2048, cols - c0)
                        cw8 = cw // 8
                        i0 = c0 // 8
                        ia, ib = 2048, 2048 + 1024
                        for nk in range(8):
                            vector.tensor_scalar(
                                pI[:, ia:ia + cw8], pI[:, i0:i0 + cw8],
                                4 * nk, 15, ALU.logical_shift_right,
                                ALU.bitwise_and)
                            vector.drain()
                            vector.tensor_copy(Fb[0][:, nk:cw:8],
                                               pI[:, ia:ia + cw8])
                            vector.tensor_scalar(
                                pI[:, ib:ib + cw8], pI[:, ia:ia + cw8],
                                1, None, ALU.bitwise_and)
                            vector.drain()
                            vector.tensor_copy(Fb[1][:, nk:cw:8],
                                               pI[:, ib:ib + cw8])
                        An, Bm = Fb[0][:, 0:cw], Fb[1][:, 0:cw]
                        F2, F3, F4, F5 = (Fb[2][:, 0:cw], Fb[3][:, 0:cw],
                                          Fb[4][:, 0:cw], Fb[5][:, 0:cw])
                        Sg = Ob[0][:, 0:cw]
                        vector.tensor_scalar(F2, An, 8.0, None, ALU.is_ge)
                        vector.drain()
                        vector.tensor_scalar(Sg, F2, -2.0, 1.0, ALU.mult, ALU.add)
                        vector.tensor_scalar(F3, An, 0.5, -1.0, ALU.mult, ALU.add)
                        vector.drain()
                        vector.scalar_tensor_tensor(F4, F2, -4.0, F3,
                                                    ALU.mult, ALU.add)
                        vector.drain()
                        vector.scalar_tensor_tensor(F5, Bm, C1, F4,
                                                    ALU.mult, ALU.add)
                        vector.tensor_scalar(F2, F4, -0.25, None, ALU.is_le)
                        vector.tensor_scalar(F3, F4, 396.83008, 197.83008,
                                             ALU.mult, ALU.add)
                        vector.drain()
                        vector.tensor_tensor(F4, F2, F3, ALU.mult)
                        vector.drain()
                        s_v.inc(vector.tensor_tensor(F2, F5, F4, ALU.add))
                        vector.wait_ge(s_a.h, a2 + sum(chunks_per[:k]) + cidx + 1)
                        vector.tensor_tensor(Ob[1][:, 0:cw], F3, Sg, ALU.mult)
                        vector.drain()
                        ng0 = scT_off[k] + c0 // P
                        li = None
                        for g in range(cw // P):
                            li = vector.tensor_scalar(
                                pOT[:, c0 + g * P:c0 + (g + 1) * P],
                                Ob[1][:, g * P:(g + 1) * P],
                                scT_all[:, ng0 + g:ng0 + g + 1], None, ALU.mult)
                        s_v.inc(li)
            block.vector(f_dec_v)

            def f_dec_a(scalar):
                for k, (name, pk, sc, t, cols) in enumerate(tiles_f):
                    for cidx, c0 in enumerate(range(0, cols, 2048)):
                        cw = min(2048, cols - c0)
                        scalar.wait_ge(s_v.h, v_f[k][cidx])
                        s_a.inc(scalar.activation(Fb[3][:, 0:cw], Fb[2][:, 0:cw],
                                                  AF.Exp, scale=LN2))
            block.scalar(f_dec_a)

            def f_dec_st(gpsimd):
                for k, (name, pk, sc, t, cols) in enumerate(tiles_f):
                    gpsimd.wait_ge(s_v.h, v_tile_done[k])
                    s_gf.inc(gpsimd.dma_start(
                        out=ag[name][0][t * P:(t + 1) * P, :],
                        in_=pOT[:, 0:cols]), 16)
            block.gpsimd(f_dec_st)
            fp4_stores = s_gf.n

            def f_ag(gpsimd):
                gpsimd.wait_ge(s_gf.h, fp4_stores)
                for name in ("wgu", "wdn"):
                    ag_sem[name].inc(gpsimd.collective_compute(
                        "AllGather", ALU.bypass, replica_groups=[list(range(C))],
                        ins=[ag[name][0].ap().opt()], outs=[ag[name][1].ap().opt()]))
            block.gpsimd(f_ag)

            if tap is not None and tap.startswith("prep"):
                wname = tap.split(":")[1] if ":" in tap else "w1"
                wcols = dict((n, c) for n, r, c in ag_shapes)[wname]
                wcols = min(wcols, J * D)
                with nc.sbuf_tensor("dmp", [P, wcols], BF16) as dmp, \
                     nc.sbuf_tensor("dmpf", [P, wcols], F32) as dmpf:
                    s_dmp = sem("dmp")

                    def d1(sync):
                        sync.wait_ge(ag_sem[wname].h, 1)
                        s_dmp.inc(sync.dma_start(out=dmp[:, :],
                                                 in_=ag[wname][1][0:P, 0:wcols]), 16)

                    def d2(vector):
                        vector.wait_ge(s_dmp.h, s_dmp.n)
                        s_v.inc(vector.tensor_copy(dmpf[:, :], dmp[:, :]))
                    block.sync(d1)
                    block.vector(d2)
                    tap_out(dmpf[:, :], wcols, [(s_v, s_v.n)])
                    return nc

        barrier()

        NS = max(1, L // 512)
        SW = min(512, L)

        def spans(lo, hi):
            out = []
            a = lo
            while a < hi:
                b = min(hi, (a // 512 + 1) * 512)
                out.append((a, b))
                a = b
            return out

        agk_sem = [sem(f"agk{l}") for l in range(depth)]
        agv_sem = [sem(f"agv{l}") for l in range(depth)]
        tapret = {"stop": False}

        def do_tap(name, src_ap, ncols, wait_sems):
            if tap == name:
                tap_out(src_ap, ncols, wait_sems)
                tapret["stop"] = True

        def ph_ln(l, xn, lnscr):
            a0, v0 = s_a.n, s_v.n

            def vsum(vector):
                vector.wait_ge(s_c0.h, consts_done)
                if l > 0:
                    vector.wait_ge(bar, barn[0])
                vector.drain()
                for t in range(J):
                    s_v.inc(vector.tensor_reduce(
                        sml[:, t:t + 1], x_res[:, t * D:(t + 1) * D],
                        AX.X, ALU.add))
            block.vector(vsum)

            def asq(scalar):
                scalar.wait_ge(s_c0.h, consts_done)
                if l > 0:
                    scalar.wait_ge(bar, barn[0])
                for t in range(J):
                    s_a.inc(scalar.activation(
                        lnscr[:, :], x_res[:, t * D:(t + 1) * D], AF.Square,
                        accum_out=sml[:, 8 + t:9 + t]))
                    scalar.drain()
            block.scalar(asq)

            def vstats(vector):
                vector.wait_ge(s_a.h, a0 + J)
                vector.drain()
                vector.tensor_scalar(sml[:, 16:16 + J], sml[:, 0:J],
                                     1.0 / D, None, ALU.mult)
                vector.drain()
                vector.tensor_scalar(sml[:, 24:24 + J], sml[:, 8:8 + J],
                                     1.0 / D, None, ALU.mult)
                vector.tensor_tensor(sml[:, 32:32 + J], sml[:, 16:16 + J],
                                     sml[:, 16:16 + J], ALU.mult)
                vector.drain()
                vector.tensor_tensor(sml[:, 8:8 + J], sml[:, 24:24 + J],
                                     sml[:, 32:32 + J], ALU.subtract)
                vector.drain()
                s_v.inc(vector.tensor_scalar(sml[:, 24:24 + J], sml[:, 8:8 + J],
                                             1e-5, None, ALU.add))
            block.vector(vstats)

            def asqrt(scalar):
                scalar.wait_ge(s_v.h, v0 + J + 1)
                s_a.inc(scalar.sqrt(sml[:, 32:32 + J], sml[:, 24:24 + J]))
            block.scalar(asqrt)

            def vrecip(vector):
                vector.wait_ge(s_a.h, a0 + J + 1)
                vector.reciprocal(sml[:, 0:J], sml[:, 32:32 + J])
                vector.drain()
                s_v.inc(vector.scalar_tensor_tensor(
                    sml[:, 8:8 + J], sml[:, 16:16 + J],
                    -1.0, sml[:, 0:J], ALU.mult, ALU.mult))
            block.vector(vrecip)

            def anorm(vector):
                vector.wait_ge(s_v.h, v0 + J + 2)
                li = None
                for t in range(J):
                    li = vector.tensor_scalar(
                        xn[:, t * D:(t + 1) * D], x_res[:, t * D:(t + 1) * D],
                        sml[:, t:t + 1], sml[:, 8 + t:9 + t],
                        ALU.mult, ALU.add)
                s_v.inc(li)
            block.vector(anorm)
            return s_v.n

        def ph_transpose(src, dst, ntile, ready, src_cols=D):
            nblk = src_cols // P
            t0, v0 = s_t.n, s_v.n
            with nc.psum_tensor(U("psTTa"), [P, P], BF16) as psTTa, \
                 nc.psum_tensor(U("psTTb"), [P, P], BF16) as psTTb:
                psTT = [psTTa, psTTb]
                def tt_(tensor):
                    tensor.wait_ge(ready[0].h, ready[1])
                    i = 0
                    for dk in range(nblk):
                        for t in range(ntile):
                            par = i % 2
                            if i >= 2:
                                tensor.wait_ge(s_v.h, v0 + i - 1)
                            s_t.inc(tensor.transpose(
                                psTT[par][:, :],
                                src[:, t * src_cols + dk * P:
                                    t * src_cols + (dk + 1) * P],
                                id_s[:, :]))
                            i += 1
                block.tensor(tt_)

                def tv_(vector):
                    i = 0
                    for dk in range(nblk):
                        for t in range(ntile):
                            par = i % 2
                            vector.wait_ge(s_t.h, t0 + i + 1)
                            s_v.inc(vector.tensor_copy(
                                dst[:, dk * L + t * P:dk * L + (t + 1) * P],
                                psTT[par][:, :]))
                            i += 1
                block.vector(tv_)
            return s_v.n


        def ph_qkv(l, xn, xnT, xnT_ready_v):
            wname = "wq0" if l == 0 else "wq1"
            wfull = ag[wname][1]
            rope = (l == 1)
            ki, ko, vi, vo = kvt[l]
            w0, st0 = s_w.n, s_st.n
            sp_base = [s_sp[0].n, s_sp[1].n]
            NOB = (3 * D) // 512
            tiles = []
            tt, vv = s_t.n, s_v.n
            for ob in range(NOB):
                for ot in range(4):
                    o = ob * 4 + ot
                    for sh in range(NS):
                        d = {"ob": ob, "o": o, "sh": sh}
                        tt += 1
                        d["t_grp"] = tt
                        vv += 1
                        d["v_evac"] = vv
                        if rope and o < 32:
                            tt += 1
                            d["t_rot"] = tt
                            vv += 1
                            d["v_rot"] = vv
                        elif o >= 32:
                            nss = SW // P
                            d["t_vt"] = []
                            d["v_vt"] = []
                            for ss in range(nss):
                                tt += 1
                                d["t_vt"].append(tt)
                                vv += 1
                                d["v_vt"].append(vv)
                        d["t_done"] = tt
                        d["v_done"] = vv
                        tiles.append(d)
            with nc.sbuf_tensor(U("wbufA"), [P, 16 * 512], BF16) as wbufA, \
                 nc.sbuf_tensor(U("wbufB"), [P, 16 * 512], BF16) as wbufB, \
                 nc.sbuf_tensor(U("qs0"), [P, 2 * SW], BF16) as qs0, \
                 nc.sbuf_tensor(U("qs1"), [P, 2 * SW], BF16) as qs1, \
                 nc.sbuf_tensor(U("vst"), [P, 2 * SW], BF16) as vst, \
                 nc.sbuf_tensor(U("rm1"), [P, SW], F32) as rm1, \
                 nc.sbuf_tensor(U("rm2"), [P, SW], F32) as rm2, \
                 nc.psum_tensor(U("psQ"), [P, 2 * 512], F32) as psQ, \
                 nc.psum_tensor(U("psR"), [P, 512], F32) as psR, \
                 nc.psum_tensor(U("psVTa"), [P, P], BF16) as psVTa, \
                 nc.psum_tensor(U("psVTb"), [P, P], BF16) as psVTb:
                psVT = [psVTa, psVTb]
                wbufs = [wbufA, wbufB]

                wq_tgt = [s_wp[0].n + 256 * (ob // 2 + 1) if ob % 2 == 0
                          else s_wp[1].n + 256 * (ob // 2 + 1)
                          for ob in range(NOB)]

                def qkv_w(scalar):
                    scalar.wait_ge(ag_sem[wname].h, 1)
                    scalar.wait_ge(s_v.h, xnT_ready_v)
                    for ob in range(NOB):
                        if ob >= 2:
                            scalar.wait_ge(s_t.h,
                                           tiles[(ob - 1) * 4 * NS - 1]["t_done"])
                        for dk in range(16):
                            s_wp[ob % 2].inc(scalar.dma_start(
                                out=wbufs[ob % 2][:, dk * 512:(dk + 1) * 512],
                                in_=wfull[ob * 512:(ob + 1) * 512,
                                          dk * P:(dk + 1) * P],
                                transpose=True), 16)
                block.scalar(qkv_w)

                def qkv_t(tensor):
                    tensor.wait_ge(s_v.h, xnT_ready_v)
                    for i, d in enumerate(tiles):
                        ob, o, sh = d["ob"], d["o"], d["sh"]
                        pp = i % 2
                        if sh == 0 and d["o"] % 4 == 0:
                            tensor.wait_ge(s_wp[ob % 2].h, wq_tgt[ob])
                        if i >= 2:
                            tensor.wait_ge(s_v.h, tiles[i - 2]["v_done"])
                        for dk in range(16):
                            mm = tensor.matmul(
                                psQ[:, pp * 512:pp * 512 + SW],
                                wbufs[ob % 2][:, dk * 512 + ot_col(o):
                                              dk * 512 + ot_col(o) + P],
                                xnT[:, dk * L + sh * 512:dk * L + sh * 512 + SW],
                                start=(dk == 0), stop=(dk == 15))
                        s_t.inc(mm)
                        if "t_rot" in d:
                            tensor.wait_ge(s_v.h, d["v_evac"])
                            s_t.inc(tensor.matmul(
                                psR[:, 0:SW], rot_s[:, :],
                                qs0[:, pp * SW:pp * SW + SW],
                                start=True, stop=True))
                        elif "t_vt" in d:
                            tensor.wait_ge(s_v.h, d["v_evac"])
                            for ss in range(len(d["t_vt"])):
                                if ss >= 2:
                                    tensor.wait_ge(s_v.h, d["v_vt"][ss - 2])
                                s_t.inc(tensor.transpose(
                                    psVT[ss % 2][:, :],
                                    qs0[:, pp * SW + ss * P:pp * SW + (ss + 1) * P],
                                    id_s[:, :]))
                block.tensor(qkv_t)

                def qkv_v(vector):
                    for i, d in enumerate(tiles):
                        o, sh = d["o"], d["sh"]
                        pp = i % 2
                        vector.wait_ge(s_t.h, d["t_grp"])
                        if i >= 2:
                            vector.wait_ge(s_sp[i % 2].h,
                                           sp_base[i % 2] + 16 * (i // 2))
                        s_v.inc(vector.tensor_copy(
                            qs0[:, pp * SW:pp * SW + SW],
                            psQ[:, pp * 512:pp * 512 + SW]))
                        if "t_rot" in d:
                            vector.drain()
                        if "t_rot" in d:
                            vector.wait_ge(s_t.h, d["t_rot"])
                            cs = sh * 512
                            vector.tensor_tensor(
                                rm1[:, 0:SW], qs0[:, pp * SW:pp * SW + SW],
                                cos_s[:, cs:cs + SW], ALU.mult)
                            vector.tensor_tensor(
                                rm2[:, 0:SW], psR[:, 0:SW],
                                sin_s[:, cs:cs + SW], ALU.mult)
                            vector.drain()
                            s_v.inc(vector.tensor_tensor(
                                qs1[:, pp * SW:pp * SW + SW],
                                rm1[:, 0:SW], rm2[:, 0:SW], ALU.add))
                        elif "t_vt" in d:
                            for ss in range(len(d["t_vt"])):
                                vector.wait_ge(s_t.h, d["t_vt"][ss])
                                s_v.inc(vector.tensor_copy(
                                    vst[:, pp * SW + ss * P:pp * SW + (ss + 1) * P],
                                    psVT[ss % 2][:, :]))
                block.vector(qkv_v)

                def qkv_st(gpsimd):
                    for i, d in enumerate(tiles):
                        o, sh = d["o"], d["sh"]
                        pp = i % 2
                        gpsimd.wait_ge(s_v.h, d["v_done"])
                        if o < 16:
                            srcb = qs1 if rope else qs0
                            s_sp[i % 2].inc(gpsimd.dma_start(
                                out=q_stash[:, o * L + sh * 512:
                                            o * L + sh * 512 + SW],
                                in_=srcb[:, pp * SW:pp * SW + SW]), 16)
                        elif o < 32:
                            srcb = qs1 if rope else qs0
                            s_sp[i % 2].inc(gpsimd.dma_start(
                                out=ki[(o - 16) * P:(o - 15) * P,
                                       sh * 512:sh * 512 + SW],
                                in_=srcb[:, pp * SW:pp * SW + SW]), 16)
                        else:
                            s_sp[i % 2].inc(gpsimd.dma_start(
                                out=vi.ap()[o - 32, :,
                                            sh * 512:sh * 512 + SW],
                                in_=vst[:, pp * SW:pp * SW + SW]), 16)
                    if l == 0:
                        xr = xn_stash.ap().rearrange("(t p) d -> p t d", p=P)
                        s_st.inc(gpsimd.dma_start(
                            out=xr,
                            in_=xn.ap().rearrange("p (t d) -> p t d", d=D)), 16)
                    else:
                        s_st.inc(gpsimd.dma_start(
                            out=xnT_stash[:, :], in_=xnT[:, :]), 16)
                    gpsimd.wait_ge(s_sp[0].h, s_sp[0].n)
                    gpsimd.wait_ge(s_sp[1].h, s_sp[1].n)
                    gpsimd.wait_ge(s_st.h, s_st.n)
                    agk_sem[l].inc(gpsimd.collective_compute(
                        "AllGather", ALU.bypass, replica_groups=[list(range(C))],
                        ins=[ki.ap().opt()], outs=[ko.ap().opt()]))
                    agv_sem[l].inc(gpsimd.collective_compute(
                        "AllGather", ALU.bypass, replica_groups=[list(range(C))],
                        ins=[vi.ap().opt()], outs=[vo.ap().opt()]))
                block.gpsimd(qkv_st)

        def ot_col(o):
            return (o % 4) * P

        def ph_attn(l):
            ki, ko, vi, vo = kvt[l]
            with nc.sbuf_tensor(U("QT"), [P, 2 * L], BF16) as QT, \
                 nc.sbuf_tensor(U("aoT"), [P, 16 * L], BF16) as aoT:
              with nc.sbuf_tensor(U("KTb"), [P, 2 * NKC * P], BF16) as KTb, \
                 nc.sbuf_tensor(U("Vb"), [P, 2 * NKC * P], BF16) as Vb, \
                 nc.sbuf_tensor(U("exb"), [P, 2 * L], BF16) as exb, \
                 nc.sbuf_tensor(U("dsc"), [P, 2 * P], BF16) as dsc, \
                 nc.sbuf_tensor(U("zsb"), [1, 2 * 512], F32) as zsb, \
                 nc.sbuf_tensor(U("rzb"), [1, 2 * 512], F32) as rzb, \
                 nc.sbuf_tensor(U("bbs"), [P, 2 * 512], F32) as bbs, \
                 nc.psum_tensor(U("Sps0"), [P, L], F32) as Sps0, \
                 nc.psum_tensor(U("Sps1"), [P, L], F32) as Sps1, \
                 nc.psum_tensor(U("Ops"), [P, L], F32) as Ops, \
                 nc.psum_tensor(U("Zps0"), [1, 512], F32) as Zps0, \
                 nc.psum_tensor(U("Zps1"), [1, 512], F32) as Zps1:
                Sps = [Sps0, Sps1]
                Zps = [Zps0, Zps1]
                kv0 = s_kv.n
                s_kp = [sem(U("kpa")), sem(U("kpb"))]
                kp0 = [s_kp[0].n, s_kp[1].n]

                heads_plan = []
                tt, aa, vv = s_t.n, s_a.n, s_v.n
                for h in range(H):
                    plan = {"qk": [0] * NKC, "zav": [0] * NKC,
                            "exp": [], "mask": []}
                    for kc in range(NKC):
                        tt += 1
                        plan["qk"][kc] = tt
                        if kc >= 1:
                            tt += 1
                            plan["zav"][kc - 1] = tt
                    tt += 1
                    plan["zav"][NKC - 1] = tt
                    for kc in range(NKC):
                        aa += 1
                        plan["exp"].append(aa)
                        vv += 1
                        plan["mask"].append(vv)
                    tt += NS
                    plan["bmm"] = tt
                    vv += 2 * NS
                    plan["vdone"] = vv
                    heads_plan.append(plan)

                def kvload(sync):
                    sync.wait_ge(agk_sem[l].h, agk_sem[l].n)
                    sync.wait_ge(agv_sem[l].h, agv_sem[l].n)
                    for h in range(H):
                        hb = (h % 2) * NKC * P
                        if h >= 2:
                            sync.wait_ge(s_t.h, heads_plan[h - 2]["zav"][-1])
                        s_kp[h % 2].inc(sync.dma_start(
                            out=QT[:, (h % 2) * L:(h % 2 + 1) * L],
                            in_=q_stash[:, h * L:(h + 1) * L]), 16)
                        for r in range(C):
                            s_kp[h % 2].inc(sync.dma_start(
                                out=KTb[:, hb + r * J * P:hb + (r + 1) * J * P],
                                in_=ko.ap()[r, h * P:(h + 1) * P, :]), 16)
                            s_kp[h % 2].inc(sync.dma_start(
                                out=Vb[:, hb + r * J * P:hb + (r + 1) * J * P],
                                in_=vo.ap()[r, h, :, :]), 16)
                block.sync(kvload)

                def attn_t(tensor):
                    def emit_qk(plan, h, kc):
                        jmin = kc // 8
                        qlo = jmin * P
                        slot = kc % 2
                        if kc >= 2:
                            tensor.wait_ge(s_a.h, plan["exp"][kc - 2])
                        kb = (h % 2) * NKC * P + ((kc % 8) * J + kc // 8) * P
                        qb = (h % 2) * L
                        for (pa, pb) in spans(qlo, L):
                            mm = tensor.matmul(
                                Sps[slot][:, pa:pb],
                                KTb[:, kb:kb + P],
                                QT[:, qb + pa:qb + pb],
                                start=True, stop=True)
                        s_t.inc(mm)

                    def emit_zav(plan, h, kc):
                        jmin = kc // 8
                        qlo = jmin * P
                        slot = kc % 2
                        tensor.wait_ge(s_v.h, plan["mask"][kc])
                        kb = (h % 2) * NKC * P + ((kc % 8) * J + kc // 8) * P
                        for (pa, pb) in spans(qlo, L):
                            half = pa // 512
                            kc_max = 8 * min(J, (half + 1) * 4)
                            tensor.matmul(
                                Zps[half][:, pa - half * 512:pb - half * 512],
                                onec_s[:, :],
                                exb[:, slot * L + pa:slot * L + pb],
                                start=(kc == 0), stop=(kc == kc_max - 1))
                            mm = tensor.matmul(
                                Ops[:, pa:pb],
                                Vb[:, kb:kb + P],
                                exb[:, slot * L + pa:slot * L + pb],
                                start=(kc == 0), stop=(kc == kc_max - 1))
                        s_t.inc(mm)

                    for h in range(H):
                        plan = heads_plan[h]
                        tensor.wait_ge(s_kp[h % 2].h,
                                       kp0[h % 2] + 272 * (h // 2 + 1))
                        if h >= 1:
                            tensor.wait_ge(s_v.h, heads_plan[h - 1]["vdone"])
                        for kc in range(NKC):
                            emit_qk(plan, h, kc)
                            if kc >= 1:
                                emit_zav(plan, h, kc - 1)
                        emit_zav(plan, h, NKC - 1)
                        tensor.wait_ge(s_a.h, plan["exp"][NKC - 1])
                        tensor.wait_ge(s_v.h, plan["vdone"] - NS)
                        for half in range(NS):
                            s_t.inc(tensor.matmul(
                                Sps[half][:, 0:SW], onerf_s[:, :],
                                rzb[0:1, half * 512:half * 512 + SW],
                                start=True, stop=True))
                block.tensor(attn_t)

                def attn_a(scalar):
                    for h in range(H):
                        plan = heads_plan[h]
                        for kc in range(NKC):
                            jmin = kc // 8
                            qlo = jmin * P
                            slot = kc % 2
                            scalar.wait_ge(s_t.h, plan["qk"][kc])
                            if kc >= 2:
                                scalar.wait_ge(s_t.h, plan["zav"][kc - 2])
                            ai = scalar.activation(
                                dsc[:, slot * P:(slot + 1) * P],
                                Sps[slot][:, qlo:qlo + P], AF.Exp,
                                scale=1.0 / math.sqrt(DH))
                            if qlo + P < L:
                                ai = scalar.activation(
                                    exb[:, slot * L + qlo + P:slot * L + L],
                                    Sps[slot][:, qlo + P:L], AF.Exp,
                                    scale=1.0 / math.sqrt(DH))
                            s_a.inc(ai)
                block.scalar(attn_a)

                def attn_v(vector):
                    for h in range(H):
                        plan = heads_plan[h]
                        for kc in range(NKC):
                            jmin = kc // 8
                            qlo = jmin * P
                            slot = kc % 2
                            m = kc % 8
                            vector.wait_ge(s_a.h, plan["exp"][kc])
                            s_v.inc(vector.tensor_tensor(
                                exb[:, slot * L + qlo:slot * L + qlo + P],
                                dsc[:, slot * P:(slot + 1) * P],
                                mask_s[:, m * P:(m + 1) * P], ALU.mult))
                        for half in range(NS):
                            kc_max = 8 * min(J, (half + 1) * 4)
                            vector.wait_ge(s_t.h, plan["zav"][kc_max - 1])
                            vector.tensor_copy(zsb[0:1, half * 512:half * 512 + SW],
                                               Zps[half][0:1, 0:SW])
                            vector.drain()
                            s_v.inc(vector.reciprocal(
                                rzb[0:1, half * 512:half * 512 + SW],
                                zsb[0:1, half * 512:half * 512 + SW]))
                        for half in range(NS):
                            vector.wait_ge(s_t.h, plan["bmm"] - NS + half + 1)
                            vector.tensor_copy(bbs[:, half * 512:half * 512 + SW],
                                               Sps[half][:, 0:SW])
                            vector.drain()
                            s_v.inc(vector.tensor_tensor(
                                aoT[:, h * L + half * 512:h * L + half * 512 + SW],
                                Ops[:, half * 512:half * 512 + SW],
                                bbs[:, half * 512:half * 512 + SW], ALU.mult))
                block.vector(attn_v)
              do_tap(f"attn{l}", aoT[:, 0:min(16 * L, J * D)],
                     min(16 * L, J * D), [(s_v, s_v.n)])
              if tapret["stop"]:
                  return
              barrier()

              if True:
                woname = "wo0" if l == 0 else "wo1"
                wofull = ag[woname][1]
                t0, v0 = s_t.n, s_v.n
                wo_tgt = [s_wp[ob % 2].n + 256 * (ob // 2 + 1)
                          for ob in range(D // 512)]
                with nc.sbuf_tensor(U("wbufO"), [P, 16 * 512], BF16) as wbufO, \
                     nc.sbuf_tensor(U("wbufO2"), [P, 16 * 512], BF16) as wbufO2, \
                     nc.sbuf_tensor(U("tmpO"), [P, 2 * 512], F32) as tmpO, \
                     nc.psum_tensor(U("psO"), [P, 2 * 512], F32) as psO:
                    wobufs = [wbufO, wbufO2]
                    NOB2 = D // 512

                    def wo_w(scalar):
                        scalar.wait_ge(ag_sem[woname].h, 1)
                        scalar.wait_ge(bar, barn[0])
                        for ob in range(NOB2):
                            if ob >= 2:
                                scalar.wait_ge(s_t.h, t0 + (ob - 1) * J)
                            for dk in range(16):
                                s_wp[ob % 2].inc(scalar.dma_start(
                                    out=wobufs[ob % 2][:, dk * 512:(dk + 1) * 512],
                                    in_=wofull[ob * 512:(ob + 1) * 512,
                                               dk * P:(dk + 1) * P],
                                    transpose=True), 16)
                    block.scalar(wo_w)

                    def wo_t(tensor):
                        i = 0
                        for ob in range(NOB2):
                            tensor.wait_ge(s_wp[ob % 2].h, wo_tgt[ob])
                            for st in range(J):
                                pp = i % 2
                                if i >= 2:
                                    tensor.wait_ge(s_v.h, v0 + i - 1)
                                for dk in range(16):
                                    mm = tensor.matmul(
                                        psO[:, pp * 512:(pp + 1) * 512],
                                        aoT[:, dk * L + st * P:dk * L + (st + 1) * P],
                                        wobufs[ob % 2][:, dk * 512:(dk + 1) * 512],
                                        start=(dk == 0), stop=(dk == 15))
                                s_t.inc(mm)
                                i += 1
                    block.tensor(wo_t)

                    def wo_v(vector):
                        vector.drain()
                        i = 0
                        for ob in range(NOB2):
                            for st in range(J):
                                pp = i % 2
                                vector.wait_ge(s_t.h, t0 + i + 1)
                                xsl = x_res[:, st * D + ob * 512:
                                            st * D + (ob + 1) * 512]
                                tsl = tmpO[:, pp * 512:(pp + 1) * 512]
                                vector.tensor_tensor(
                                    tsl, psO[:, pp * 512:(pp + 1) * 512],
                                    xsl, ALU.add)
                                vector.drain()
                                s_v.inc(vector.tensor_copy(xsl, tsl))
                                i += 1
                    block.vector(wo_v)

        def ph_ffn0():
            NH = HID // 512
            NHK = HID // P
            HF = min(2, J)
            nhalves = (J + HF - 1) // HF
            if True:
                s_xn = sem("ffn0xn")
                with nc.sbuf_tensor(U("xqT"), [P, 16 * L], BF16) as xqT:
                    with nc.sbuf_tensor(U("xnt_in"), [P, D], BF16) as xnt_in, \
                         nc.sbuf_tensor(U("xqt"), [P, D], BF16) as xqt, \
                         nc.sbuf_tensor(U("qsf"), [P, 512], F32) as qsf, \
                         nc.sbuf_tensor(U("qsi"), [P, 512], I32) as qsi, \
                         nc.psum_tensor(U("psTqa"), [P, P], BF16) as psTqa, \
                         nc.psum_tensor(U("psTqb"), [P, P], BF16) as psTqb:
                        psTq = [psTqa, psTqb]
                        tt, vv = s_t.n, s_v.n
                        q_plan = []
                        for t in range(J):
                            vv += 1
                            d = {"vq": vv, "t_vt": [], "v_vt": []}
                            for dk in range(16):
                                tt += 1
                                d["t_vt"].append(tt)
                                vv += 1
                                d["v_vt"].append(vv)
                            q_plan.append(d)

                        def xload(sync):
                            sync.wait_ge(bar, barn[0])
                            for t in range(J):
                                if t >= 1:
                                    sync.wait_ge(s_v.h, q_plan[t - 1]["vq"])
                                s_xn.inc(sync.dma_start(
                                    out=xnt_in[:, :],
                                    in_=xn_stash[t * P:(t + 1) * P, :]), 16)
                        block.sync(xload)

                        def q_v(vector):
                            for t in range(J):
                                vector.wait_ge(s_xn.h, 16 * (t + 1))
                                vector.tensor_reduce(
                                    sml[:, 48:49], xnt_in[:, :], AX.X, ALU.max,
                                    apply_absolute_value=True)
                                vector.drain()
                                vector.tensor_scalar(sml[:, 49:50], sml[:, 48:49],
                                                     1e-5, None, ALU.max)
                                vector.drain()
                                vector.reciprocal(sml[:, 50:51], sml[:, 49:50])
                                vector.drain()
                                vector.tensor_scalar(sml[:, 51:52], sml[:, 50:51],
                                                     127.0, None, ALU.mult)
                                vector.tensor_scalar(sml[:, 24 + t:25 + t],
                                                     sml[:, 49:50],
                                                     1.0 / 127.0, None, ALU.mult)
                                vector.drain()
                                vector.tensor_scalar(sml[:, 32 + t:33 + t],
                                                     sml[:, 24 + t:25 + t],
                                                     sml[:, 52:53], None, ALU.mult)
                                vector.tensor_scalar(sml[:, 40 + t:41 + t],
                                                     sml[:, 24 + t:25 + t],
                                                     sml[:, 53:54], None, ALU.mult)
                                vector.drain()
                                li = None
                                for ch in range(4):
                                    sl = slice(ch * 512, (ch + 1) * 512)
                                    vector.tensor_scalar(qsf[:, :], xnt_in[:, sl],
                                                         sml[:, 51:52], None,
                                                         ALU.mult)
                                    vector.drain()
                                    vector.tensor_copy(qsi[:, :], qsf[:, :])
                                    vector.drain()
                                    li = vector.tensor_scalar(xqt[:, sl], qsi[:, :],
                                                              127, -128,
                                                              ALU.min, ALU.max)
                                s_v.inc(li)
                                for dk in range(16):
                                    vector.wait_ge(s_t.h, q_plan[t]["t_vt"][dk])
                                    s_v.inc(vector.tensor_copy(
                                        xqT[:, dk * L + t * P:dk * L + (t + 1) * P],
                                        psTq[dk % 2][:, :]))
                        block.vector(q_v)

                        def q_t(tensor):
                            for t in range(J):
                                tensor.wait_ge(s_v.h, q_plan[t]["vq"])
                                for dk in range(16):
                                    if dk >= 2:
                                        tensor.wait_ge(s_v.h,
                                                       q_plan[t]["v_vt"][dk - 2])
                                    s_t.inc(tensor.transpose(
                                        psTq[dk % 2][:, :],
                                        xqt[:, dk * P:(dk + 1) * P],
                                        id_s[:, :]))
                        block.tensor(q_t)
                    xqT_ready_v = s_v.n

                    do_tap("xq0", xqT[:, 0:min(16 * L, J * D)],
                           min(16 * L, J * D), [(s_v, xqT_ready_v)])
                    if tapret["stop"]:
                        return

                    with nc.sbuf_tensor(U("wb1"), [P, 2 * 16 * 512], BF16) as wb1, \
                         nc.sbuf_tensor(U("wb3"), [P, 16 * 512], BF16) as wb3, \
                         nc.sbuf_tensor(U("hbuf"), [P, HF * HID], BF16) as hbuf, \
                         nc.sbuf_tensor(U("tm1"), [P, 512], F32) as tm1, \
                         nc.sbuf_tensor(U("hqf"), [P, 512], F32) as hqf, \
                         nc.sbuf_tensor(U("hqi"), [P, 512], I32) as hqi, \
                         nc.sbuf_tensor(U("hq"), [P, HID], BF16) as hq, \
                         nc.sbuf_tensor(U("hstg"), [P, NHK * P], BF16) as hstg, \
                         nc.psum_tensor(U("psB1"), [P, 512], F32) as psB1, \
                         nc.psum_tensor(U("psB3"), [P, 512], F32) as psB3, \
                         nc.psum_tensor(U("psHTa"), [P, P], BF16) as psHTa, \
                         nc.psum_tensor(U("psHTb"), [P, P], BF16) as psHTb:
                        psHT = [psHTa, psHTb]
                        w0, st0 = s_w.n, s_st.n
                        s_b3 = sem("fb3")
                        spb = [s_wp[0].n, s_wp[1].n]
                        tt, aa, vv = s_t.n, s_a.n, s_v.n
                        plan_mm = {}
                        plan_q = {}
                        for hf in range(nhalves):
                            sts = list(range(hf * HF, min(J, (hf + 1) * HF)))
                            for hg in range(NH):
                                for si, st in enumerate(sts):
                                    tt += 1
                                    aa += 1
                                    vv += 1
                                    plan_mm[(hf, hg, si)] = {
                                        "t": tt, "a": aa, "v": vv}
                            for si, st in enumerate(sts):
                                vv += 1
                                q = {"vq": vv, "t_vt": [], "v_vt": []}
                                for hk in range(NHK):
                                    tt += 1
                                    q["t_vt"].append(tt)
                                    vv += 1
                                    q["v_vt"].append(vv)
                                plan_q[(hf, si)] = q

                        groups_f = [(hf, hg) for hf in range(nhalves)
                                    for hg in range(NH)]
                        b3_0 = s_b3.n

                        def f_w(scalar):
                            scalar.wait_ge(ag_sem["w1"].h, 1)
                            scalar.wait_ge(ag_sem["w3"].h, 1)
                            scalar.wait_ge(s_v.h, xqT_ready_v)
                            for g, (hf, hg) in enumerate(groups_f):
                                pg = (g % 2) * 16 * 512
                                if g >= 2:
                                    phf, phg = groups_f[g - 2]
                                    nsi = len(list(range(
                                        phf * HF, min(J, (phf + 1) * HF))))
                                    scalar.wait_ge(
                                        s_t.h,
                                        plan_mm[(phf, phg, nsi - 1)]["t"])
                                for dk in range(16):
                                    s_wp[g % 2].inc(scalar.dma_start(
                                        out=wb1[:, pg + dk * 512:
                                                pg + (dk + 1) * 512],
                                        in_=ag["w1"][1][hg * 512:(hg + 1) * 512,
                                                        dk * P:(dk + 1) * P],
                                        transpose=True), 16)
                                if g >= 1:
                                    phf, phg = groups_f[g - 1]
                                    nsi = len(list(range(
                                        phf * HF, min(J, (phf + 1) * HF))))
                                    scalar.wait_ge(
                                        s_t.h,
                                        plan_mm[(phf, phg, nsi - 1)]["t"])
                                for dk in range(16):
                                    s_b3.inc(scalar.dma_start(
                                        out=wb3[:, dk * 512:(dk + 1) * 512],
                                        in_=ag["w3"][1][hg * 512:(hg + 1) * 512,
                                                        dk * P:(dk + 1) * P],
                                        transpose=True), 16)
                        block.sync(f_w)

                        def f_t(tensor):
                            i = 0
                            for hf in range(nhalves):
                                sts = list(range(hf * HF, min(J, (hf + 1) * HF)))
                                for hg in range(NH):
                                    pg = (i % 2) * 16 * 512
                                    tensor.wait_ge(
                                        s_wp[i % 2].h,
                                        spb[i % 2] + 256 * (i // 2 + 1))
                                    i += 1
                                    tensor.wait_ge(s_b3.h, b3_0 + 256 * i)
                                    for si, st in enumerate(sts):
                                        key = (hf, hg, si)
                                        if not (hf == 0 and hg == 0 and si == 0):
                                            tensor.wait_ge(
                                                s_v.h, plan_mm[key]["v"] - 1)
                                        for dk in range(16):
                                            tensor.matmul(
                                                psB1[:, :],
                                                xqT[:, dk * L + st * P:
                                                    dk * L + (st + 1) * P],
                                                wb1[:, pg + dk * 512:
                                                    pg + (dk + 1) * 512],
                                                start=(dk == 0), stop=(dk == 15))
                                        for dk in range(16):
                                            mm = tensor.matmul(
                                                psB3[:, :],
                                                xqT[:, dk * L + st * P:
                                                    dk * L + (st + 1) * P],
                                                wb3[:, dk * 512:(dk + 1) * 512],
                                                start=(dk == 0), stop=(dk == 15))
                                        s_t.inc(mm)
                                for si, st in enumerate(sts):
                                    q = plan_q[(hf, si)]
                                    tensor.wait_ge(s_v.h, q["vq"])
                                    for hk in range(NHK):
                                        if hk >= 2:
                                            tensor.wait_ge(s_v.h,
                                                           q["v_vt"][hk - 2])
                                        s_t.inc(tensor.transpose(
                                            psHT[hk % 2][:, :],
                                            hq[:, hk * P:(hk + 1) * P],
                                            id_s[:, :]))
                        block.tensor(f_t)

                        def f_a(scalar):
                            for hf in range(nhalves):
                                sts = list(range(hf * HF, min(J, (hf + 1) * HF)))
                                for hg in range(NH):
                                    for si, st in enumerate(sts):
                                        key = (hf, hg, si)
                                        scalar.wait_ge(s_t.h, plan_mm[key]["t"])
                                        ai = scalar.activation(
                                            tm1[:, :], psB1[:, :], AF.Silu,
                                            scale=sml[:, 32 + st:33 + st])
                                        s_a.inc(ai)
                        block.scalar(f_a)

                        def f_v(vector):
                            for hf in range(nhalves):
                                sts = list(range(hf * HF, min(J, (hf + 1) * HF)))
                                for hg in range(NH):
                                    for si, st in enumerate(sts):
                                        key = (hf, hg, si)
                                        vector.wait_ge(s_a.h, plan_mm[key]["a"])
                                        s_v.inc(vector.scalar_tensor_tensor(
                                            hbuf[:, si * HID + hg * 512:
                                                 si * HID + (hg + 1) * 512],
                                            psB3[:, :],
                                            sml[:, 40 + st:41 + st],
                                            tm1[:, :], ALU.mult, ALU.mult))
                                for si, st in enumerate(sts):
                                    q = plan_q[(hf, si)]
                                    if not (hf == 0 and si == 0):
                                        nprev = hf * HF + si
                                        vector.wait_ge(s_st.h,
                                                       st0 + 16 * nprev)
                                    hrow = hbuf[:, si * HID:(si + 1) * HID]
                                    vector.drain()
                                    vector.tensor_reduce(
                                        sml[:, 48:49], hrow, AX.X, ALU.max,
                                        apply_absolute_value=True)
                                    vector.drain()
                                    vector.tensor_scalar(sml[:, 49:50],
                                                         sml[:, 48:49],
                                                         1e-5, None, ALU.max)
                                    vector.drain()
                                    vector.reciprocal(sml[:, 50:51],
                                                      sml[:, 49:50])
                                    vector.drain()
                                    vector.tensor_scalar(sml[:, 51:52],
                                                         sml[:, 50:51],
                                                         127.0, None, ALU.mult)
                                    vector.tensor_scalar(sml[:, 55:56],
                                                         sml[:, 49:50],
                                                         1.0 / 127.0, None,
                                                         ALU.mult)
                                    vector.drain()
                                    vector.tensor_scalar(sml[:, 56 + st:57 + st],
                                                         sml[:, 55:56],
                                                         sml[:, 54:55], None,
                                                         ALU.mult)
                                    li = None
                                    for hg2 in range(NH):
                                        vector.tensor_scalar(
                                            hqf[:, :],
                                            hrow[:, hg2 * 512:(hg2 + 1) * 512],
                                            sml[:, 51:52], None, ALU.mult)
                                        vector.drain()
                                        vector.tensor_copy(hqi[:, :], hqf[:, :])
                                        vector.drain()
                                        li = vector.tensor_scalar(
                                            hq[:, hg2 * 512:(hg2 + 1) * 512],
                                            hqi[:, :], 127, -128,
                                            ALU.min, ALU.max)
                                    s_v.inc(li)
                                    for hk in range(NHK):
                                        vector.wait_ge(s_t.h, q["t_vt"][hk])
                                        s_v.inc(vector.tensor_copy(
                                            hstg[:, hk * P:(hk + 1) * P],
                                            psHT[hk % 2][:, :]))
                        block.vector(f_v)

                        def f_st(gpsimd):
                            for hf in range(nhalves):
                                sts = list(range(hf * HF, min(J, (hf + 1) * HF)))
                                for si, st in enumerate(sts):
                                    q = plan_q[(hf, si)]
                                    gpsimd.wait_ge(s_v.h, q["v_vt"][-1])
                                    s_st.inc(gpsimd.dma_start(
                                        out=hT_stash[:, st * NHK * P:
                                                     (st + 1) * NHK * P],
                                        in_=hstg[:, :]), 16)
                        block.gpsimd(f_st)
                        hstores = s_st.n

                with nc.sbuf_tensor(U("w2buf"), [P, 2 * NHK * 512], BF16) as w2buf, \
                     nc.sbuf_tensor(U("hq1"), [P, 2 * NHK * P], BF16) as hq1, \
                     nc.sbuf_tensor(U("tmp2b"), [P, 2 * 512], F32) as tmp2b, \
                     nc.psum_tensor(U("psW2"), [P, 2 * 512], F32) as psW2:
                    w0, t0, v0 = s_w.n, s_t.n, s_v.n
                    s_w2p = [sem("w2pa"), sem("w2pb")]
                    w2p0 = [s_w2p[0].n, s_w2p[1].n]
                    h2_tgt = [s_hp[i % 2].n + 16 * (i // 2 + 1)
                              for i in range(4 * J)]

                    def w2_w(scalar):
                        scalar.wait_ge(ag_sem["w2"].h, 1)
                        scalar.wait_ge(s_st.h, hstores)
                        for og in range(4):
                            po = (og % 2) * NHK * 512
                            if og >= 2:
                                scalar.wait_ge(s_t.h, t0 + (og - 1) * J)
                            for hk in range(NHK):
                                s_w2p[og % 2].inc(scalar.dma_start(
                                    out=w2buf[:, po + hk * 512:
                                              po + (hk + 1) * 512],
                                    in_=ag["w2"][1][og * 512:(og + 1) * 512,
                                                    hk * P:(hk + 1) * P],
                                    transpose=True), 16)
                    block.scalar(w2_w)

                    def w2_h(sync):
                        sync.wait_ge(s_st.h, hstores)
                        i = 0
                        for og in range(4):
                            for st in range(J):
                                if i >= 2:
                                    sync.wait_ge(s_t.h, t0 + i - 1)
                                s_hp[i % 2].inc(sync.dma_start(
                                    out=hq1[:, (i % 2) * NHK * P:
                                            (i % 2 + 1) * NHK * P],
                                    in_=hT_stash[:, st * NHK * P:
                                                 (st + 1) * NHK * P]), 16)
                                i += 1
                    block.sync(w2_h)

                    def w2_t(tensor):
                        i = 0
                        for og in range(4):
                            po = (og % 2) * NHK * 512
                            tensor.wait_ge(s_w2p[og % 2].h,
                                           w2p0[og % 2] + 16 * NHK * (og // 2 + 1))
                            for st in range(J):
                                pp = i % 2
                                tensor.wait_ge(s_hp[i % 2].h, h2_tgt[i])
                                if i >= 2:
                                    tensor.wait_ge(s_v.h, v0 + i - 1)
                                for hk in range(NHK):
                                    mm = tensor.matmul(
                                        psW2[:, pp * 512:(pp + 1) * 512],
                                        hq1[:, pp * NHK * P + hk * P:
                                            pp * NHK * P + (hk + 1) * P],
                                        w2buf[:, po + hk * 512:
                                              po + (hk + 1) * 512],
                                        start=(hk == 0), stop=(hk == NHK - 1))
                                s_t.inc(mm)
                                i += 1
                    block.tensor(w2_t)

                    def w2_v(vector):
                        vector.drain()
                        i = 0
                        for og in range(4):
                            for st in range(J):
                                pp = i % 2
                                vector.wait_ge(s_t.h, t0 + i + 1)
                                xsl = x_res[:, st * D + og * 512:
                                            st * D + (og + 1) * 512]
                                tsl = tmp2b[:, pp * 512:(pp + 1) * 512]
                                vector.scalar_tensor_tensor(
                                    tsl, psW2[:, pp * 512:(pp + 1) * 512],
                                    sml[:, 56 + st:57 + st], xsl,
                                    ALU.mult, ALU.add)
                                vector.drain()
                                s_v.inc(vector.tensor_copy(xsl, tsl))
                                i += 1
                    block.vector(w2_v)

        def ph_ffn1():
            NH = HID // 512
            NHK = HID // P
            HF = min(4, J)
            nhalves = (J + HF - 1) // HF
            s_x1 = sem("ffn1xn")
            with nc.sbuf_tensor(U("xnT1"), [P, 16 * L], BF16) as xnT1:
                def x1load(sync):
                    sync.wait_ge(bar, barn[0])
                    s_x1.inc(sync.dma_start(
                        out=xnT1[:, :], in_=xnT_stash[:, :]), 16)
                block.sync(x1load)

                with nc.sbuf_tensor(U("wbg"), [P, 2 * 16 * 512], BF16) as wbg, \
                     nc.sbuf_tensor(U("wbu"), [P, 16 * 512], BF16) as wbu, \
                     nc.sbuf_tensor(U("h1buf"), [P, HF * HID], BF16) as h1buf, \
                     nc.sbuf_tensor(U("tg1"), [P, 512], F32) as tg1, \
                     nc.sbuf_tensor(U("h1stg"), [P, (NHK // 2) * P], BF16) as h1stg, \
                     nc.psum_tensor(U("psG"), [P, 512], F32) as psG, \
                     nc.psum_tensor(U("psU"), [P, 512], F32) as psU, \
                     nc.psum_tensor(U("psHT1a"), [P, P], BF16) as psHT1a, \
                     nc.psum_tensor(U("psHT1b"), [P, P], BF16) as psHT1b:
                    psHT1 = [psHT1a, psHT1b]
                    w0, st0 = s_w.n, s_st.n
                    s_u1 = sem("gup")
                    spb = [s_wp[0].n, s_wp[1].n]
                    tt, aa, vv = s_t.n, s_a.n, s_v.n
                    plan_mm = {}
                    plan_q = {}
                    for hf in range(nhalves):
                        sts = list(range(hf * HF, min(J, (hf + 1) * HF)))
                        for hg in range(NH):
                            for si, st in enumerate(sts):
                                tt += 1
                                aa += 1
                                vv += 1
                                plan_mm[(hf, hg, si)] = {"t": tt, "a": aa, "v": vv}
                        for si, st in enumerate(sts):
                            q = {"t_vt": [], "v_vt": []}
                            for hk in range(NHK):
                                tt += 1
                                q["t_vt"].append(tt)
                                vv += 1
                                q["v_vt"].append(vv)
                            plan_q[(hf, si)] = q

                    groups_g = [(hf, hg) for hf in range(nhalves)
                                for hg in range(NH)]
                    u1_0 = s_u1.n

                    def g_w(scalar):
                        scalar.wait_ge(ag_sem["wgu"].h, 1)
                        scalar.wait_ge(s_x1.h, s_x1.n)
                        for g, (hf, hg) in enumerate(groups_g):
                            pg = (g % 2) * 16 * 512
                            if g >= 2:
                                phf, phg = groups_g[g - 2]
                                nsi = len(list(range(
                                    phf * HF, min(J, (phf + 1) * HF))))
                                scalar.wait_ge(
                                    s_t.h,
                                    plan_mm[(phf, phg, nsi - 1)]["t"])
                            for dk in range(16):
                                s_wp[g % 2].inc(scalar.dma_start(
                                    out=wbg[:, pg + dk * 512:pg + (dk + 1) * 512],
                                    in_=ag["wgu"][1][hg * 512:(hg + 1) * 512,
                                                     dk * P:(dk + 1) * P],
                                    transpose=True), 16)
                            if g >= 1:
                                phf, phg = groups_g[g - 1]
                                nsi = len(list(range(
                                    phf * HF, min(J, (phf + 1) * HF))))
                                scalar.wait_ge(
                                    s_t.h,
                                    plan_mm[(phf, phg, nsi - 1)]["t"])
                            for dk in range(16):
                                s_u1.inc(scalar.dma_start(
                                    out=wbu[:, dk * 512:(dk + 1) * 512],
                                    in_=ag["wgu"][1][HID + hg * 512:
                                                     HID + (hg + 1) * 512,
                                                     dk * P:(dk + 1) * P],
                                    transpose=True), 16)
                    block.sync(g_w)

                    def g_t(tensor):
                        tensor.wait_ge(s_x1.h, s_x1.n)
                        i = 0
                        for hf in range(nhalves):
                            sts = list(range(hf * HF, min(J, (hf + 1) * HF)))
                            for hg in range(NH):
                                pg = (i % 2) * 16 * 512
                                tensor.wait_ge(
                                    s_wp[i % 2].h,
                                    spb[i % 2] + 256 * (i // 2 + 1))
                                i += 1
                                tensor.wait_ge(s_u1.h, u1_0 + 256 * i)
                                for si, st in enumerate(sts):
                                    key = (hf, hg, si)
                                    if not (hf == 0 and hg == 0 and si == 0):
                                        tensor.wait_ge(s_v.h,
                                                       plan_mm[key]["v"] - 1)
                                    for dk in range(16):
                                        tensor.matmul(
                                            psG[:, :],
                                            xnT1[:, dk * L + st * P:
                                                 dk * L + (st + 1) * P],
                                            wbg[:, pg + dk * 512:
                                                pg + (dk + 1) * 512],
                                            start=(dk == 0), stop=(dk == 15))
                                    for dk in range(16):
                                        mm = tensor.matmul(
                                            psU[:, :],
                                            xnT1[:, dk * L + st * P:
                                                 dk * L + (st + 1) * P],
                                            wbu[:, dk * 512:(dk + 1) * 512],
                                            start=(dk == 0), stop=(dk == 15))
                                    s_t.inc(mm)
                            for si, st in enumerate(sts):
                                q = plan_q[(hf, si)]
                                tensor.wait_ge(s_v.h,
                                               plan_mm[(hf, NH - 1, si)]["v"])
                                for hk in range(NHK):
                                    if hk >= 2:
                                        tensor.wait_ge(s_v.h, q["v_vt"][hk - 2])
                                    elif si > 0:
                                        tensor.wait_ge(
                                            s_v.h,
                                            plan_q[(hf, si - 1)]["v_vt"][
                                                NHK - 2 + hk])
                                    s_t.inc(tensor.transpose(
                                        psHT1[hk % 2][:, :],
                                        h1buf[:, si * HID + hk * P:
                                              si * HID + (hk + 1) * P],
                                        id_s[:, :]))
                    block.tensor(g_t)

                    def g_a(scalar):
                        for hf in range(nhalves):
                            sts = list(range(hf * HF, min(J, (hf + 1) * HF)))
                            for hg in range(NH):
                                for si, st in enumerate(sts):
                                    key = (hf, hg, si)
                                    scalar.wait_ge(s_t.h, plan_mm[key]["t"])
                                    ai = scalar.activation(tg1[:, :], psG[:, :],
                                                           AF.Silu)
                                    s_a.inc(ai)
                    block.scalar(g_a)

                    def g_v(vector):
                        for hf in range(nhalves):
                            sts = list(range(hf * HF, min(J, (hf + 1) * HF)))
                            for hg in range(NH):
                                for si, st in enumerate(sts):
                                    key = (hf, hg, si)
                                    vector.wait_ge(s_a.h, plan_mm[key]["a"])
                                    s_v.inc(vector.scalar_tensor_tensor(
                                        h1buf[:, si * HID + hg * 512:
                                              si * HID + (hg + 1) * 512],
                                        psU[:, :], 1.0,
                                        tg1[:, :], ALU.mult, ALU.mult))
                            for si, st in enumerate(sts):
                                q = plan_q[(hf, si)]
                                nprev = hf * HF + si
                                if nprev > 0:
                                    vector.wait_ge(s_st.h, st0 + 32 * nprev)
                                for hk in range(NHK):
                                    if hk == NHK // 2:
                                        vector.wait_ge(s_st.h,
                                                       st0 + 32 * nprev + 16)
                                    vector.wait_ge(s_t.h, q["t_vt"][hk])
                                    s_v.inc(vector.tensor_copy(
                                        h1stg[:, (hk % (NHK // 2)) * P:
                                              (hk % (NHK // 2) + 1) * P],
                                        psHT1[hk % 2][:, :]))
                    block.vector(g_v)

                    def g_st(gpsimd):
                        HK2 = NHK // 2
                        for hf in range(nhalves):
                            sts = list(range(hf * HF, min(J, (hf + 1) * HF)))
                            for si, st in enumerate(sts):
                                q = plan_q[(hf, si)]
                                for half in range(2):
                                    gpsimd.wait_ge(
                                        s_v.h,
                                        q["v_vt"][(half + 1) * HK2 - 1])
                                    s_st.inc(gpsimd.dma_start(
                                        out=hT_stash[:, st * NHK * P +
                                                     half * HK2 * P:
                                                     st * NHK * P +
                                                     (half + 1) * HK2 * P],
                                        in_=h1stg[:, :]), 16)
                    block.gpsimd(g_st)
                    h1stores = s_st.n

            with nc.sbuf_tensor(U("wdnb"), [P, (HID // P) * 512], BF16) as wdnb, \
                 nc.sbuf_tensor(U("hq2"), [P, 2 * (HID // P) * P], BF16) as hq2, \
                 nc.sbuf_tensor(U("y_all"), [P, J * D], BF16) as y_all, \
                 nc.sbuf_tensor(U("rscr"), [P, D], F32) as rscr, \
                 nc.sbuf_tensor(U("tmp3"), [P, 2 * 512], F32) as tmp3, \
                 nc.psum_tensor(U("psDN"), [P, 2 * 512], F32) as psDN:
                NHK = HID // P
                w0, t0, v0, a0 = s_w.n, s_t.n, s_v.n, s_a.n
                dn_tgt = [s_hp[i % 2].n + 16 * (i // 2 + 1)
                          for i in range(4 * J)]

                def dn_w(scalar):
                    scalar.wait_ge(ag_sem["wdn"].h, 1)
                    scalar.wait_ge(s_st.h, h1stores)
                    for og in range(4):
                        if og >= 1:
                            scalar.wait_ge(s_t.h, t0 + og * J)
                        for hk in range(NHK):
                            s_w.inc(scalar.dma_start(
                                out=wdnb[:, hk * 512:(hk + 1) * 512],
                                in_=ag["wdn"][1][og * 512:(og + 1) * 512,
                                                 hk * P:(hk + 1) * P],
                                transpose=True), 16)
                block.scalar(dn_w)

                def dn_h(sync):
                    sync.wait_ge(s_st.h, h1stores)
                    i = 0
                    for og in range(4):
                        for st in range(J):
                            if i >= 2:
                                sync.wait_ge(s_t.h, t0 + i - 1)
                            s_hp[i % 2].inc(sync.dma_start(
                                out=hq2[:, (i % 2) * NHK * P:
                                        (i % 2 + 1) * NHK * P],
                                in_=hT_stash[:, st * NHK * P:
                                             (st + 1) * NHK * P]), 16)
                            i += 1
                block.sync(dn_h)

                def dn_t(tensor):
                    i = 0
                    for og in range(4):
                        tensor.wait_ge(s_w.h, w0 + 16 * NHK * (og + 1))
                        for st in range(J):
                            pp = i % 2
                            tensor.wait_ge(s_hp[i % 2].h, dn_tgt[i])
                            if i >= 2:
                                tensor.wait_ge(s_v.h, v0 + i - 1)
                            for hk in range(NHK):
                                mm = tensor.matmul(
                                    psDN[:, pp * 512:(pp + 1) * 512],
                                    hq2[:, pp * NHK * P + hk * P:
                                        pp * NHK * P + (hk + 1) * P],
                                    wdnb[:, hk * 512:(hk + 1) * 512],
                                    start=(hk == 0), stop=(hk == NHK - 1))
                            s_t.inc(mm)
                            i += 1
                block.tensor(dn_t)

                def dn_v(vector):
                    i = 0
                    for og in range(4):
                        for st in range(J):
                            pp = i % 2
                            vector.wait_ge(s_t.h, t0 + i + 1)
                            s_v.inc(vector.tensor_copy(
                                y_all[:, st * D + og * 512:st * D + (og + 1) * 512],
                                psDN[:, pp * 512:(pp + 1) * 512]))
                            i += 1
                block.vector(dn_v)
                yall_done_v = s_v.n

                def rms_a(scalar):
                    scalar.wait_ge(s_v.h, yall_done_v)
                    for st in range(J):
                        s_a.inc(scalar.activation(
                            rscr[:, :], y_all[:, st * D:(st + 1) * D], AF.Square,
                            accum_out=sml[:, st:st + 1]))
                        scalar.drain()
                block.scalar(rms_a)

                def rms_v(vector):
                    vector.wait_ge(s_a.h, a0 + J)
                    s_v.inc(vector.tensor_scalar(sml[:, 8:8 + J], sml[:, 0:J],
                                                 1.0 / D, 1e-6, ALU.mult, ALU.add))
                block.vector(rms_v)

                def rms_a2(scalar):
                    scalar.wait_ge(s_v.h, s_v.n)
                    s_a.inc(scalar.sqrt(sml[:, 16:16 + J], sml[:, 8:8 + J]))
                block.scalar(rms_a2)

                def rms_v2(vector):
                    vector.wait_ge(s_a.h, s_a.n)
                    vector.reciprocal(sml[:, 24:24 + J], sml[:, 16:16 + J])
                    vector.drain()
                    ii = 0
                    for st in range(J):
                        for og in range(4):
                            xsl = x_res[:, st * D + og * 512:
                                        st * D + (og + 1) * 512]
                            t3 = tmp3[:, (ii % 2) * 512:(ii % 2 + 1) * 512]
                            vector.scalar_tensor_tensor(
                                t3,
                                y_all[:, st * D + og * 512:st * D + (og + 1) * 512],
                                sml[:, 24 + st:25 + st], xsl, ALU.mult, ALU.add)
                            vector.drain()
                            li = vector.tensor_copy(xsl, t3)
                            ii += 1
                    s_v.inc(li)
                block.vector(rms_v2)

        if tap is None or not tap.startswith("prep"):
            for l in range(depth):
                with nc.sbuf_tensor(f"xn{l}", [P, J * D], BF16) as xn, \
                     nc.sbuf_tensor(f"lnscr{l}", [P, D], F32) as lnscr:
                    xn_ready = ph_ln(l, xn, lnscr)
                    with nc.sbuf_tensor(f"xnT{l}", [P, 16 * L], BF16) as xnT:
                        xnT_ready = ph_transpose(xn, xnT, J, (s_v, xn_ready))
                        ph_qkv(l, xn, xnT, xnT_ready)
                barrier()
                if l == 0:
                    ph_attn(0)
                    if tapret["stop"]:
                        return nc
                    barrier()
                    if tap != "L0":
                        ph_ffn0()
                        if tapret["stop"]:
                            return nc
                        barrier()
                else:
                    ph_ffn1()
                    barrier()
                    ph_attn(1)
                    if tapret["stop"]:
                        return nc
                    barrier()
                if tap == "L0" and l == 0:
                    break
                if tap == "L0a" and l == 0:
                    break

        def ph_out(sync):
            sync.wait_ge(bar, barn[0])
            dst = y_out.ap().rearrange("(t p) d -> p t d", p=P)
            s_o.inc(sync.dma_start(
                out=dst,
                in_=x_res.ap().rearrange("p (t d) -> p t d", d=D)), 16)
            sync.wait_ge(s_o.h, s_o.n)
        block.sync(ph_out)

    return nc



def make_inputs(inputs, J=8):
    L = J * P
    x = np.asarray(inputs["L0_attn_x"])[0]
    in_maps = []
    for c in range(C):
        tiles = [x[(c + 8 * j) * P:(c + 8 * j + 1) * P] for j in range(J)]
        xs = np.ascontiguousarray(np.concatenate(tiles, axis=0))
        pos = np.concatenate([np.arange((c + 8 * j) * P, (c + 8 * j + 1) * P)
                              for j in range(J)])
        sin2, cos2 = _rope_tables(pos)
        m = {
            "x_in": xs.astype(np.float32),
            "sin_in": sin2, "cos_in": cos2,
            "mask_in": _masks(c),
            "wq0_s": np.ascontiguousarray(inputs["L0_attn_w_project"][c * (3 * D // C):(c + 1) * (3 * D // C)]),
            "wo0_s": np.ascontiguousarray(inputs["L0_attn_w_out"][c * (D // C):(c + 1) * (D // C)]),
            "w1_s": np.ascontiguousarray(inputs["L0_ffn_w1"][c * (HID // C):(c + 1) * (HID // C)]),
            "w3_s": np.ascontiguousarray(inputs["L0_ffn_w3"][c * (HID // C):(c + 1) * (HID // C)]),
            "w2_s": np.ascontiguousarray(inputs["L0_ffn_w2"][c * (D // C):(c + 1) * (D // C)]),
            "wq1_s": np.ascontiguousarray(inputs["L1_attn_Wqkv"][c * (3 * D // C):(c + 1) * (3 * D // C)]),
            "wo1_s": np.ascontiguousarray(inputs["L1_attn_WO"][c * (D // C):(c + 1) * (D // C)]),
            "gup_s": np.ascontiguousarray(inputs["L1_ffn_gate_up_packed"][c * (GU // C):(c + 1) * (GU // C)]),
            "gus_s": np.ascontiguousarray(np.asarray(inputs["L1_ffn_gate_up_scales"])[:, c * (GU // C):(c + 1) * (GU // C)]),
            "dnp_s": np.ascontiguousarray(inputs["L1_ffn_down_packed"][c * (D // C):(c + 1) * (D // C)]),
            "dns_s": np.ascontiguousarray(np.asarray(inputs["L1_ffn_down_scales"])[:, c * (D // C):(c + 1) * (D // C)]),
        }
        in_maps.append(m)
    return in_maps


def assemble(results, J=8):
    S = C * J * P
    out = np.empty((1, S, D), dtype=np.float32)
    for c in range(C):
        y = results[c]["y_out"]
        for j in range(J):
            out[0, (c + 8 * j) * P:(c + 8 * j + 1) * P] = y[j * P:(j + 1) * P]
    return out


def kernel(**inputs):
    nc = build(J=8, depth=2)
    in_maps = make_inputs(inputs, J=8)
    res = run_bass_kernel_spmd(nc, in_maps, core_ids=list(range(C)))
    return assemble(res.results, J=8)


# version 10
# speedup vs baseline: 1.6059x; 1.0189x over previous
import math
from contextlib import ExitStack

import numpy as np
import ml_dtypes
import concourse.bass as bass
import concourse.mybir as mybir
from concourse.bass_isa import ReduceOp
from concourse.bass_utils import run_bass_kernel_spmd

F32 = mybir.dt.float32
BF16 = mybir.dt.bfloat16
F16 = mybir.dt.float16
I32 = mybir.dt.int32
AF = mybir.ActivationFunctionType
ALU = mybir.AluOpType
AX = mybir.AxisListType

C = 8
D = 2048
H = 16
DH = 128
HID = 5632
GU = 2 * HID
P = 128
LN2 = math.log(2.0)
C1 = math.log2(3.0) - 1.5


class Sem:
    def __init__(self, nc, name):
        self.h = nc.alloc_semaphore(name)
        self.n = 0

    def inc(self, instr, k=1):
        instr.then_inc(self.h, k)
        self.n += k
        return instr



def _rope_tables(pos):
    i = np.arange(0, DH, 2)
    freqs = np.exp(-math.log(10000.0) * i / DH).astype(np.float32)
    ang = (freqs[:, None] * pos.astype(np.float32)[None, :]).astype(np.float32)
    ang64 = ang.astype(np.float64)
    sin2 = np.repeat(np.sin(ang64), 2, axis=0).astype(np.float32)
    cos2 = np.repeat(np.cos(ang64), 2, axis=0).astype(np.float32)
    return np.ascontiguousarray(sin2), np.ascontiguousarray(cos2)


def _masks(c):
    out = np.zeros((8, P, P), dtype=np.float32)
    kl = np.arange(P)[:, None]
    ql = np.arange(P)[None, :]
    for m in range(8):
        if m < c:
            out[m] = 1.0
        elif m == c:
            out[m] = (kl <= ql).astype(np.float32)
    return out.astype(ml_dtypes.bfloat16)


def _rotmat():
    M = np.zeros((DH, DH), dtype=np.float32)
    for i in range(0, DH, 2):
        M[i, i + 1] = -1.0
        M[i + 1, i] = 1.0
    return np.ascontiguousarray(M.T).astype(ml_dtypes.bfloat16)



def build(J=8, depth=2, tap=None):
    nc = bass.Bass("TRN2", target_bir_lowering=False)
    L = J * P
    NKC = 8 * J

    sems = {}
    _uid = [0]

    def U(name):
        _uid[0] += 1
        return f"{name}_{_uid[0]}"

    def sem(name):
        if name not in sems:
            sems[name] = Sem(nc, name)
        return sems[name]

    x_in = nc.dram_tensor("x_in", [L, D], F32, kind="ExternalInput")
    sin_in = nc.dram_tensor("sin_in", [DH, L], F32, kind="ExternalInput")
    cos_in = nc.dram_tensor("cos_in", [DH, L], F32, kind="ExternalInput")
    mask_in = nc.dram_tensor("mask_in", [8, P, P], BF16, kind="ExternalInput")
    wq0_s = nc.dram_tensor("wq0_s", [3 * D // C, D], F32, kind="ExternalInput")
    wo0_s = nc.dram_tensor("wo0_s", [D // C, D], F32, kind="ExternalInput")
    w1_s = nc.dram_tensor("w1_s", [HID // C, D], F32, kind="ExternalInput")
    w3_s = nc.dram_tensor("w3_s", [HID // C, D], F32, kind="ExternalInput")
    w2_s = nc.dram_tensor("w2_s", [D // C, HID], F32, kind="ExternalInput")
    wq1_s = nc.dram_tensor("wq1_s", [3 * D // C, D], F32, kind="ExternalInput")
    wo1_s = nc.dram_tensor("wo1_s", [D // C, D], F32, kind="ExternalInput")
    gup_s = nc.dram_tensor("gup_s", [GU // C, D // 8], I32, kind="ExternalInput")
    gus_s = nc.dram_tensor("gus_s", [D // P, GU // C], F16, kind="ExternalInput")
    dnp_s = nc.dram_tensor("dnp_s", [D // C, HID // 8], I32, kind="ExternalInput")
    dns_s = nc.dram_tensor("dns_s", [HID // P, D // C], F16, kind="ExternalInput")
    y_out = nc.dram_tensor("y_out", [L, D], F32, kind="ExternalOutput")

    ag = {}
    ag_shapes = (
        ("wq0", 3 * D // C, D), ("wo0", D // C, D), ("w1", HID // C, D),
        ("w3", HID // C, D), ("w2", D // C, HID), ("wq1", 3 * D // C, D),
        ("wo1", D // C, D), ("wgu", GU // C, D), ("wdn", D // C, HID))
    for name, rows, cols in ag_shapes:
        i = nc.dram_tensor(f"agi_{name}", [rows, cols], BF16, kind="Internal")
        o = nc.dram_tensor(f"ago_{name}", [C * rows, cols], BF16, kind="Internal")
        ag[name] = (i, o)

    tsum_i = nc.dram_tensor("tsum_i", [1, 4], F32, kind="Internal")
    tsum_o = nc.dram_tensor("tsum_o", [1, 4], F32, kind="Internal")
    psum_scr = nc.dram_tensor("psum_scr", [P, 4], F32, kind="Internal")
    rep_scr = nc.dram_tensor("rep_scr", [P, 4], F32, kind="Internal")

    kvt = []
    for l in range(depth):
        ki = nc.dram_tensor(f"kvKi{l}", [D, L], BF16, kind="Internal")
        ko = nc.dram_tensor(f"kvKo{l}", [C, D, L], BF16, kind="Internal")
        vi = nc.dram_tensor(f"kvVi{l}", [16, P, L], BF16, kind="Internal")
        vo = nc.dram_tensor(f"kvVo{l}", [C, 16, P, L], BF16, kind="Internal")
        kvt.append((ki, ko, vi, vo))

    q_stash = nc.dram_tensor("q_stash", [P, 16 * L], BF16, kind="Internal")
    xn_stash = nc.dram_tensor("xn_stash", [L, D], BF16, kind="Internal")
    xnT_stash = nc.dram_tensor("xnT_stash", [P, 16 * L], BF16, kind="Internal")
    hT_stash = nc.dram_tensor("hT_stash", [P, (HID // P) * L], BF16, kind="Internal")

    ident_c = nc.inline_tensor(np.eye(P, dtype=ml_dtypes.bfloat16), "ident_c")
    onec_c = nc.inline_tensor(np.ones((P, 1), dtype=ml_dtypes.bfloat16), "onec_c")
    oner_c = nc.inline_tensor(np.ones((1, P), dtype=ml_dtypes.bfloat16), "oner_c")
    onerf_c = nc.inline_tensor(np.ones((1, P), dtype=np.float32), "onerf_c")
    rot_cst = nc.inline_tensor(_rotmat(), "rot_cst")

    with ExitStack() as ctx:
        e = ctx.enter_context
        block = e(nc.Block())

        x_res = e(nc.sbuf_tensor("x_res", [P, J * D], F32))
        id_s = e(nc.sbuf_tensor("id_s", [P, P], BF16))
        rot_s = e(nc.sbuf_tensor("rot_s", [P, P], BF16))
        mask_s = e(nc.sbuf_tensor("mask_s", [P, 8 * P], BF16))
        onec_s = e(nc.sbuf_tensor("onec_s", [P, 1], BF16))
        oner_s = e(nc.sbuf_tensor("oner_s", [1, P], BF16))
        onerf_s = e(nc.sbuf_tensor("onerf_s", [1, P], F32))
        sml = e(nc.sbuf_tensor("sml", [P, 64], F32))

        s_ld = sem("ld")
        s_g = sem("gd")
        s_t = sem("t")
        s_v = sem("v")
        s_a = sem("a")
        s_w = sem("w")
        s_kv = sem("kvd")
        s_st = sem("st")
        s_wp = [sem("wpa"), sem("wpb")]
        s_sp = [sem("spa"), sem("spb")]
        s_hp = [sem("hpa"), sem("hpb")]
        s_o = sem("o")
        bar = nc.alloc_semaphore("bar")
        barn = [0]

        def barrier():
            barn[0] += 5
            tgt = barn[0]
            for name in ("sync", "scalar", "tensor", "vector", "gpsimd"):
                def f(eng, tgt=tgt, name=name):
                    if name in ("scalar", "vector"):
                        eng.drain()
                    eng.sem_inc(bar, 1)
                    eng.wait_ge(bar, tgt)
                getattr(block, name)(f)

        def tap_out(src_ap, ncols, wait_sems=()):
            nt = ncols // D
            def f(gpsimd):
                for ss, vv in wait_sems:
                    gpsimd.wait_ge(ss.h, vv)
                dst = y_out.ap().rearrange("(t p) d -> p t d", p=P)
                s_o.inc(gpsimd.dma_start(
                    out=dst[:, 0:nt, :],
                    in_=src_ap.rearrange("p (t d) -> p t d", d=D)), 16)
                gpsimd.wait_ge(s_o.h, s_o.n)
            block.gpsimd(f)

        s_c0 = sem("ldc")

        def ph0(sync):
            xr = x_in.ap().rearrange("(t p) d -> p t d", p=P)
            s_c0.inc(sync.dma_start(
                out=x_res.ap().rearrange("p (t d) -> p t d", d=D), in_=xr), 16)
            s_c0.inc(sync.dma_start(out=id_s[:, :], in_=ident_c[:, :]), 16)
            s_c0.inc(sync.dma_start(out=rot_s[:, :], in_=rot_cst[:, :]), 16)
            for m in range(8):
                s_c0.inc(sync.dma_start(out=mask_s[:, m * P:(m + 1) * P],
                                        in_=mask_in[m, :, :]), 16)
            s_c0.inc(sync.dma_start(out=onec_s[:, :], in_=onec_c[:, :]), 16)
            s_c0.inc(sync.dma_start(out=oner_s[:, :], in_=oner_c[:, :]), 16)
            s_c0.inc(sync.dma_start(out=onerf_s[:, :], in_=onerf_c[:, :]), 16)
        block.sync(ph0)
        consts_done = s_c0.n

        ag_sem = {name: sem(f"ag_{name}") for name, _, _ in ag_shapes}
        ar_sem = sem("ar_tern")

        def ph1a(gpsimd):
            g0 = s_g.n
            for name, src in (("wq0", wq0_s), ("wo0", wo0_s),
                              ("wq1", wq1_s), ("wo1", wo1_s)):
                s_g.inc(gpsimd.dma_start(out=ag[name][0][:, :], in_=src[:, :]), 16)
            gpsimd.wait_ge(s_g.h, s_g.n)
            for name in ("wq0", "wo0", "wq1", "wo1"):
                ag_sem[name].inc(gpsimd.collective_compute(
                    "AllGather", ALU.bypass, replica_groups=[list(range(C))],
                    ins=[ag[name][0].ap().opt()], outs=[ag[name][1].ap().opt()]))
        block.gpsimd(ph1a)

        with nc.sbuf_tensor("pL", [P, HID], F32) as pL, \
             nc.sbuf_tensor("pI", [P, HID], I32) as pI, \
             nc.sbuf_tensor("pOT", [P, HID], BF16) as pOT, \
             nc.sbuf_tensor("pF", [P, 6 * 2048], F32) as pF, \
             nc.sbuf_tensor("pOb", [P, 2 * 2048], BF16) as pOb, \
             nc.sbuf_tensor("sc16", [64, P], F16) as sc16, \
             nc.sbuf_tensor("sc16b", [64, P], BF16) as sc16b, \
             nc.sbuf_tensor("scT_all", [P, 264], F32) as scT_all, \
             nc.sbuf_tensor("pSt", [4, P], F32) as pSt, \
             nc.sbuf_tensor("pSr", [4, 1], F32) as pSr, \
             nc.psum_tensor("psT", [P, 64], BF16) as psT:

            Fb = [pF[:, i * 2048:(i + 1) * 2048] for i in range(6)]
            Ob = [pOb[:, i * 2048:(i + 1) * 2048] for i in range(2)]

            tern = [("w1", w1_s, HID // C, D), ("w3", w3_s, HID // C, D),
                    ("w2", w2_s, D // C, HID)]
            tiles_t = []
            for mi, (name, src, rows, cols) in enumerate(tern):
                r0 = 0
                while r0 < rows:
                    nr = min(P, rows - r0)
                    tiles_t.append((mi, name, src, r0, nr, cols))
                    r0 += nr
            nts = [sum(1 for tt in tiles_t if tt[0] == mi) for mi in range(3)]
            offs = [0, nts[0], nts[0] + nts[1], nts[0] + nts[1] + nts[2]]

            s_lt1 = sem("lt1")
            v0 = s_v.n

            def t_loads(sync):
                for k, (mi, name, src, r0, nr, cols) in enumerate(tiles_t):
                    if k >= 1:
                        sync.wait_ge(s_v.h, v0 + k + 1)
                    s_lt1.inc(sync.dma_start(out=pL[0:nr, 0:cols],
                                             in_=src[r0:r0 + nr, :]), 16)

            def t_abs(vector):
                vector.memset(sml[:, 0:len(tiles_t)], 0.0)
                vector.drain()
                s_v.inc(vector.engine_nop())
                for k, (mi, name, src, r0, nr, cols) in enumerate(tiles_t):
                    vector.wait_ge(s_lt1.h, 16 * (k + 1))
                    s_v.inc(vector.tensor_reduce(
                        sml[0:nr, k:k + 1], pL[0:nr, 0:cols], AX.X, ALU.add,
                        apply_absolute_value=True))
                vector.drain()
                for mi in range(3):
                    vector.tensor_reduce(sml[:, 32 + mi:33 + mi],
                                         sml[:, offs[mi]:offs[mi + 1]],
                                         AX.X, ALU.add)
                s_v.inc(vector.memset(sml[:, 35:36], 0.0))
            block.sync(t_loads)
            block.vector(t_abs)
            absums_done = s_v.n

            s_vr = sem("vred")

            def t_pred1(gpsimd):
                gpsimd.wait_ge(s_v.h, absums_done)
                s_g.inc(gpsimd.dma_start(out=psum_scr[:, :], in_=sml[:, 32:36]), 16)
                gpsimd.wait_ge(s_g.h, s_g.n)
                s_g.inc(gpsimd.dma_start(
                    out=pSt[:, :], in_=psum_scr.ap().rearrange("p f -> f p"),
                    allow_slow_non_contiguous=True), 16)
                gpsimd.wait_ge(s_g.h, s_g.n)
            block.gpsimd(t_pred1)
            g_tr = s_g.n

            def t_red(vector):
                vector.wait_ge(s_g.h, g_tr)
                s_vr.inc(vector.tensor_reduce(pSr[:, :], pSt[:, :], AX.X, ALU.add))
            block.vector(t_red)

            def t_pred2(gpsimd):
                gpsimd.wait_ge(s_vr.h, s_vr.n)
                s_g.inc(gpsimd.dma_start(
                    out=tsum_i.ap()[0:1, 0:4].rearrange("o f -> f o"),
                    in_=pSr[:, 0:1]), 16)
                gpsimd.wait_ge(s_g.h, s_g.n)
                ar_sem.inc(gpsimd.collective_compute(
                    "AllReduce", ALU.add, replica_groups=[list(range(C))],
                    ins=[tsum_i.ap().opt()], outs=[tsum_o.ap().opt()]))
                gpsimd.wait_ge(ar_sem.h, ar_sem.n)
                repf = rep_scr.ap().rearrange("p f -> (p f)")
                s_g.inc(gpsimd.dma_start(out=repf[0:4], in_=tsum_o.ap().rearrange("o f -> (o f)")), 16)
                gpsimd.wait_ge(s_g.h, s_g.n)
                n = 4
                while n < 512:
                    s_g.inc(gpsimd.dma_start(out=repf[n:2 * n], in_=repf[0:n]), 16)
                    gpsimd.wait_ge(s_g.h, s_g.n)
                    n *= 2
                s_g.inc(gpsimd.dma_start(out=sml[:, 48:52], in_=rep_scr[:, :]), 16)
                gpsimd.wait_ge(s_g.h, s_g.n)
                s_g.inc(gpsimd.dma_start(out=sml[0:1, 44:48], in_=tsum_o[:, :]), 16)
                gpsimd.wait_ge(s_g.h, s_g.n)
            block.gpsimd(t_pred2)
            pred_done = s_g.n

            def t_scale(vector):
                vector.wait_ge(s_g.h, pred_done)
                for mi, (name, src, rows, cols) in enumerate(tern):
                    n_el = float(rows * C * cols)
                    vector.tensor_scalar(sml[:, 52 + mi:53 + mi],
                                         sml[:, 48 + mi:49 + mi],
                                         1.0 / n_el, 1e-5, ALU.mult, ALU.max)
                vector.drain()
                s_v.inc(vector.reciprocal(sml[:, 56:59], sml[:, 52:55]))
            block.vector(t_scale)
            scale_ready = s_v.n

            s_lt2 = sem("lt2")
            s_gs = sem("gs")
            v1 = s_v.n

            def t2_loads(sync):
                sync.wait_ge(s_v.h, scale_ready)
                for k, (mi, name, src, r0, nr, cols) in enumerate(tiles_t):
                    if k >= 1:
                        sync.wait_ge(s_v.h, v1 + k)
                    s_lt2.inc(sync.dma_start(out=pL[0:nr, 0:cols],
                                             in_=src[r0:r0 + nr, :]), 16)

            def t2_q(vector):
                for k, (mi, name, src, r0, nr, cols) in enumerate(tiles_t):
                    vector.wait_ge(s_lt2.h, 16 * (k + 1))
                    if k >= 1:
                        vector.wait_ge(s_gs.h, 16 * k)
                    li = None
                    for c0 in range(0, cols, 2048):
                        cw = min(2048, cols - c0)
                        vector.tensor_scalar(Fb[0][0:nr, 0:cw], pL[0:nr, c0:c0 + cw],
                                             sml[0:nr, 56 + mi:57 + mi], None,
                                             ALU.mult)
                        vector.drain()
                        vector.tensor_copy(pI[0:nr, c0:c0 + cw], Fb[0][0:nr, 0:cw])
                        vector.drain()
                        li = vector.tensor_scalar(pOT[0:nr, c0:c0 + cw], pI[0:nr, c0:c0 + cw],
                                             1, -1, ALU.min, ALU.max)
                    s_v.inc(li)

            def t2_st(gpsimd):
                for k, (mi, name, src, r0, nr, cols) in enumerate(tiles_t):
                    gpsimd.wait_ge(s_v.h, v1 + k + 1)
                    s_gs.inc(gpsimd.dma_start(
                        out=ag[name][0][r0:r0 + nr, :],
                        in_=pOT[0:nr, 0:cols]), 16)
            block.sync(t2_loads)
            block.vector(t2_q)
            block.gpsimd(t2_st)
            tern_stores = s_gs.n

            def t_ag(gpsimd):
                gpsimd.wait_ge(s_gs.h, tern_stores)
                for name in ("w1", "w3", "w2"):
                    ag_sem[name].inc(gpsimd.collective_compute(
                        "AllGather", ALU.bypass, replica_groups=[list(range(C))],
                        ins=[ag[name][0].ap().opt()], outs=[ag[name][1].ap().opt()]))
            block.gpsimd(t_ag)

            fp4 = [("wgu", gup_s, gus_s, GU // C, D),
                   ("wdn", dnp_s, dns_s, D // C, HID)]
            tiles_f = [(name, pk, sc, t, cols)
                       for name, pk, sc, rows, cols in fp4
                       for t in range(rows // P)]
            scT_off = []
            off = 0
            for name, pk, sc, t, cols in tiles_f:
                scT_off.append(off)
                off += cols // P
            chunks_per = [(cols + 2047) // 2048 for _, _, _, _, cols in tiles_f]

            s_lf = sem("lf")
            s_gf = sem("gf")
            t2c = s_t.n
            a2 = s_a.n

            v_base = s_v.n
            v_cvt, v_f, v_chunk_done, v_tile_done = [], [], [], []
            vv = v_base
            ci_total = 0
            for k in range(len(tiles_f)):
                vv += 1
                v_cvt.append(vv)
                vv += 1
                fs, ds = [], []
                for c in range(chunks_per[k]):
                    vv += 1
                    fs.append(vv)
                    vv += 1
                    ds.append(vv)
                v_f.append(fs)
                v_chunk_done.append(ds)
                v_tile_done.append(vv)
                ci_total += chunks_per[k]

            def f_loads(sync):
                sync.wait_ge(s_v.h, v1 + len(tiles_t))
                for k, (name, pk, sc, t, cols) in enumerate(tiles_f):
                    if k >= 1:
                        sync.wait_ge(s_gf.h, 16 * k)
                    s_lf.inc(sync.dma_start(
                        out=pI[:, 0:cols // 8],
                        in_=pk[t * P:(t + 1) * P, :]), 16)
                    s_lf.inc(sync.dma_start(
                        out=sc16[0:cols // P, :],
                        in_=sc[:, t * P:(t + 1) * P]), 16)
            block.sync(f_loads)

            def f_sct(tensor):
                tensor.wait_ge(s_c0.h, consts_done)
                for k, (name, pk, sc, t, cols) in enumerate(tiles_f):
                    ng = cols // P
                    tensor.wait_ge(s_v.h, v_cvt[k])
                    s_t.inc(tensor.transpose(psT[:, 0:ng], sc16b[0:ng, :],
                                             id_s[0:ng, 0:ng]))
            block.tensor(f_sct)

            def f_dec_v(vector):
                vector.wait_ge(s_gs.h, tern_stores)
                for k, (name, pk, sc, t, cols) in enumerate(tiles_f):
                    ng = cols // P
                    vector.wait_ge(s_lf.h, 32 * (k + 1))
                    if k >= 1:
                        vector.wait_ge(s_gf.h, 16 * k)
                    s_v.inc(vector.tensor_copy(sc16b[0:ng, :], sc16[0:ng, :]))
                    vector.wait_ge(s_t.h, t2c + k + 1)
                    s_v.inc(vector.tensor_copy(
                        scT_all[:, scT_off[k]:scT_off[k] + ng], psT[:, 0:ng]))
                    for cidx, c0 in enumerate(range(0, cols, 2048)):
                        cw = min(2048, cols - c0)
                        cw8 = cw // 8
                        i0 = c0 // 8
                        ia, ib = 2048, 2048 + 1024
                        for nk in range(8):
                            vector.tensor_scalar(
                                pI[:, ia:ia + cw8], pI[:, i0:i0 + cw8],
                                4 * nk, 15, ALU.logical_shift_right,
                                ALU.bitwise_and)
                            vector.drain()
                            vector.tensor_copy(Fb[0][:, nk:cw:8],
                                               pI[:, ia:ia + cw8])
                            vector.tensor_scalar(
                                pI[:, ib:ib + cw8], pI[:, ia:ia + cw8],
                                1, None, ALU.bitwise_and)
                            vector.drain()
                            vector.tensor_copy(Fb[1][:, nk:cw:8],
                                               pI[:, ib:ib + cw8])
                        An, Bm = Fb[0][:, 0:cw], Fb[1][:, 0:cw]
                        F2, F3, F4, F5 = (Fb[2][:, 0:cw], Fb[3][:, 0:cw],
                                          Fb[4][:, 0:cw], Fb[5][:, 0:cw])
                        Sg = Ob[0][:, 0:cw]
                        vector.tensor_scalar(F2, An, 8.0, None, ALU.is_ge)
                        vector.drain()
                        vector.tensor_scalar(Sg, F2, -2.0, 1.0, ALU.mult, ALU.add)
                        vector.tensor_scalar(F3, An, 0.5, -1.0, ALU.mult, ALU.add)
                        vector.drain()
                        vector.scalar_tensor_tensor(F4, F2, -4.0, F3,
                                                    ALU.mult, ALU.add)
                        vector.drain()
                        vector.scalar_tensor_tensor(F5, Bm, C1, F4,
                                                    ALU.mult, ALU.add)
                        vector.tensor_scalar(F2, F4, -0.25, None, ALU.is_le)
                        vector.tensor_scalar(F3, F4, 396.83008, 197.83008,
                                             ALU.mult, ALU.add)
                        vector.drain()
                        vector.tensor_tensor(F4, F2, F3, ALU.mult)
                        vector.drain()
                        s_v.inc(vector.tensor_tensor(F2, F5, F4, ALU.add))
                        vector.wait_ge(s_a.h, a2 + sum(chunks_per[:k]) + cidx + 1)
                        vector.tensor_tensor(Ob[1][:, 0:cw], F3, Sg, ALU.mult)
                        vector.drain()
                        ng0 = scT_off[k] + c0 // P
                        li = None
                        for g in range(cw // P):
                            li = vector.tensor_scalar(
                                pOT[:, c0 + g * P:c0 + (g + 1) * P],
                                Ob[1][:, g * P:(g + 1) * P],
                                scT_all[:, ng0 + g:ng0 + g + 1], None, ALU.mult)
                        s_v.inc(li)
            block.vector(f_dec_v)

            def f_dec_a(scalar):
                for k, (name, pk, sc, t, cols) in enumerate(tiles_f):
                    for cidx, c0 in enumerate(range(0, cols, 2048)):
                        cw = min(2048, cols - c0)
                        scalar.wait_ge(s_v.h, v_f[k][cidx])
                        s_a.inc(scalar.activation(Fb[3][:, 0:cw], Fb[2][:, 0:cw],
                                                  AF.Exp, scale=LN2))
            block.scalar(f_dec_a)

            def f_dec_st(gpsimd):
                for k, (name, pk, sc, t, cols) in enumerate(tiles_f):
                    gpsimd.wait_ge(s_v.h, v_tile_done[k])
                    s_gf.inc(gpsimd.dma_start(
                        out=ag[name][0][t * P:(t + 1) * P, :],
                        in_=pOT[:, 0:cols]), 16)
            block.gpsimd(f_dec_st)
            fp4_stores = s_gf.n

            def f_ag(gpsimd):
                gpsimd.wait_ge(s_gf.h, fp4_stores)
                for name in ("wgu", "wdn"):
                    ag_sem[name].inc(gpsimd.collective_compute(
                        "AllGather", ALU.bypass, replica_groups=[list(range(C))],
                        ins=[ag[name][0].ap().opt()], outs=[ag[name][1].ap().opt()]))
            block.gpsimd(f_ag)

            if tap is not None and tap.startswith("prep"):
                wname = tap.split(":")[1] if ":" in tap else "w1"
                wcols = dict((n, c) for n, r, c in ag_shapes)[wname]
                wcols = min(wcols, J * D)
                with nc.sbuf_tensor("dmp", [P, wcols], BF16) as dmp, \
                     nc.sbuf_tensor("dmpf", [P, wcols], F32) as dmpf:
                    s_dmp = sem("dmp")

                    def d1(sync):
                        sync.wait_ge(ag_sem[wname].h, 1)
                        s_dmp.inc(sync.dma_start(out=dmp[:, :],
                                                 in_=ag[wname][1][0:P, 0:wcols]), 16)

                    def d2(vector):
                        vector.wait_ge(s_dmp.h, s_dmp.n)
                        s_v.inc(vector.tensor_copy(dmpf[:, :], dmp[:, :]))
                    block.sync(d1)
                    block.vector(d2)
                    tap_out(dmpf[:, :], wcols, [(s_v, s_v.n)])
                    return nc

        barrier()

        NS = max(1, L // 512)
        SW = min(512, L)

        def spans(lo, hi):
            out = []
            a = lo
            while a < hi:
                b = min(hi, (a // 512 + 1) * 512)
                out.append((a, b))
                a = b
            return out

        agk_sem = [sem(f"agk{l}") for l in range(depth)]
        agv_sem = [sem(f"agv{l}") for l in range(depth)]
        tapret = {"stop": False}

        def do_tap(name, src_ap, ncols, wait_sems):
            if tap == name:
                tap_out(src_ap, ncols, wait_sems)
                tapret["stop"] = True

        def ph_ln(l, xn, lnscr):
            a0, v0 = s_a.n, s_v.n

            def vsum(vector):
                vector.wait_ge(s_c0.h, consts_done)
                if l > 0:
                    vector.wait_ge(bar, barn[0])
                vector.drain()
                for t in range(J):
                    s_v.inc(vector.tensor_reduce(
                        sml[:, t:t + 1], x_res[:, t * D:(t + 1) * D],
                        AX.X, ALU.add))
            block.vector(vsum)

            def asq(scalar):
                scalar.wait_ge(s_c0.h, consts_done)
                if l > 0:
                    scalar.wait_ge(bar, barn[0])
                for t in range(J):
                    s_a.inc(scalar.activation(
                        lnscr[:, :], x_res[:, t * D:(t + 1) * D], AF.Square,
                        accum_out=sml[:, 8 + t:9 + t]))
                    scalar.drain()
            block.scalar(asq)

            def vstats(vector):
                vector.wait_ge(s_a.h, a0 + J)
                vector.drain()
                vector.tensor_scalar(sml[:, 16:16 + J], sml[:, 0:J],
                                     1.0 / D, None, ALU.mult)
                vector.drain()
                vector.tensor_scalar(sml[:, 24:24 + J], sml[:, 8:8 + J],
                                     1.0 / D, None, ALU.mult)
                vector.tensor_tensor(sml[:, 32:32 + J], sml[:, 16:16 + J],
                                     sml[:, 16:16 + J], ALU.mult)
                vector.drain()
                vector.tensor_tensor(sml[:, 8:8 + J], sml[:, 24:24 + J],
                                     sml[:, 32:32 + J], ALU.subtract)
                vector.drain()
                s_v.inc(vector.tensor_scalar(sml[:, 24:24 + J], sml[:, 8:8 + J],
                                             1e-5, None, ALU.add))
            block.vector(vstats)

            def asqrt(scalar):
                scalar.wait_ge(s_v.h, v0 + J + 1)
                s_a.inc(scalar.sqrt(sml[:, 32:32 + J], sml[:, 24:24 + J]))
            block.scalar(asqrt)

            def vrecip(vector):
                vector.wait_ge(s_a.h, a0 + J + 1)
                vector.reciprocal(sml[:, 0:J], sml[:, 32:32 + J])
                vector.drain()
                s_v.inc(vector.scalar_tensor_tensor(
                    sml[:, 8:8 + J], sml[:, 16:16 + J],
                    -1.0, sml[:, 0:J], ALU.mult, ALU.mult))
            block.vector(vrecip)

            def anorm(vector):
                vector.wait_ge(s_v.h, v0 + J + 2)
                li = None
                for t in range(J):
                    li = vector.tensor_scalar(
                        xn[:, t * D:(t + 1) * D], x_res[:, t * D:(t + 1) * D],
                        sml[:, t:t + 1], sml[:, 8 + t:9 + t],
                        ALU.mult, ALU.add)
                s_v.inc(li)
            block.vector(anorm)
            return s_v.n

        def ph_transpose(src, dst, ntile, ready, src_cols=D):
            nblk = src_cols // P
            t0, v0 = s_t.n, s_v.n
            with nc.psum_tensor(U("psTTa"), [P, P], BF16) as psTTa, \
                 nc.psum_tensor(U("psTTb"), [P, P], BF16) as psTTb:
                psTT = [psTTa, psTTb]
                def tt_(tensor):
                    tensor.wait_ge(ready[0].h, ready[1])
                    i = 0
                    for dk in range(nblk):
                        for t in range(ntile):
                            par = i % 2
                            if i >= 2:
                                tensor.wait_ge(s_v.h, v0 + i - 1)
                            s_t.inc(tensor.transpose(
                                psTT[par][:, :],
                                src[:, t * src_cols + dk * P:
                                    t * src_cols + (dk + 1) * P],
                                id_s[:, :]))
                            i += 1
                block.tensor(tt_)

                def tv_(vector):
                    i = 0
                    for dk in range(nblk):
                        for t in range(ntile):
                            par = i % 2
                            vector.wait_ge(s_t.h, t0 + i + 1)
                            s_v.inc(vector.tensor_copy(
                                dst[:, dk * L + t * P:dk * L + (t + 1) * P],
                                psTT[par][:, :]))
                            i += 1
                block.vector(tv_)
            return s_v.n


        def ph_qkv(l, xn, xnT, xnT_ready_v):
            wname = "wq0" if l == 0 else "wq1"
            wfull = ag[wname][1]
            rope = (l == 1)
            ki, ko, vi, vo = kvt[l]
            w0, st0 = s_w.n, s_st.n
            sp_base = [s_sp[0].n, s_sp[1].n]
            NOB = (3 * D) // 512
            tiles = []
            tt, vv = s_t.n, s_v.n
            for ob in range(NOB):
                for ot in range(4):
                    o = ob * 4 + ot
                    for sh in range(NS):
                        d = {"ob": ob, "o": o, "sh": sh}
                        tt += 1
                        d["t_grp"] = tt
                        vv += 1
                        d["v_evac"] = vv
                        if rope and o < 32:
                            tt += 1
                            d["t_rot"] = tt
                            vv += 1
                            d["v_rot"] = vv
                        elif o >= 32:
                            nss = SW // P
                            d["t_vt"] = []
                            d["v_vt"] = []
                            for ss in range(nss):
                                tt += 1
                                d["t_vt"].append(tt)
                                vv += 1
                                d["v_vt"].append(vv)
                        d["t_done"] = tt
                        d["v_done"] = vv
                        tiles.append(d)
            with nc.sbuf_tensor(U("ropes"), [P, (2 * L) if rope else 2], F32) \
                    as rope_s, \
                 nc.sbuf_tensor(U("wbufA"), [P, 16 * 512], BF16) as wbufA, \
                 nc.sbuf_tensor(U("wbufB"), [P, 16 * 512], BF16) as wbufB, \
                 nc.sbuf_tensor(U("qs0"), [P, 2 * SW], BF16) as qs0, \
                 nc.sbuf_tensor(U("qs1"), [P, 2 * SW], BF16) as qs1, \
                 nc.sbuf_tensor(U("vst"), [P, 2 * SW], BF16) as vst, \
                 nc.sbuf_tensor(U("rm1"), [P, SW], F32) as rm1, \
                 nc.sbuf_tensor(U("rm2"), [P, SW], F32) as rm2, \
                 nc.psum_tensor(U("psQ"), [P, 2 * 512], F32) as psQ, \
                 nc.psum_tensor(U("psR"), [P, 512], F32) as psR, \
                 nc.psum_tensor(U("psVTa"), [P, P], BF16) as psVTa, \
                 nc.psum_tensor(U("psVTb"), [P, P], BF16) as psVTb:
                psVT = [psVTa, psVTb]
                wbufs = [wbufA, wbufB]
                sin_s = rope_s[:, 0:L] if rope else None
                cos_s = rope_s[:, L:2 * L] if rope else None
                s_rp = sem(U("rp"))
                if rope:
                    def rload(sync):
                        s_rp.inc(sync.dma_start(out=sin_s,
                                                in_=sin_in[:, :]), 16)
                        s_rp.inc(sync.dma_start(out=cos_s,
                                                in_=cos_in[:, :]), 16)
                    block.sync(rload)

                wq_tgt = [s_wp[0].n + 256 * (ob // 2 + 1) if ob % 2 == 0
                          else s_wp[1].n + 256 * (ob // 2 + 1)
                          for ob in range(NOB)]

                def qkv_w(scalar):
                    scalar.wait_ge(ag_sem[wname].h, 1)
                    scalar.wait_ge(s_v.h, xnT_ready_v)
                    for ob in range(NOB):
                        if ob >= 2:
                            scalar.wait_ge(s_t.h,
                                           tiles[(ob - 1) * 4 * NS - 1]["t_done"])
                        for dk in range(16):
                            s_wp[ob % 2].inc(scalar.dma_start(
                                out=wbufs[ob % 2][:, dk * 512:(dk + 1) * 512],
                                in_=wfull[ob * 512:(ob + 1) * 512,
                                          dk * P:(dk + 1) * P],
                                transpose=True), 16)
                block.scalar(qkv_w)

                def qkv_t(tensor):
                    tensor.wait_ge(s_v.h, xnT_ready_v)
                    for i, d in enumerate(tiles):
                        ob, o, sh = d["ob"], d["o"], d["sh"]
                        pp = i % 2
                        if sh == 0 and d["o"] % 4 == 0:
                            tensor.wait_ge(s_wp[ob % 2].h, wq_tgt[ob])
                        if i >= 2:
                            tensor.wait_ge(s_v.h, tiles[i - 2]["v_done"])
                        for dk in range(16):
                            mm = tensor.matmul(
                                psQ[:, pp * 512:pp * 512 + SW],
                                wbufs[ob % 2][:, dk * 512 + ot_col(o):
                                              dk * 512 + ot_col(o) + P],
                                xnT[:, dk * L + sh * 512:dk * L + sh * 512 + SW],
                                start=(dk == 0), stop=(dk == 15))
                        s_t.inc(mm)
                        if "t_rot" in d:
                            tensor.wait_ge(s_v.h, d["v_evac"])
                            s_t.inc(tensor.matmul(
                                psR[:, 0:SW], rot_s[:, :],
                                qs0[:, pp * SW:pp * SW + SW],
                                start=True, stop=True))
                        elif "t_vt" in d:
                            tensor.wait_ge(s_v.h, d["v_evac"])
                            for ss in range(len(d["t_vt"])):
                                if ss >= 2:
                                    tensor.wait_ge(s_v.h, d["v_vt"][ss - 2])
                                s_t.inc(tensor.transpose(
                                    psVT[ss % 2][:, :],
                                    qs0[:, pp * SW + ss * P:pp * SW + (ss + 1) * P],
                                    id_s[:, :]))
                block.tensor(qkv_t)

                def qkv_v(vector):
                    if rope:
                        vector.wait_ge(s_rp.h, s_rp.n)
                    for i, d in enumerate(tiles):
                        o, sh = d["o"], d["sh"]
                        pp = i % 2
                        vector.wait_ge(s_t.h, d["t_grp"])
                        if i >= 2:
                            vector.wait_ge(s_sp[i % 2].h,
                                           sp_base[i % 2] + 16 * (i // 2))
                        s_v.inc(vector.tensor_copy(
                            qs0[:, pp * SW:pp * SW + SW],
                            psQ[:, pp * 512:pp * 512 + SW]))
                        if "t_rot" in d:
                            vector.drain()
                        if "t_rot" in d:
                            vector.wait_ge(s_t.h, d["t_rot"])
                            cs = sh * 512
                            vector.tensor_tensor(
                                rm1[:, 0:SW], qs0[:, pp * SW:pp * SW + SW],
                                cos_s[:, cs:cs + SW], ALU.mult)
                            vector.tensor_tensor(
                                rm2[:, 0:SW], psR[:, 0:SW],
                                sin_s[:, cs:cs + SW], ALU.mult)
                            vector.drain()
                            s_v.inc(vector.tensor_tensor(
                                qs1[:, pp * SW:pp * SW + SW],
                                rm1[:, 0:SW], rm2[:, 0:SW], ALU.add))
                        elif "t_vt" in d:
                            for ss in range(len(d["t_vt"])):
                                vector.wait_ge(s_t.h, d["t_vt"][ss])
                                s_v.inc(vector.tensor_copy(
                                    vst[:, pp * SW + ss * P:pp * SW + (ss + 1) * P],
                                    psVT[ss % 2][:, :]))
                block.vector(qkv_v)

                def qkv_st(gpsimd):
                    for i, d in enumerate(tiles):
                        o, sh = d["o"], d["sh"]
                        pp = i % 2
                        gpsimd.wait_ge(s_v.h, d["v_done"])
                        if o < 16:
                            srcb = qs1 if rope else qs0
                            s_sp[i % 2].inc(gpsimd.dma_start(
                                out=q_stash[:, o * L + sh * 512:
                                            o * L + sh * 512 + SW],
                                in_=srcb[:, pp * SW:pp * SW + SW]), 16)
                        elif o < 32:
                            srcb = qs1 if rope else qs0
                            s_sp[i % 2].inc(gpsimd.dma_start(
                                out=ki[(o - 16) * P:(o - 15) * P,
                                       sh * 512:sh * 512 + SW],
                                in_=srcb[:, pp * SW:pp * SW + SW]), 16)
                        else:
                            s_sp[i % 2].inc(gpsimd.dma_start(
                                out=vi.ap()[o - 32, :,
                                            sh * 512:sh * 512 + SW],
                                in_=vst[:, pp * SW:pp * SW + SW]), 16)
                    if l == 0:
                        xr = xn_stash.ap().rearrange("(t p) d -> p t d", p=P)
                        s_st.inc(gpsimd.dma_start(
                            out=xr,
                            in_=xn.ap().rearrange("p (t d) -> p t d", d=D)), 16)
                    else:
                        s_st.inc(gpsimd.dma_start(
                            out=xnT_stash[:, :], in_=xnT[:, :]), 16)
                    gpsimd.wait_ge(s_sp[0].h, s_sp[0].n)
                    gpsimd.wait_ge(s_sp[1].h, s_sp[1].n)
                    gpsimd.wait_ge(s_st.h, s_st.n)
                    agk_sem[l].inc(gpsimd.collective_compute(
                        "AllGather", ALU.bypass, replica_groups=[list(range(C))],
                        ins=[ki.ap().opt()], outs=[ko.ap().opt()]))
                    agv_sem[l].inc(gpsimd.collective_compute(
                        "AllGather", ALU.bypass, replica_groups=[list(range(C))],
                        ins=[vi.ap().opt()], outs=[vo.ap().opt()]))
                block.gpsimd(qkv_st)

        def ot_col(o):
            return (o % 4) * P

        def ph_attn(l):
            ki, ko, vi, vo = kvt[l]
            with nc.sbuf_tensor(U("QT"), [P, 2 * L], BF16) as QT, \
                 nc.sbuf_tensor(U("aoT"), [P, 16 * L], BF16) as aoT:
              with nc.sbuf_tensor(U("KTb"), [P, 2 * NKC * P], BF16) as KTb, \
                 nc.sbuf_tensor(U("Vb"), [P, 2 * NKC * P], BF16) as Vb, \
                 nc.sbuf_tensor(U("exb"), [P, 2 * L], BF16) as exb, \
                 nc.sbuf_tensor(U("dsc"), [P, 2 * P], BF16) as dsc, \
                 nc.sbuf_tensor(U("zsb"), [1, 2 * 512], F32) as zsb, \
                 nc.sbuf_tensor(U("rzb"), [1, 2 * 512], F32) as rzb, \
                 nc.sbuf_tensor(U("bbs"), [P, 2 * 512], F32) as bbs, \
                 nc.psum_tensor(U("Sps0"), [P, L], F32) as Sps0, \
                 nc.psum_tensor(U("Sps1"), [P, L], F32) as Sps1, \
                 nc.psum_tensor(U("Ops"), [P, L], F32) as Ops, \
                 nc.psum_tensor(U("Zps0"), [1, 512], F32) as Zps0, \
                 nc.psum_tensor(U("Zps1"), [1, 512], F32) as Zps1:
                Sps = [Sps0, Sps1]
                Zps = [Zps0, Zps1]
                kv0 = s_kv.n
                s_kp = [sem(U("kpa")), sem(U("kpb"))]
                kp0 = [s_kp[0].n, s_kp[1].n]

                heads_plan = []
                tt, aa, vv = s_t.n, s_a.n, s_v.n
                for h in range(H):
                    plan = {"qk": [0] * NKC, "zav": [0] * NKC,
                            "exp": [], "mask": []}
                    for kc in range(NKC):
                        tt += 1
                        plan["qk"][kc] = tt
                        if kc >= 1:
                            tt += 1
                            plan["zav"][kc - 1] = tt
                    tt += 1
                    plan["zav"][NKC - 1] = tt
                    for kc in range(NKC):
                        aa += 1
                        plan["exp"].append(aa)
                        vv += 1
                        plan["mask"].append(vv)
                    tt += NS
                    plan["bmm"] = tt
                    vv += 2 * NS
                    plan["vdone"] = vv
                    heads_plan.append(plan)

                def kvload(sync):
                    sync.wait_ge(agk_sem[l].h, agk_sem[l].n)
                    sync.wait_ge(agv_sem[l].h, agv_sem[l].n)
                    for h in range(H):
                        hb = (h % 2) * NKC * P
                        if h >= 2:
                            sync.wait_ge(s_t.h, heads_plan[h - 2]["zav"][-1])
                        s_kp[h % 2].inc(sync.dma_start(
                            out=QT[:, (h % 2) * L:(h % 2 + 1) * L],
                            in_=q_stash[:, h * L:(h + 1) * L]), 16)
                        for r in range(C):
                            s_kp[h % 2].inc(sync.dma_start(
                                out=KTb[:, hb + r * J * P:hb + (r + 1) * J * P],
                                in_=ko.ap()[r, h * P:(h + 1) * P, :]), 16)
                            s_kp[h % 2].inc(sync.dma_start(
                                out=Vb[:, hb + r * J * P:hb + (r + 1) * J * P],
                                in_=vo.ap()[r, h, :, :]), 16)
                block.sync(kvload)

                def attn_t(tensor):
                    def emit_qk(plan, h, kc):
                        jmin = kc // 8
                        qlo = jmin * P
                        slot = kc % 2
                        if kc >= 2:
                            tensor.wait_ge(s_a.h, plan["exp"][kc - 2])
                        kb = (h % 2) * NKC * P + ((kc % 8) * J + kc // 8) * P
                        qb = (h % 2) * L
                        for (pa, pb) in spans(qlo, L):
                            mm = tensor.matmul(
                                Sps[slot][:, pa:pb],
                                KTb[:, kb:kb + P],
                                QT[:, qb + pa:qb + pb],
                                start=True, stop=True)
                        s_t.inc(mm)

                    def emit_zav(plan, h, kc):
                        jmin = kc // 8
                        qlo = jmin * P
                        slot = kc % 2
                        tensor.wait_ge(s_v.h, plan["mask"][kc])
                        kb = (h % 2) * NKC * P + ((kc % 8) * J + kc // 8) * P
                        for (pa, pb) in spans(qlo, L):
                            half = pa // 512
                            kc_max = 8 * min(J, (half + 1) * 4)
                            tensor.matmul(
                                Zps[half][:, pa - half * 512:pb - half * 512],
                                onec_s[:, :],
                                exb[:, slot * L + pa:slot * L + pb],
                                start=(kc == 0), stop=(kc == kc_max - 1))
                            mm = tensor.matmul(
                                Ops[:, pa:pb],
                                Vb[:, kb:kb + P],
                                exb[:, slot * L + pa:slot * L + pb],
                                start=(kc == 0), stop=(kc == kc_max - 1))
                        s_t.inc(mm)

                    for h in range(H):
                        plan = heads_plan[h]
                        tensor.wait_ge(s_kp[h % 2].h,
                                       kp0[h % 2] + 272 * (h // 2 + 1))
                        if h >= 1:
                            tensor.wait_ge(s_v.h, heads_plan[h - 1]["vdone"])
                        for kc in range(NKC):
                            emit_qk(plan, h, kc)
                            if kc >= 1:
                                emit_zav(plan, h, kc - 1)
                        emit_zav(plan, h, NKC - 1)
                        tensor.wait_ge(s_a.h, plan["exp"][NKC - 1])
                        tensor.wait_ge(s_v.h, plan["vdone"] - NS)
                        for half in range(NS):
                            s_t.inc(tensor.matmul(
                                Sps[half][:, 0:SW], onerf_s[:, :],
                                rzb[0:1, half * 512:half * 512 + SW],
                                start=True, stop=True))
                block.tensor(attn_t)

                def attn_a(scalar):
                    for h in range(H):
                        plan = heads_plan[h]
                        for kc in range(NKC):
                            jmin = kc // 8
                            qlo = jmin * P
                            slot = kc % 2
                            scalar.wait_ge(s_t.h, plan["qk"][kc])
                            if kc >= 2:
                                scalar.wait_ge(s_t.h, plan["zav"][kc - 2])
                            ai = scalar.activation(
                                dsc[:, slot * P:(slot + 1) * P],
                                Sps[slot][:, qlo:qlo + P], AF.Exp,
                                scale=1.0 / math.sqrt(DH))
                            if qlo + P < L:
                                ai = scalar.activation(
                                    exb[:, slot * L + qlo + P:slot * L + L],
                                    Sps[slot][:, qlo + P:L], AF.Exp,
                                    scale=1.0 / math.sqrt(DH))
                            s_a.inc(ai)
                block.scalar(attn_a)

                def attn_v(vector):
                    for h in range(H):
                        plan = heads_plan[h]
                        for kc in range(NKC):
                            jmin = kc // 8
                            qlo = jmin * P
                            slot = kc % 2
                            m = kc % 8
                            vector.wait_ge(s_a.h, plan["exp"][kc])
                            s_v.inc(vector.tensor_tensor(
                                exb[:, slot * L + qlo:slot * L + qlo + P],
                                dsc[:, slot * P:(slot + 1) * P],
                                mask_s[:, m * P:(m + 1) * P], ALU.mult))
                        for half in range(NS):
                            kc_max = 8 * min(J, (half + 1) * 4)
                            vector.wait_ge(s_t.h, plan["zav"][kc_max - 1])
                            vector.tensor_copy(zsb[0:1, half * 512:half * 512 + SW],
                                               Zps[half][0:1, 0:SW])
                            vector.drain()
                            s_v.inc(vector.reciprocal(
                                rzb[0:1, half * 512:half * 512 + SW],
                                zsb[0:1, half * 512:half * 512 + SW]))
                        for half in range(NS):
                            vector.wait_ge(s_t.h, plan["bmm"] - NS + half + 1)
                            vector.tensor_copy(bbs[:, half * 512:half * 512 + SW],
                                               Sps[half][:, 0:SW])
                            vector.drain()
                            s_v.inc(vector.tensor_tensor(
                                aoT[:, h * L + half * 512:h * L + half * 512 + SW],
                                Ops[:, half * 512:half * 512 + SW],
                                bbs[:, half * 512:half * 512 + SW], ALU.mult))
                block.vector(attn_v)
              do_tap(f"attn{l}", aoT[:, 0:min(16 * L, J * D)],
                     min(16 * L, J * D), [(s_v, s_v.n)])
              if tapret["stop"]:
                  return
              barrier()

              if True:
                woname = "wo0" if l == 0 else "wo1"
                wofull = ag[woname][1]
                t0, v0 = s_t.n, s_v.n
                wo_tgt = [s_wp[ob % 2].n + 256 * (ob // 2 + 1)
                          for ob in range(D // 512)]
                with nc.sbuf_tensor(U("wbufO"), [P, 16 * 512], BF16) as wbufO, \
                     nc.sbuf_tensor(U("wbufO2"), [P, 16 * 512], BF16) as wbufO2, \
                     nc.sbuf_tensor(U("tmpO"), [P, 2 * 512], F32) as tmpO, \
                     nc.psum_tensor(U("psO"), [P, 2 * 512], F32) as psO:
                    wobufs = [wbufO, wbufO2]
                    NOB2 = D // 512

                    def wo_w(scalar):
                        scalar.wait_ge(ag_sem[woname].h, 1)
                        scalar.wait_ge(bar, barn[0])
                        for ob in range(NOB2):
                            if ob >= 2:
                                scalar.wait_ge(s_t.h, t0 + (ob - 1) * J)
                            for dk in range(16):
                                s_wp[ob % 2].inc(scalar.dma_start(
                                    out=wobufs[ob % 2][:, dk * 512:(dk + 1) * 512],
                                    in_=wofull[ob * 512:(ob + 1) * 512,
                                               dk * P:(dk + 1) * P],
                                    transpose=True), 16)
                    block.scalar(wo_w)

                    def wo_t(tensor):
                        i = 0
                        for ob in range(NOB2):
                            tensor.wait_ge(s_wp[ob % 2].h, wo_tgt[ob])
                            for st in range(J):
                                pp = i % 2
                                if i >= 2:
                                    tensor.wait_ge(s_v.h, v0 + i - 1)
                                for dk in range(16):
                                    mm = tensor.matmul(
                                        psO[:, pp * 512:(pp + 1) * 512],
                                        aoT[:, dk * L + st * P:dk * L + (st + 1) * P],
                                        wobufs[ob % 2][:, dk * 512:(dk + 1) * 512],
                                        start=(dk == 0), stop=(dk == 15))
                                s_t.inc(mm)
                                i += 1
                    block.tensor(wo_t)

                    def wo_v(vector):
                        vector.drain()
                        i = 0
                        for ob in range(NOB2):
                            for st in range(J):
                                pp = i % 2
                                vector.wait_ge(s_t.h, t0 + i + 1)
                                xsl = x_res[:, st * D + ob * 512:
                                            st * D + (ob + 1) * 512]
                                tsl = tmpO[:, pp * 512:(pp + 1) * 512]
                                vector.tensor_tensor(
                                    tsl, psO[:, pp * 512:(pp + 1) * 512],
                                    xsl, ALU.add)
                                vector.drain()
                                s_v.inc(vector.tensor_copy(xsl, tsl))
                                i += 1
                    block.vector(wo_v)

        def ph_ffn0():
            NH = HID // 512
            NHK = HID // P
            HF = min(3, J)
            nhalves = (J + HF - 1) // HF
            if True:
                s_xn = sem("ffn0xn")
                with nc.sbuf_tensor(U("xqT"), [P, 16 * L], BF16) as xqT:
                    with nc.sbuf_tensor(U("xnt_in"), [P, D], BF16) as xnt_in, \
                         nc.sbuf_tensor(U("xqt"), [P, D], BF16) as xqt, \
                         nc.sbuf_tensor(U("qsf"), [P, 512], F32) as qsf, \
                         nc.sbuf_tensor(U("qsi"), [P, 512], I32) as qsi, \
                         nc.psum_tensor(U("psTqa"), [P, P], BF16) as psTqa, \
                         nc.psum_tensor(U("psTqb"), [P, P], BF16) as psTqb:
                        psTq = [psTqa, psTqb]
                        tt, vv = s_t.n, s_v.n
                        q_plan = []
                        for t in range(J):
                            vv += 1
                            d = {"vq": vv, "t_vt": [], "v_vt": []}
                            for dk in range(16):
                                tt += 1
                                d["t_vt"].append(tt)
                                vv += 1
                                d["v_vt"].append(vv)
                            q_plan.append(d)

                        def xload(sync):
                            sync.wait_ge(bar, barn[0])
                            for t in range(J):
                                if t >= 1:
                                    sync.wait_ge(s_v.h, q_plan[t - 1]["vq"])
                                s_xn.inc(sync.dma_start(
                                    out=xnt_in[:, :],
                                    in_=xn_stash[t * P:(t + 1) * P, :]), 16)
                        block.sync(xload)

                        def q_v(vector):
                            for t in range(J):
                                vector.wait_ge(s_xn.h, 16 * (t + 1))
                                vector.tensor_reduce(
                                    sml[:, 48:49], xnt_in[:, :], AX.X, ALU.max,
                                    apply_absolute_value=True)
                                vector.drain()
                                vector.tensor_scalar(sml[:, 49:50], sml[:, 48:49],
                                                     1e-5, None, ALU.max)
                                vector.drain()
                                vector.reciprocal(sml[:, 50:51], sml[:, 49:50])
                                vector.drain()
                                vector.tensor_scalar(sml[:, 51:52], sml[:, 50:51],
                                                     127.0, None, ALU.mult)
                                vector.tensor_scalar(sml[:, 24 + t:25 + t],
                                                     sml[:, 49:50],
                                                     1.0 / 127.0, None, ALU.mult)
                                vector.drain()
                                vector.tensor_scalar(sml[:, 32 + t:33 + t],
                                                     sml[:, 24 + t:25 + t],
                                                     sml[:, 52:53], None, ALU.mult)
                                vector.tensor_scalar(sml[:, 40 + t:41 + t],
                                                     sml[:, 24 + t:25 + t],
                                                     sml[:, 53:54], None, ALU.mult)
                                vector.drain()
                                li = None
                                for ch in range(4):
                                    sl = slice(ch * 512, (ch + 1) * 512)
                                    vector.tensor_scalar(qsf[:, :], xnt_in[:, sl],
                                                         sml[:, 51:52], None,
                                                         ALU.mult)
                                    vector.drain()
                                    vector.tensor_copy(qsi[:, :], qsf[:, :])
                                    vector.drain()
                                    li = vector.tensor_scalar(xqt[:, sl], qsi[:, :],
                                                              127, -128,
                                                              ALU.min, ALU.max)
                                s_v.inc(li)
                                for dk in range(16):
                                    vector.wait_ge(s_t.h, q_plan[t]["t_vt"][dk])
                                    s_v.inc(vector.tensor_copy(
                                        xqT[:, dk * L + t * P:dk * L + (t + 1) * P],
                                        psTq[dk % 2][:, :]))
                        block.vector(q_v)

                        def q_t(tensor):
                            for t in range(J):
                                tensor.wait_ge(s_v.h, q_plan[t]["vq"])
                                for dk in range(16):
                                    if dk >= 2:
                                        tensor.wait_ge(s_v.h,
                                                       q_plan[t]["v_vt"][dk - 2])
                                    s_t.inc(tensor.transpose(
                                        psTq[dk % 2][:, :],
                                        xqt[:, dk * P:(dk + 1) * P],
                                        id_s[:, :]))
                        block.tensor(q_t)
                    xqT_ready_v = s_v.n

                    do_tap("xq0", xqT[:, 0:min(16 * L, J * D)],
                           min(16 * L, J * D), [(s_v, xqT_ready_v)])
                    if tapret["stop"]:
                        return

                    with nc.sbuf_tensor(U("wb1"), [P, 2 * 16 * 512], BF16) as wb1, \
                         nc.sbuf_tensor(U("wb3"), [P, 16 * 512], BF16) as wb3, \
                         nc.sbuf_tensor(U("hbuf"), [P, HF * HID], BF16) as hbuf, \
                         nc.sbuf_tensor(U("tm1"), [P, 512], F32) as tm1, \
                         nc.sbuf_tensor(U("hqf"), [P, 256], F32) as hqf, \
                         nc.sbuf_tensor(U("hqi"), [P, 256], I32) as hqi, \
                         nc.sbuf_tensor(U("hq"), [P, HID], BF16) as hq, \
                         nc.sbuf_tensor(U("hstg"), [P, (NHK // 2) * P], BF16) as hstg, \
                         nc.psum_tensor(U("psB1"), [P, 512], F32) as psB1, \
                         nc.psum_tensor(U("psB3"), [P, 512], F32) as psB3, \
                         nc.psum_tensor(U("psHTa"), [P, P], BF16) as psHTa, \
                         nc.psum_tensor(U("psHTb"), [P, P], BF16) as psHTb:
                        psHT = [psHTa, psHTb]
                        w0, st0 = s_w.n, s_st.n
                        s_b3 = sem("fb3")
                        spb = [s_wp[0].n, s_wp[1].n]
                        tt, aa, vv = s_t.n, s_a.n, s_v.n
                        plan_mm = {}
                        plan_q = {}
                        for hf in range(nhalves):
                            sts = list(range(hf * HF, min(J, (hf + 1) * HF)))
                            for hg in range(NH):
                                for si, st in enumerate(sts):
                                    tt += 1
                                    aa += 1
                                    vv += 1
                                    plan_mm[(hf, hg, si)] = {
                                        "t": tt, "a": aa, "v": vv}
                            for si, st in enumerate(sts):
                                vv += 1
                                q = {"vq": vv, "t_vt": [], "v_vt": []}
                                for hk in range(NHK):
                                    tt += 1
                                    q["t_vt"].append(tt)
                                    vv += 1
                                    q["v_vt"].append(vv)
                                plan_q[(hf, si)] = q

                        groups_f = [(hf, hg) for hf in range(nhalves)
                                    for hg in range(NH)]
                        b3_0 = s_b3.n

                        def f_w(scalar):
                            scalar.wait_ge(ag_sem["w1"].h, 1)
                            scalar.wait_ge(ag_sem["w3"].h, 1)
                            scalar.wait_ge(s_v.h, xqT_ready_v)
                            for g, (hf, hg) in enumerate(groups_f):
                                pg = (g % 2) * 16 * 512
                                if g >= 2:
                                    phf, phg = groups_f[g - 2]
                                    nsi = len(list(range(
                                        phf * HF, min(J, (phf + 1) * HF))))
                                    scalar.wait_ge(
                                        s_t.h,
                                        plan_mm[(phf, phg, nsi - 1)]["t"])
                                for dk in range(16):
                                    s_wp[g % 2].inc(scalar.dma_start(
                                        out=wb1[:, pg + dk * 512:
                                                pg + (dk + 1) * 512],
                                        in_=ag["w1"][1][hg * 512:(hg + 1) * 512,
                                                        dk * P:(dk + 1) * P],
                                        transpose=True), 16)
                                if g >= 1:
                                    phf, phg = groups_f[g - 1]
                                    nsi = len(list(range(
                                        phf * HF, min(J, (phf + 1) * HF))))
                                    scalar.wait_ge(
                                        s_t.h,
                                        plan_mm[(phf, phg, nsi - 1)]["t"])
                                for dk in range(16):
                                    s_b3.inc(scalar.dma_start(
                                        out=wb3[:, dk * 512:(dk + 1) * 512],
                                        in_=ag["w3"][1][hg * 512:(hg + 1) * 512,
                                                        dk * P:(dk + 1) * P],
                                        transpose=True), 16)
                        block.sync(f_w)

                        def f_t(tensor):
                            i = 0
                            for hf in range(nhalves):
                                sts = list(range(hf * HF, min(J, (hf + 1) * HF)))
                                for hg in range(NH):
                                    pg = (i % 2) * 16 * 512
                                    tensor.wait_ge(
                                        s_wp[i % 2].h,
                                        spb[i % 2] + 256 * (i // 2 + 1))
                                    i += 1
                                    tensor.wait_ge(s_b3.h, b3_0 + 256 * i)
                                    for si, st in enumerate(sts):
                                        key = (hf, hg, si)
                                        if not (hf == 0 and hg == 0 and si == 0):
                                            tensor.wait_ge(
                                                s_v.h, plan_mm[key]["v"] - 1)
                                        for dk in range(16):
                                            tensor.matmul(
                                                psB1[:, :],
                                                xqT[:, dk * L + st * P:
                                                    dk * L + (st + 1) * P],
                                                wb1[:, pg + dk * 512:
                                                    pg + (dk + 1) * 512],
                                                start=(dk == 0), stop=(dk == 15))
                                        for dk in range(16):
                                            mm = tensor.matmul(
                                                psB3[:, :],
                                                xqT[:, dk * L + st * P:
                                                    dk * L + (st + 1) * P],
                                                wb3[:, dk * 512:(dk + 1) * 512],
                                                start=(dk == 0), stop=(dk == 15))
                                        s_t.inc(mm)
                                for si, st in enumerate(sts):
                                    q = plan_q[(hf, si)]
                                    tensor.wait_ge(s_v.h, q["vq"])
                                    for hk in range(NHK):
                                        if hk >= 2:
                                            tensor.wait_ge(s_v.h,
                                                           q["v_vt"][hk - 2])
                                        s_t.inc(tensor.transpose(
                                            psHT[hk % 2][:, :],
                                            hq[:, hk * P:(hk + 1) * P],
                                            id_s[:, :]))
                        block.tensor(f_t)

                        def f_a(scalar):
                            for hf in range(nhalves):
                                sts = list(range(hf * HF, min(J, (hf + 1) * HF)))
                                for hg in range(NH):
                                    for si, st in enumerate(sts):
                                        key = (hf, hg, si)
                                        scalar.wait_ge(s_t.h, plan_mm[key]["t"])
                                        ai = scalar.activation(
                                            tm1[:, :], psB1[:, :], AF.Silu,
                                            scale=sml[:, 32 + st:33 + st])
                                        s_a.inc(ai)
                        block.scalar(f_a)

                        def f_v(vector):
                            for hf in range(nhalves):
                                sts = list(range(hf * HF, min(J, (hf + 1) * HF)))
                                for hg in range(NH):
                                    for si, st in enumerate(sts):
                                        key = (hf, hg, si)
                                        vector.wait_ge(s_a.h, plan_mm[key]["a"])
                                        s_v.inc(vector.scalar_tensor_tensor(
                                            hbuf[:, si * HID + hg * 512:
                                                 si * HID + (hg + 1) * 512],
                                            psB3[:, :],
                                            sml[:, 40 + st:41 + st],
                                            tm1[:, :], ALU.mult, ALU.mult))
                                for si, st in enumerate(sts):
                                    q = plan_q[(hf, si)]
                                    nprev = hf * HF + si
                                    if nprev > 0:
                                        vector.wait_ge(s_st.h,
                                                       st0 + 32 * nprev)
                                    hrow = hbuf[:, si * HID:(si + 1) * HID]
                                    vector.drain()
                                    vector.tensor_reduce(
                                        sml[:, 48:49], hrow, AX.X, ALU.max,
                                        apply_absolute_value=True)
                                    vector.drain()
                                    vector.tensor_scalar(sml[:, 49:50],
                                                         sml[:, 48:49],
                                                         1e-5, None, ALU.max)
                                    vector.drain()
                                    vector.reciprocal(sml[:, 50:51],
                                                      sml[:, 49:50])
                                    vector.drain()
                                    vector.tensor_scalar(sml[:, 51:52],
                                                         sml[:, 50:51],
                                                         127.0, None, ALU.mult)
                                    vector.tensor_scalar(sml[:, 55:56],
                                                         sml[:, 49:50],
                                                         1.0 / 127.0, None,
                                                         ALU.mult)
                                    vector.drain()
                                    vector.tensor_scalar(sml[:, 56 + st:57 + st],
                                                         sml[:, 55:56],
                                                         sml[:, 54:55], None,
                                                         ALU.mult)
                                    li = None
                                    for hg2 in range(2 * NH):
                                        vector.tensor_scalar(
                                            hqf[:, :],
                                            hrow[:, hg2 * 256:(hg2 + 1) * 256],
                                            sml[:, 51:52], None, ALU.mult)
                                        vector.drain()
                                        vector.tensor_copy(hqi[:, :], hqf[:, :])
                                        vector.drain()
                                        li = vector.tensor_scalar(
                                            hq[:, hg2 * 256:(hg2 + 1) * 256],
                                            hqi[:, :], 127, -128,
                                            ALU.min, ALU.max)
                                    s_v.inc(li)
                                    for hk in range(NHK):
                                        if hk == NHK // 2:
                                            vector.wait_ge(s_st.h,
                                                           st0 + 32 * nprev + 16)
                                        vector.wait_ge(s_t.h, q["t_vt"][hk])
                                        s_v.inc(vector.tensor_copy(
                                            hstg[:, (hk % (NHK // 2)) * P:
                                                 (hk % (NHK // 2) + 1) * P],
                                            psHT[hk % 2][:, :]))
                        block.vector(f_v)

                        def f_st(gpsimd):
                            HK2 = NHK // 2
                            for hf in range(nhalves):
                                sts = list(range(hf * HF, min(J, (hf + 1) * HF)))
                                for si, st in enumerate(sts):
                                    q = plan_q[(hf, si)]
                                    for half in range(2):
                                        gpsimd.wait_ge(
                                            s_v.h,
                                            q["v_vt"][(half + 1) * HK2 - 1])
                                        s_st.inc(gpsimd.dma_start(
                                            out=hT_stash[:, st * NHK * P +
                                                         half * HK2 * P:
                                                         st * NHK * P +
                                                         (half + 1) * HK2 * P],
                                            in_=hstg[:, :]), 16)
                        block.gpsimd(f_st)
                        hstores = s_st.n

                with nc.sbuf_tensor(U("w2buf"), [P, 2 * NHK * 512], BF16) as w2buf, \
                     nc.sbuf_tensor(U("hq1"), [P, 2 * NHK * P], BF16) as hq1, \
                     nc.sbuf_tensor(U("tmp2b"), [P, 2 * 512], F32) as tmp2b, \
                     nc.psum_tensor(U("psW2"), [P, 2 * 512], F32) as psW2:
                    w0, t0, v0 = s_w.n, s_t.n, s_v.n
                    s_w2p = [sem("w2pa"), sem("w2pb")]
                    w2p0 = [s_w2p[0].n, s_w2p[1].n]
                    h2_tgt = [s_hp[i % 2].n + 16 * (i // 2 + 1)
                              for i in range(4 * J)]

                    def w2_w(scalar):
                        scalar.wait_ge(ag_sem["w2"].h, 1)
                        scalar.wait_ge(s_st.h, hstores)
                        for og in range(4):
                            po = (og % 2) * NHK * 512
                            if og >= 2:
                                scalar.wait_ge(s_t.h, t0 + (og - 1) * J)
                            for hk in range(NHK):
                                s_w2p[og % 2].inc(scalar.dma_start(
                                    out=w2buf[:, po + hk * 512:
                                              po + (hk + 1) * 512],
                                    in_=ag["w2"][1][og * 512:(og + 1) * 512,
                                                    hk * P:(hk + 1) * P],
                                    transpose=True), 16)
                    block.scalar(w2_w)

                    def w2_h(sync):
                        sync.wait_ge(s_st.h, hstores)
                        i = 0
                        for og in range(4):
                            for st in range(J):
                                if i >= 2:
                                    sync.wait_ge(s_t.h, t0 + i - 1)
                                s_hp[i % 2].inc(sync.dma_start(
                                    out=hq1[:, (i % 2) * NHK * P:
                                            (i % 2 + 1) * NHK * P],
                                    in_=hT_stash[:, st * NHK * P:
                                                 (st + 1) * NHK * P]), 16)
                                i += 1
                    block.sync(w2_h)

                    def w2_t(tensor):
                        i = 0
                        for og in range(4):
                            po = (og % 2) * NHK * 512
                            tensor.wait_ge(s_w2p[og % 2].h,
                                           w2p0[og % 2] + 16 * NHK * (og // 2 + 1))
                            for st in range(J):
                                pp = i % 2
                                tensor.wait_ge(s_hp[i % 2].h, h2_tgt[i])
                                if i >= 2:
                                    tensor.wait_ge(s_v.h, v0 + i - 1)
                                for hk in range(NHK):
                                    mm = tensor.matmul(
                                        psW2[:, pp * 512:(pp + 1) * 512],
                                        hq1[:, pp * NHK * P + hk * P:
                                            pp * NHK * P + (hk + 1) * P],
                                        w2buf[:, po + hk * 512:
                                              po + (hk + 1) * 512],
                                        start=(hk == 0), stop=(hk == NHK - 1))
                                s_t.inc(mm)
                                i += 1
                    block.tensor(w2_t)

                    def w2_v(vector):
                        vector.drain()
                        i = 0
                        for og in range(4):
                            for st in range(J):
                                pp = i % 2
                                vector.wait_ge(s_t.h, t0 + i + 1)
                                xsl = x_res[:, st * D + og * 512:
                                            st * D + (og + 1) * 512]
                                tsl = tmp2b[:, pp * 512:(pp + 1) * 512]
                                vector.scalar_tensor_tensor(
                                    tsl, psW2[:, pp * 512:(pp + 1) * 512],
                                    sml[:, 56 + st:57 + st], xsl,
                                    ALU.mult, ALU.add)
                                vector.drain()
                                s_v.inc(vector.tensor_copy(xsl, tsl))
                                i += 1
                    block.vector(w2_v)

        def ph_ffn1():
            NH = HID // 512
            NHK = HID // P
            HF = min(4, J)
            nhalves = (J + HF - 1) // HF
            s_x1 = sem("ffn1xn")
            with nc.sbuf_tensor(U("xnT1"), [P, 16 * L], BF16) as xnT1:
                def x1load(sync):
                    sync.wait_ge(bar, barn[0])
                    s_x1.inc(sync.dma_start(
                        out=xnT1[:, :], in_=xnT_stash[:, :]), 16)
                block.sync(x1load)

                with nc.sbuf_tensor(U("wbg"), [P, 2 * 16 * 512], BF16) as wbg, \
                     nc.sbuf_tensor(U("wbu"), [P, 16 * 512], BF16) as wbu, \
                     nc.sbuf_tensor(U("h1buf"), [P, HF * HID], BF16) as h1buf, \
                     nc.sbuf_tensor(U("tg1"), [P, 512], F32) as tg1, \
                     nc.sbuf_tensor(U("h1stg"), [P, (NHK // 2) * P], BF16) as h1stg, \
                     nc.psum_tensor(U("psG"), [P, 512], F32) as psG, \
                     nc.psum_tensor(U("psU"), [P, 512], F32) as psU, \
                     nc.psum_tensor(U("psHT1a"), [P, P], BF16) as psHT1a, \
                     nc.psum_tensor(U("psHT1b"), [P, P], BF16) as psHT1b:
                    psHT1 = [psHT1a, psHT1b]
                    w0, st0 = s_w.n, s_st.n
                    s_u1 = sem("gup")
                    spb = [s_wp[0].n, s_wp[1].n]
                    tt, aa, vv = s_t.n, s_a.n, s_v.n
                    plan_mm = {}
                    plan_q = {}
                    for hf in range(nhalves):
                        sts = list(range(hf * HF, min(J, (hf + 1) * HF)))
                        for hg in range(NH):
                            for si, st in enumerate(sts):
                                tt += 1
                                aa += 1
                                vv += 1
                                plan_mm[(hf, hg, si)] = {"t": tt, "a": aa, "v": vv}
                        for si, st in enumerate(sts):
                            q = {"t_vt": [], "v_vt": []}
                            for hk in range(NHK):
                                tt += 1
                                q["t_vt"].append(tt)
                                vv += 1
                                q["v_vt"].append(vv)
                            plan_q[(hf, si)] = q

                    groups_g = [(hf, hg) for hf in range(nhalves)
                                for hg in range(NH)]
                    u1_0 = s_u1.n

                    def g_w(scalar):
                        scalar.wait_ge(ag_sem["wgu"].h, 1)
                        scalar.wait_ge(s_x1.h, s_x1.n)
                        for g, (hf, hg) in enumerate(groups_g):
                            pg = (g % 2) * 16 * 512
                            if g >= 2:
                                phf, phg = groups_g[g - 2]
                                nsi = len(list(range(
                                    phf * HF, min(J, (phf + 1) * HF))))
                                scalar.wait_ge(
                                    s_t.h,
                                    plan_mm[(phf, phg, nsi - 1)]["t"])
                            for dk in range(16):
                                s_wp[g % 2].inc(scalar.dma_start(
                                    out=wbg[:, pg + dk * 512:pg + (dk + 1) * 512],
                                    in_=ag["wgu"][1][hg * 512:(hg + 1) * 512,
                                                     dk * P:(dk + 1) * P],
                                    transpose=True), 16)
                            if g >= 1:
                                phf, phg = groups_g[g - 1]
                                nsi = len(list(range(
                                    phf * HF, min(J, (phf + 1) * HF))))
                                scalar.wait_ge(
                                    s_t.h,
                                    plan_mm[(phf, phg, nsi - 1)]["t"])
                            for dk in range(16):
                                s_u1.inc(scalar.dma_start(
                                    out=wbu[:, dk * 512:(dk + 1) * 512],
                                    in_=ag["wgu"][1][HID + hg * 512:
                                                     HID + (hg + 1) * 512,
                                                     dk * P:(dk + 1) * P],
                                    transpose=True), 16)
                    block.sync(g_w)

                    def g_t(tensor):
                        tensor.wait_ge(s_x1.h, s_x1.n)
                        i = 0
                        for hf in range(nhalves):
                            sts = list(range(hf * HF, min(J, (hf + 1) * HF)))
                            for hg in range(NH):
                                pg = (i % 2) * 16 * 512
                                tensor.wait_ge(
                                    s_wp[i % 2].h,
                                    spb[i % 2] + 256 * (i // 2 + 1))
                                i += 1
                                tensor.wait_ge(s_u1.h, u1_0 + 256 * i)
                                for si, st in enumerate(sts):
                                    key = (hf, hg, si)
                                    if not (hf == 0 and hg == 0 and si == 0):
                                        tensor.wait_ge(s_v.h,
                                                       plan_mm[key]["v"] - 1)
                                    for dk in range(16):
                                        tensor.matmul(
                                            psG[:, :],
                                            xnT1[:, dk * L + st * P:
                                                 dk * L + (st + 1) * P],
                                            wbg[:, pg + dk * 512:
                                                pg + (dk + 1) * 512],
                                            start=(dk == 0), stop=(dk == 15))
                                    for dk in range(16):
                                        mm = tensor.matmul(
                                            psU[:, :],
                                            xnT1[:, dk * L + st * P:
                                                 dk * L + (st + 1) * P],
                                            wbu[:, dk * 512:(dk + 1) * 512],
                                            start=(dk == 0), stop=(dk == 15))
                                    s_t.inc(mm)
                            for si, st in enumerate(sts):
                                q = plan_q[(hf, si)]
                                tensor.wait_ge(s_v.h,
                                               plan_mm[(hf, NH - 1, si)]["v"])
                                for hk in range(NHK):
                                    if hk >= 2:
                                        tensor.wait_ge(s_v.h, q["v_vt"][hk - 2])
                                    elif si > 0:
                                        tensor.wait_ge(
                                            s_v.h,
                                            plan_q[(hf, si - 1)]["v_vt"][
                                                NHK - 2 + hk])
                                    s_t.inc(tensor.transpose(
                                        psHT1[hk % 2][:, :],
                                        h1buf[:, si * HID + hk * P:
                                              si * HID + (hk + 1) * P],
                                        id_s[:, :]))
                    block.tensor(g_t)

                    def g_a(scalar):
                        for hf in range(nhalves):
                            sts = list(range(hf * HF, min(J, (hf + 1) * HF)))
                            for hg in range(NH):
                                for si, st in enumerate(sts):
                                    key = (hf, hg, si)
                                    scalar.wait_ge(s_t.h, plan_mm[key]["t"])
                                    ai = scalar.activation(tg1[:, :], psG[:, :],
                                                           AF.Silu)
                                    s_a.inc(ai)
                    block.scalar(g_a)

                    def g_v(vector):
                        for hf in range(nhalves):
                            sts = list(range(hf * HF, min(J, (hf + 1) * HF)))
                            for hg in range(NH):
                                for si, st in enumerate(sts):
                                    key = (hf, hg, si)
                                    vector.wait_ge(s_a.h, plan_mm[key]["a"])
                                    s_v.inc(vector.scalar_tensor_tensor(
                                        h1buf[:, si * HID + hg * 512:
                                              si * HID + (hg + 1) * 512],
                                        psU[:, :], 1.0,
                                        tg1[:, :], ALU.mult, ALU.mult))
                            for si, st in enumerate(sts):
                                q = plan_q[(hf, si)]
                                nprev = hf * HF + si
                                if nprev > 0:
                                    vector.wait_ge(s_st.h, st0 + 32 * nprev)
                                for hk in range(NHK):
                                    if hk == NHK // 2:
                                        vector.wait_ge(s_st.h,
                                                       st0 + 32 * nprev + 16)
                                    vector.wait_ge(s_t.h, q["t_vt"][hk])
                                    s_v.inc(vector.tensor_copy(
                                        h1stg[:, (hk % (NHK // 2)) * P:
                                              (hk % (NHK // 2) + 1) * P],
                                        psHT1[hk % 2][:, :]))
                    block.vector(g_v)

                    def g_st(gpsimd):
                        HK2 = NHK // 2
                        for hf in range(nhalves):
                            sts = list(range(hf * HF, min(J, (hf + 1) * HF)))
                            for si, st in enumerate(sts):
                                q = plan_q[(hf, si)]
                                for half in range(2):
                                    gpsimd.wait_ge(
                                        s_v.h,
                                        q["v_vt"][(half + 1) * HK2 - 1])
                                    s_st.inc(gpsimd.dma_start(
                                        out=hT_stash[:, st * NHK * P +
                                                     half * HK2 * P:
                                                     st * NHK * P +
                                                     (half + 1) * HK2 * P],
                                        in_=h1stg[:, :]), 16)
                    block.gpsimd(g_st)
                    h1stores = s_st.n

            with nc.sbuf_tensor(U("wdnb"), [P, (HID // P) * 512], BF16) as wdnb, \
                 nc.sbuf_tensor(U("hq2"), [P, 2 * (HID // P) * P], BF16) as hq2, \
                 nc.sbuf_tensor(U("y_all"), [P, J * D], BF16) as y_all, \
                 nc.sbuf_tensor(U("rscr"), [P, D], F32) as rscr, \
                 nc.sbuf_tensor(U("tmp3"), [P, 2 * 512], F32) as tmp3, \
                 nc.psum_tensor(U("psDN"), [P, 2 * 512], F32) as psDN:
                NHK = HID // P
                w0, t0, v0, a0 = s_w.n, s_t.n, s_v.n, s_a.n
                dn_tgt = [s_hp[i % 2].n + 16 * (i // 2 + 1)
                          for i in range(4 * J)]

                def dn_w(scalar):
                    scalar.wait_ge(ag_sem["wdn"].h, 1)
                    scalar.wait_ge(s_st.h, h1stores)
                    for og in range(4):
                        if og >= 1:
                            scalar.wait_ge(s_t.h, t0 + og * J)
                        for hk in range(NHK):
                            s_w.inc(scalar.dma_start(
                                out=wdnb[:, hk * 512:(hk + 1) * 512],
                                in_=ag["wdn"][1][og * 512:(og + 1) * 512,
                                                 hk * P:(hk + 1) * P],
                                transpose=True), 16)
                block.scalar(dn_w)

                def dn_h(sync):
                    sync.wait_ge(s_st.h, h1stores)
                    i = 0
                    for og in range(4):
                        for st in range(J):
                            if i >= 2:
                                sync.wait_ge(s_t.h, t0 + i - 1)
                            s_hp[i % 2].inc(sync.dma_start(
                                out=hq2[:, (i % 2) * NHK * P:
                                        (i % 2 + 1) * NHK * P],
                                in_=hT_stash[:, st * NHK * P:
                                             (st + 1) * NHK * P]), 16)
                            i += 1
                block.sync(dn_h)

                def dn_t(tensor):
                    i = 0
                    for og in range(4):
                        tensor.wait_ge(s_w.h, w0 + 16 * NHK * (og + 1))
                        for st in range(J):
                            pp = i % 2
                            tensor.wait_ge(s_hp[i % 2].h, dn_tgt[i])
                            if i >= 2:
                                tensor.wait_ge(s_v.h, v0 + i - 1)
                            for hk in range(NHK):
                                mm = tensor.matmul(
                                    psDN[:, pp * 512:(pp + 1) * 512],
                                    hq2[:, pp * NHK * P + hk * P:
                                        pp * NHK * P + (hk + 1) * P],
                                    wdnb[:, hk * 512:(hk + 1) * 512],
                                    start=(hk == 0), stop=(hk == NHK - 1))
                            s_t.inc(mm)
                            i += 1
                block.tensor(dn_t)

                def dn_v(vector):
                    i = 0
                    for og in range(4):
                        for st in range(J):
                            pp = i % 2
                            vector.wait_ge(s_t.h, t0 + i + 1)
                            s_v.inc(vector.tensor_copy(
                                y_all[:, st * D + og * 512:st * D + (og + 1) * 512],
                                psDN[:, pp * 512:(pp + 1) * 512]))
                            i += 1
                block.vector(dn_v)
                yall_done_v = s_v.n

                def rms_a(scalar):
                    scalar.wait_ge(s_v.h, yall_done_v)
                    for st in range(J):
                        s_a.inc(scalar.activation(
                            rscr[:, :], y_all[:, st * D:(st + 1) * D], AF.Square,
                            accum_out=sml[:, st:st + 1]))
                        scalar.drain()
                block.scalar(rms_a)

                def rms_v(vector):
                    vector.wait_ge(s_a.h, a0 + J)
                    s_v.inc(vector.tensor_scalar(sml[:, 8:8 + J], sml[:, 0:J],
                                                 1.0 / D, 1e-6, ALU.mult, ALU.add))
                block.vector(rms_v)

                def rms_a2(scalar):
                    scalar.wait_ge(s_v.h, s_v.n)
                    s_a.inc(scalar.sqrt(sml[:, 16:16 + J], sml[:, 8:8 + J]))
                block.scalar(rms_a2)

                def rms_v2(vector):
                    vector.wait_ge(s_a.h, s_a.n)
                    vector.reciprocal(sml[:, 24:24 + J], sml[:, 16:16 + J])
                    vector.drain()
                    ii = 0
                    for st in range(J):
                        for og in range(4):
                            xsl = x_res[:, st * D + og * 512:
                                        st * D + (og + 1) * 512]
                            t3 = tmp3[:, (ii % 2) * 512:(ii % 2 + 1) * 512]
                            vector.scalar_tensor_tensor(
                                t3,
                                y_all[:, st * D + og * 512:st * D + (og + 1) * 512],
                                sml[:, 24 + st:25 + st], xsl, ALU.mult, ALU.add)
                            vector.drain()
                            li = vector.tensor_copy(xsl, t3)
                            ii += 1
                    s_v.inc(li)
                block.vector(rms_v2)

        if tap is None or not tap.startswith("prep"):
            for l in range(depth):
                with nc.sbuf_tensor(f"xn{l}", [P, J * D], BF16) as xn, \
                     nc.sbuf_tensor(f"lnscr{l}", [P, D], F32) as lnscr:
                    xn_ready = ph_ln(l, xn, lnscr)
                    with nc.sbuf_tensor(f"xnT{l}", [P, 16 * L], BF16) as xnT:
                        xnT_ready = ph_transpose(xn, xnT, J, (s_v, xn_ready))
                        ph_qkv(l, xn, xnT, xnT_ready)
                barrier()
                if l == 0:
                    ph_attn(0)
                    if tapret["stop"]:
                        return nc
                    barrier()
                    if tap != "L0":
                        ph_ffn0()
                        if tapret["stop"]:
                            return nc
                        barrier()
                else:
                    ph_ffn1()
                    barrier()
                    ph_attn(1)
                    if tapret["stop"]:
                        return nc
                    barrier()
                if tap == "L0" and l == 0:
                    break
                if tap == "L0a" and l == 0:
                    break

        def ph_out(sync):
            sync.wait_ge(bar, barn[0])
            dst = y_out.ap().rearrange("(t p) d -> p t d", p=P)
            s_o.inc(sync.dma_start(
                out=dst,
                in_=x_res.ap().rearrange("p (t d) -> p t d", d=D)), 16)
            sync.wait_ge(s_o.h, s_o.n)
        block.sync(ph_out)

    return nc



def make_inputs(inputs, J=8):
    L = J * P
    x = np.asarray(inputs["L0_attn_x"])[0]
    in_maps = []
    for c in range(C):
        tiles = [x[(c + 8 * j) * P:(c + 8 * j + 1) * P] for j in range(J)]
        xs = np.ascontiguousarray(np.concatenate(tiles, axis=0))
        pos = np.concatenate([np.arange((c + 8 * j) * P, (c + 8 * j + 1) * P)
                              for j in range(J)])
        sin2, cos2 = _rope_tables(pos)
        m = {
            "x_in": xs.astype(np.float32),
            "sin_in": sin2, "cos_in": cos2,
            "mask_in": _masks(c),
            "wq0_s": np.ascontiguousarray(inputs["L0_attn_w_project"][c * (3 * D // C):(c + 1) * (3 * D // C)]),
            "wo0_s": np.ascontiguousarray(inputs["L0_attn_w_out"][c * (D // C):(c + 1) * (D // C)]),
            "w1_s": np.ascontiguousarray(inputs["L0_ffn_w1"][c * (HID // C):(c + 1) * (HID // C)]),
            "w3_s": np.ascontiguousarray(inputs["L0_ffn_w3"][c * (HID // C):(c + 1) * (HID // C)]),
            "w2_s": np.ascontiguousarray(inputs["L0_ffn_w2"][c * (D // C):(c + 1) * (D // C)]),
            "wq1_s": np.ascontiguousarray(inputs["L1_attn_Wqkv"][c * (3 * D // C):(c + 1) * (3 * D // C)]),
            "wo1_s": np.ascontiguousarray(inputs["L1_attn_WO"][c * (D // C):(c + 1) * (D // C)]),
            "gup_s": np.ascontiguousarray(inputs["L1_ffn_gate_up_packed"][c * (GU // C):(c + 1) * (GU // C)]),
            "gus_s": np.ascontiguousarray(np.asarray(inputs["L1_ffn_gate_up_scales"])[:, c * (GU // C):(c + 1) * (GU // C)]),
            "dnp_s": np.ascontiguousarray(inputs["L1_ffn_down_packed"][c * (D // C):(c + 1) * (D // C)]),
            "dns_s": np.ascontiguousarray(np.asarray(inputs["L1_ffn_down_scales"])[:, c * (D // C):(c + 1) * (D // C)]),
        }
        in_maps.append(m)
    return in_maps


def assemble(results, J=8):
    S = C * J * P
    out = np.empty((1, S, D), dtype=np.float32)
    for c in range(C):
        y = results[c]["y_out"]
        for j in range(J):
            out[0, (c + 8 * j) * P:(c + 8 * j + 1) * P] = y[j * P:(j + 1) * P]
    return out


def kernel(**inputs):
    nc = build(J=8, depth=2)
    in_maps = make_inputs(inputs, J=8)
    res = run_bass_kernel_spmd(nc, in_maps, core_ids=list(range(C)))
    return assemble(res.results, J=8)


# version 11 (final)
# speedup vs baseline: 1.6089x; 1.0019x over previous
import math
from contextlib import ExitStack

import numpy as np
import ml_dtypes
import concourse.bass as bass
import concourse.mybir as mybir
from concourse.bass_isa import ReduceOp
from concourse.bass_utils import run_bass_kernel_spmd

F32 = mybir.dt.float32
BF16 = mybir.dt.bfloat16
F16 = mybir.dt.float16
I32 = mybir.dt.int32
AF = mybir.ActivationFunctionType
ALU = mybir.AluOpType
AX = mybir.AxisListType

C = 8
D = 2048
H = 16
DH = 128
HID = 5632
GU = 2 * HID
P = 128
LN2 = math.log(2.0)
C1 = math.log2(3.0) - 1.5


class Sem:
    def __init__(self, nc, name):
        self.h = nc.alloc_semaphore(name)
        self.n = 0

    def inc(self, instr, k=1):
        instr.then_inc(self.h, k)
        self.n += k
        return instr



def _rope_tables(pos):
    i = np.arange(0, DH, 2)
    freqs = np.exp(-math.log(10000.0) * i / DH).astype(np.float32)
    ang = (freqs[:, None] * pos.astype(np.float32)[None, :]).astype(np.float32)
    ang64 = ang.astype(np.float64)
    sin2 = np.repeat(np.sin(ang64), 2, axis=0).astype(np.float32)
    cos2 = np.repeat(np.cos(ang64), 2, axis=0).astype(np.float32)
    return np.ascontiguousarray(sin2), np.ascontiguousarray(cos2)


def _masks(c):
    out = np.zeros((8, P, P), dtype=np.float32)
    kl = np.arange(P)[:, None]
    ql = np.arange(P)[None, :]
    for m in range(8):
        if m < c:
            out[m] = 1.0
        elif m == c:
            out[m] = (kl <= ql).astype(np.float32)
    return out.astype(ml_dtypes.bfloat16)


def _rotmat():
    M = np.zeros((DH, DH), dtype=np.float32)
    for i in range(0, DH, 2):
        M[i, i + 1] = -1.0
        M[i + 1, i] = 1.0
    return np.ascontiguousarray(M.T).astype(ml_dtypes.bfloat16)



def build(J=8, depth=2, tap=None):
    nc = bass.Bass("TRN2", target_bir_lowering=False)
    L = J * P
    NKC = 8 * J

    sems = {}
    _uid = [0]

    def U(name):
        _uid[0] += 1
        return f"{name}_{_uid[0]}"

    def sem(name):
        if name not in sems:
            sems[name] = Sem(nc, name)
        return sems[name]

    x_in = nc.dram_tensor("x_in", [L, D], F32, kind="ExternalInput")
    sin_in = nc.dram_tensor("sin_in", [DH, L], F32, kind="ExternalInput")
    cos_in = nc.dram_tensor("cos_in", [DH, L], F32, kind="ExternalInput")
    mask_in = nc.dram_tensor("mask_in", [8, P, P], BF16, kind="ExternalInput")
    wq0_s = nc.dram_tensor("wq0_s", [3 * D // C, D], F32, kind="ExternalInput")
    wo0_s = nc.dram_tensor("wo0_s", [D // C, D], F32, kind="ExternalInput")
    w1_s = nc.dram_tensor("w1_s", [HID // C, D], F32, kind="ExternalInput")
    w3_s = nc.dram_tensor("w3_s", [HID // C, D], F32, kind="ExternalInput")
    w2_s = nc.dram_tensor("w2_s", [D // C, HID], F32, kind="ExternalInput")
    wq1_s = nc.dram_tensor("wq1_s", [3 * D // C, D], F32, kind="ExternalInput")
    wo1_s = nc.dram_tensor("wo1_s", [D // C, D], F32, kind="ExternalInput")
    gup_s = nc.dram_tensor("gup_s", [GU // C, D // 8], I32, kind="ExternalInput")
    gus_s = nc.dram_tensor("gus_s", [D // P, GU // C], F16, kind="ExternalInput")
    dnp_s = nc.dram_tensor("dnp_s", [D // C, HID // 8], I32, kind="ExternalInput")
    dns_s = nc.dram_tensor("dns_s", [HID // P, D // C], F16, kind="ExternalInput")
    y_out = nc.dram_tensor("y_out", [L, D], F32, kind="ExternalOutput")

    ag = {}
    ag_shapes = (
        ("wq0", 3 * D // C, D), ("wo0", D // C, D), ("w1", HID // C, D),
        ("w3", HID // C, D), ("w2", D // C, HID), ("wq1", 3 * D // C, D),
        ("wo1", D // C, D), ("wgu", GU // C, D), ("wdn", D // C, HID))
    for name, rows, cols in ag_shapes:
        i = nc.dram_tensor(f"agi_{name}", [rows, cols], BF16, kind="Internal")
        o = nc.dram_tensor(f"ago_{name}", [C * rows, cols], BF16, kind="Internal")
        ag[name] = (i, o)

    tsum_i = nc.dram_tensor("tsum_i", [1, 4], F32, kind="Internal")
    tsum_o = nc.dram_tensor("tsum_o", [1, 4], F32, kind="Internal")
    psum_scr = nc.dram_tensor("psum_scr", [P, 4], F32, kind="Internal")
    rep_scr = nc.dram_tensor("rep_scr", [P, 4], F32, kind="Internal")

    kvt = []
    for l in range(depth):
        ki = nc.dram_tensor(f"kvKi{l}", [D, L], BF16, kind="Internal")
        ko = nc.dram_tensor(f"kvKo{l}", [C, D, L], BF16, kind="Internal")
        vi = nc.dram_tensor(f"kvVi{l}", [16, P, L], BF16, kind="Internal")
        vo = nc.dram_tensor(f"kvVo{l}", [C, 16, P, L], BF16, kind="Internal")
        kvt.append((ki, ko, vi, vo))

    q_stash = nc.dram_tensor("q_stash", [P, 16 * L], BF16, kind="Internal")
    xn_stash = nc.dram_tensor("xn_stash", [L, D], BF16, kind="Internal")
    xnT_stash = nc.dram_tensor("xnT_stash", [P, 16 * L], BF16, kind="Internal")
    hT_stash = nc.dram_tensor("hT_stash", [P, (HID // P) * L], BF16, kind="Internal")

    ident_c = nc.inline_tensor(np.eye(P, dtype=ml_dtypes.bfloat16), "ident_c")
    onec_c = nc.inline_tensor(np.ones((P, 1), dtype=ml_dtypes.bfloat16), "onec_c")
    oner_c = nc.inline_tensor(np.ones((1, P), dtype=ml_dtypes.bfloat16), "oner_c")
    onerf_c = nc.inline_tensor(np.ones((1, P), dtype=np.float32), "onerf_c")
    rot_cst = nc.inline_tensor(_rotmat(), "rot_cst")

    with ExitStack() as ctx:
        e = ctx.enter_context
        block = e(nc.Block())

        x_res = e(nc.sbuf_tensor("x_res", [P, J * D], F32))
        id_s = e(nc.sbuf_tensor("id_s", [P, P], BF16))
        rot_s = e(nc.sbuf_tensor("rot_s", [P, P], BF16))
        onec_s = e(nc.sbuf_tensor("onec_s", [P, 1], BF16))
        oner_s = e(nc.sbuf_tensor("oner_s", [1, P], BF16))
        onerf_s = e(nc.sbuf_tensor("onerf_s", [1, P], F32))
        sml = e(nc.sbuf_tensor("sml", [P, 64], F32))

        s_ld = sem("ld")
        s_g = sem("gd")
        s_t = sem("t")
        s_v = sem("v")
        s_a = sem("a")
        s_w = sem("w")
        s_kv = sem("kvd")
        s_st = sem("st")
        s_wp = [sem("wpa"), sem("wpb")]
        s_sp = [sem("spa"), sem("spb")]
        s_hp = [sem("hpa"), sem("hpb")]
        s_o = sem("o")
        bar = nc.alloc_semaphore("bar")
        barn = [0]

        def barrier():
            barn[0] += 5
            tgt = barn[0]
            for name in ("sync", "scalar", "tensor", "vector", "gpsimd"):
                def f(eng, tgt=tgt, name=name):
                    if name in ("scalar", "vector"):
                        eng.drain()
                    eng.sem_inc(bar, 1)
                    eng.wait_ge(bar, tgt)
                getattr(block, name)(f)

        def tap_out(src_ap, ncols, wait_sems=()):
            nt = ncols // D
            def f(gpsimd):
                for ss, vv in wait_sems:
                    gpsimd.wait_ge(ss.h, vv)
                dst = y_out.ap().rearrange("(t p) d -> p t d", p=P)
                s_o.inc(gpsimd.dma_start(
                    out=dst[:, 0:nt, :],
                    in_=src_ap.rearrange("p (t d) -> p t d", d=D)), 16)
                gpsimd.wait_ge(s_o.h, s_o.n)
            block.gpsimd(f)

        s_c0 = sem("ldc")

        def ph0(sync):
            xr = x_in.ap().rearrange("(t p) d -> p t d", p=P)
            s_c0.inc(sync.dma_start(
                out=x_res.ap().rearrange("p (t d) -> p t d", d=D), in_=xr), 16)
            s_c0.inc(sync.dma_start(out=id_s[:, :], in_=ident_c[:, :]), 16)
            s_c0.inc(sync.dma_start(out=rot_s[:, :], in_=rot_cst[:, :]), 16)
            s_c0.inc(sync.dma_start(out=onec_s[:, :], in_=onec_c[:, :]), 16)
            s_c0.inc(sync.dma_start(out=oner_s[:, :], in_=oner_c[:, :]), 16)
            s_c0.inc(sync.dma_start(out=onerf_s[:, :], in_=onerf_c[:, :]), 16)
        block.sync(ph0)
        consts_done = s_c0.n

        ag_sem = {name: sem(f"ag_{name}") for name, _, _ in ag_shapes}
        ar_sem = sem("ar_tern")

        def ph1a(gpsimd):
            g0 = s_g.n
            for name, src in (("wq0", wq0_s), ("wo0", wo0_s),
                              ("wq1", wq1_s), ("wo1", wo1_s)):
                s_g.inc(gpsimd.dma_start(out=ag[name][0][:, :], in_=src[:, :]), 16)
            gpsimd.wait_ge(s_g.h, s_g.n)
            for name in ("wq0", "wo0", "wq1", "wo1"):
                ag_sem[name].inc(gpsimd.collective_compute(
                    "AllGather", ALU.bypass, replica_groups=[list(range(C))],
                    ins=[ag[name][0].ap().opt()], outs=[ag[name][1].ap().opt()]))
        block.gpsimd(ph1a)

        with nc.sbuf_tensor("pL", [P, HID], F32) as pL, \
             nc.sbuf_tensor("pI", [P, HID], I32) as pI, \
             nc.sbuf_tensor("pOT", [P, HID], BF16) as pOT, \
             nc.sbuf_tensor("pF", [P, 6 * 2048], F32) as pF, \
             nc.sbuf_tensor("pOb", [P, 2 * 2048], BF16) as pOb, \
             nc.sbuf_tensor("sc16", [64, P], F16) as sc16, \
             nc.sbuf_tensor("sc16b", [64, P], BF16) as sc16b, \
             nc.sbuf_tensor("scT_all", [P, 264], F32) as scT_all, \
             nc.sbuf_tensor("pSt", [4, P], F32) as pSt, \
             nc.sbuf_tensor("pSr", [4, 1], F32) as pSr, \
             nc.psum_tensor("psT", [P, 64], BF16) as psT:

            Fb = [pF[:, i * 2048:(i + 1) * 2048] for i in range(6)]
            Ob = [pOb[:, i * 2048:(i + 1) * 2048] for i in range(2)]

            tern = [("w1", w1_s, HID // C, D), ("w3", w3_s, HID // C, D),
                    ("w2", w2_s, D // C, HID)]
            tiles_t = []
            for mi, (name, src, rows, cols) in enumerate(tern):
                r0 = 0
                while r0 < rows:
                    nr = min(P, rows - r0)
                    tiles_t.append((mi, name, src, r0, nr, cols))
                    r0 += nr
            nts = [sum(1 for tt in tiles_t if tt[0] == mi) for mi in range(3)]
            offs = [0, nts[0], nts[0] + nts[1], nts[0] + nts[1] + nts[2]]

            s_lt1 = sem("lt1")
            v0 = s_v.n

            def t_loads(sync):
                for k, (mi, name, src, r0, nr, cols) in enumerate(tiles_t):
                    if k >= 1:
                        sync.wait_ge(s_v.h, v0 + k + 1)
                    s_lt1.inc(sync.dma_start(out=pL[0:nr, 0:cols],
                                             in_=src[r0:r0 + nr, :]), 16)

            def t_abs(vector):
                vector.memset(sml[:, 0:len(tiles_t)], 0.0)
                vector.drain()
                s_v.inc(vector.engine_nop())
                for k, (mi, name, src, r0, nr, cols) in enumerate(tiles_t):
                    vector.wait_ge(s_lt1.h, 16 * (k + 1))
                    s_v.inc(vector.tensor_reduce(
                        sml[0:nr, k:k + 1], pL[0:nr, 0:cols], AX.X, ALU.add,
                        apply_absolute_value=True))
                vector.drain()
                for mi in range(3):
                    vector.tensor_reduce(sml[:, 32 + mi:33 + mi],
                                         sml[:, offs[mi]:offs[mi + 1]],
                                         AX.X, ALU.add)
                s_v.inc(vector.memset(sml[:, 35:36], 0.0))
            block.sync(t_loads)
            block.vector(t_abs)
            absums_done = s_v.n

            s_vr = sem("vred")

            def t_pred1(gpsimd):
                gpsimd.wait_ge(s_v.h, absums_done)
                s_g.inc(gpsimd.dma_start(out=psum_scr[:, :], in_=sml[:, 32:36]), 16)
                gpsimd.wait_ge(s_g.h, s_g.n)
                s_g.inc(gpsimd.dma_start(
                    out=pSt[:, :], in_=psum_scr.ap().rearrange("p f -> f p"),
                    allow_slow_non_contiguous=True), 16)
                gpsimd.wait_ge(s_g.h, s_g.n)
            block.gpsimd(t_pred1)
            g_tr = s_g.n

            def t_red(vector):
                vector.wait_ge(s_g.h, g_tr)
                s_vr.inc(vector.tensor_reduce(pSr[:, :], pSt[:, :], AX.X, ALU.add))
            block.vector(t_red)

            def t_pred2(gpsimd):
                gpsimd.wait_ge(s_vr.h, s_vr.n)
                s_g.inc(gpsimd.dma_start(
                    out=tsum_i.ap()[0:1, 0:4].rearrange("o f -> f o"),
                    in_=pSr[:, 0:1]), 16)
                gpsimd.wait_ge(s_g.h, s_g.n)
                ar_sem.inc(gpsimd.collective_compute(
                    "AllReduce", ALU.add, replica_groups=[list(range(C))],
                    ins=[tsum_i.ap().opt()], outs=[tsum_o.ap().opt()]))
                gpsimd.wait_ge(ar_sem.h, ar_sem.n)
                repf = rep_scr.ap().rearrange("p f -> (p f)")
                s_g.inc(gpsimd.dma_start(out=repf[0:4], in_=tsum_o.ap().rearrange("o f -> (o f)")), 16)
                gpsimd.wait_ge(s_g.h, s_g.n)
                n = 4
                while n < 512:
                    s_g.inc(gpsimd.dma_start(out=repf[n:2 * n], in_=repf[0:n]), 16)
                    gpsimd.wait_ge(s_g.h, s_g.n)
                    n *= 2
                s_g.inc(gpsimd.dma_start(out=sml[:, 48:52], in_=rep_scr[:, :]), 16)
                gpsimd.wait_ge(s_g.h, s_g.n)
                s_g.inc(gpsimd.dma_start(out=sml[0:1, 44:48], in_=tsum_o[:, :]), 16)
                gpsimd.wait_ge(s_g.h, s_g.n)
            block.gpsimd(t_pred2)
            pred_done = s_g.n

            def t_scale(vector):
                vector.wait_ge(s_g.h, pred_done)
                for mi, (name, src, rows, cols) in enumerate(tern):
                    n_el = float(rows * C * cols)
                    vector.tensor_scalar(sml[:, 52 + mi:53 + mi],
                                         sml[:, 48 + mi:49 + mi],
                                         1.0 / n_el, 1e-5, ALU.mult, ALU.max)
                vector.drain()
                s_v.inc(vector.reciprocal(sml[:, 56:59], sml[:, 52:55]))
            block.vector(t_scale)
            scale_ready = s_v.n

            s_lt2 = sem("lt2")
            s_gs = sem("gs")
            v1 = s_v.n

            def t2_loads(sync):
                sync.wait_ge(s_v.h, scale_ready)
                for k, (mi, name, src, r0, nr, cols) in enumerate(tiles_t):
                    if k >= 1:
                        sync.wait_ge(s_v.h, v1 + k)
                    s_lt2.inc(sync.dma_start(out=pL[0:nr, 0:cols],
                                             in_=src[r0:r0 + nr, :]), 16)

            def t2_q(vector):
                for k, (mi, name, src, r0, nr, cols) in enumerate(tiles_t):
                    vector.wait_ge(s_lt2.h, 16 * (k + 1))
                    if k >= 1:
                        vector.wait_ge(s_gs.h, 16 * k)
                    li = None
                    for c0 in range(0, cols, 2048):
                        cw = min(2048, cols - c0)
                        vector.tensor_scalar(Fb[0][0:nr, 0:cw], pL[0:nr, c0:c0 + cw],
                                             sml[0:nr, 56 + mi:57 + mi], None,
                                             ALU.mult)
                        vector.drain()
                        vector.tensor_copy(pI[0:nr, c0:c0 + cw], Fb[0][0:nr, 0:cw])
                        vector.drain()
                        li = vector.tensor_scalar(pOT[0:nr, c0:c0 + cw], pI[0:nr, c0:c0 + cw],
                                             1, -1, ALU.min, ALU.max)
                    s_v.inc(li)

            def t2_st(gpsimd):
                for k, (mi, name, src, r0, nr, cols) in enumerate(tiles_t):
                    gpsimd.wait_ge(s_v.h, v1 + k + 1)
                    s_gs.inc(gpsimd.dma_start(
                        out=ag[name][0][r0:r0 + nr, :],
                        in_=pOT[0:nr, 0:cols]), 16)
            block.sync(t2_loads)
            block.vector(t2_q)
            block.gpsimd(t2_st)
            tern_stores = s_gs.n

            def t_ag(gpsimd):
                gpsimd.wait_ge(s_gs.h, tern_stores)
                for name in ("w1", "w3", "w2"):
                    ag_sem[name].inc(gpsimd.collective_compute(
                        "AllGather", ALU.bypass, replica_groups=[list(range(C))],
                        ins=[ag[name][0].ap().opt()], outs=[ag[name][1].ap().opt()]))
            block.gpsimd(t_ag)

            fp4 = [("wgu", gup_s, gus_s, GU // C, D),
                   ("wdn", dnp_s, dns_s, D // C, HID)]
            tiles_f = [(name, pk, sc, t, cols)
                       for name, pk, sc, rows, cols in fp4
                       for t in range(rows // P)]
            scT_off = []
            off = 0
            for name, pk, sc, t, cols in tiles_f:
                scT_off.append(off)
                off += cols // P
            chunks_per = [(cols + 2047) // 2048 for _, _, _, _, cols in tiles_f]

            s_lf = sem("lf")
            s_gf = sem("gf")
            t2c = s_t.n
            a2 = s_a.n

            v_base = s_v.n
            v_cvt, v_f, v_chunk_done, v_tile_done = [], [], [], []
            vv = v_base
            ci_total = 0
            for k in range(len(tiles_f)):
                vv += 1
                v_cvt.append(vv)
                vv += 1
                fs, ds = [], []
                for c in range(chunks_per[k]):
                    vv += 1
                    fs.append(vv)
                    vv += 1
                    ds.append(vv)
                v_f.append(fs)
                v_chunk_done.append(ds)
                v_tile_done.append(vv)
                ci_total += chunks_per[k]

            def f_loads(sync):
                sync.wait_ge(s_v.h, v1 + len(tiles_t))
                for k, (name, pk, sc, t, cols) in enumerate(tiles_f):
                    if k >= 1:
                        sync.wait_ge(s_gf.h, 16 * k)
                    s_lf.inc(sync.dma_start(
                        out=pI[:, 0:cols // 8],
                        in_=pk[t * P:(t + 1) * P, :]), 16)
                    s_lf.inc(sync.dma_start(
                        out=sc16[0:cols // P, :],
                        in_=sc[:, t * P:(t + 1) * P]), 16)
            block.sync(f_loads)

            def f_sct(tensor):
                tensor.wait_ge(s_c0.h, consts_done)
                for k, (name, pk, sc, t, cols) in enumerate(tiles_f):
                    ng = cols // P
                    tensor.wait_ge(s_v.h, v_cvt[k])
                    s_t.inc(tensor.transpose(psT[:, 0:ng], sc16b[0:ng, :],
                                             id_s[0:ng, 0:ng]))
            block.tensor(f_sct)

            def f_dec_v(vector):
                vector.wait_ge(s_gs.h, tern_stores)
                for k, (name, pk, sc, t, cols) in enumerate(tiles_f):
                    ng = cols // P
                    vector.wait_ge(s_lf.h, 32 * (k + 1))
                    if k >= 1:
                        vector.wait_ge(s_gf.h, 16 * k)
                    s_v.inc(vector.tensor_copy(sc16b[0:ng, :], sc16[0:ng, :]))
                    vector.wait_ge(s_t.h, t2c + k + 1)
                    s_v.inc(vector.tensor_copy(
                        scT_all[:, scT_off[k]:scT_off[k] + ng], psT[:, 0:ng]))
                    for cidx, c0 in enumerate(range(0, cols, 2048)):
                        cw = min(2048, cols - c0)
                        cw8 = cw // 8
                        i0 = c0 // 8
                        ia, ib = 2048, 2048 + 1024
                        for nk in range(8):
                            vector.tensor_scalar(
                                pI[:, ia:ia + cw8], pI[:, i0:i0 + cw8],
                                4 * nk, 15, ALU.logical_shift_right,
                                ALU.bitwise_and)
                            vector.drain()
                            vector.tensor_copy(Fb[0][:, nk:cw:8],
                                               pI[:, ia:ia + cw8])
                            vector.tensor_scalar(
                                pI[:, ib:ib + cw8], pI[:, ia:ia + cw8],
                                1, None, ALU.bitwise_and)
                            vector.drain()
                            vector.tensor_copy(Fb[1][:, nk:cw:8],
                                               pI[:, ib:ib + cw8])
                        An, Bm = Fb[0][:, 0:cw], Fb[1][:, 0:cw]
                        F2, F3, F4, F5 = (Fb[2][:, 0:cw], Fb[3][:, 0:cw],
                                          Fb[4][:, 0:cw], Fb[5][:, 0:cw])
                        Sg = Ob[0][:, 0:cw]
                        vector.tensor_scalar(F2, An, 8.0, None, ALU.is_ge)
                        vector.drain()
                        vector.tensor_scalar(Sg, F2, -2.0, 1.0, ALU.mult, ALU.add)
                        vector.tensor_scalar(F3, An, 0.5, -1.0, ALU.mult, ALU.add)
                        vector.drain()
                        vector.scalar_tensor_tensor(F4, F2, -4.0, F3,
                                                    ALU.mult, ALU.add)
                        vector.drain()
                        vector.scalar_tensor_tensor(F5, Bm, C1, F4,
                                                    ALU.mult, ALU.add)
                        vector.tensor_scalar(F2, F4, -0.25, None, ALU.is_le)
                        vector.tensor_scalar(F3, F4, 396.83008, 197.83008,
                                             ALU.mult, ALU.add)
                        vector.drain()
                        vector.tensor_tensor(F4, F2, F3, ALU.mult)
                        vector.drain()
                        s_v.inc(vector.tensor_tensor(F2, F5, F4, ALU.add))
                        vector.wait_ge(s_a.h, a2 + sum(chunks_per[:k]) + cidx + 1)
                        vector.tensor_tensor(Ob[1][:, 0:cw], F3, Sg, ALU.mult)
                        vector.drain()
                        ng0 = scT_off[k] + c0 // P
                        li = None
                        for g in range(cw // P):
                            li = vector.tensor_scalar(
                                pOT[:, c0 + g * P:c0 + (g + 1) * P],
                                Ob[1][:, g * P:(g + 1) * P],
                                scT_all[:, ng0 + g:ng0 + g + 1], None, ALU.mult)
                        s_v.inc(li)
            block.vector(f_dec_v)

            def f_dec_a(scalar):
                for k, (name, pk, sc, t, cols) in enumerate(tiles_f):
                    for cidx, c0 in enumerate(range(0, cols, 2048)):
                        cw = min(2048, cols - c0)
                        scalar.wait_ge(s_v.h, v_f[k][cidx])
                        s_a.inc(scalar.activation(Fb[3][:, 0:cw], Fb[2][:, 0:cw],
                                                  AF.Exp, scale=LN2))
            block.scalar(f_dec_a)

            def f_dec_st(gpsimd):
                for k, (name, pk, sc, t, cols) in enumerate(tiles_f):
                    gpsimd.wait_ge(s_v.h, v_tile_done[k])
                    s_gf.inc(gpsimd.dma_start(
                        out=ag[name][0][t * P:(t + 1) * P, :],
                        in_=pOT[:, 0:cols]), 16)
            block.gpsimd(f_dec_st)
            fp4_stores = s_gf.n

            def f_ag(gpsimd):
                gpsimd.wait_ge(s_gf.h, fp4_stores)
                for name in ("wgu", "wdn"):
                    ag_sem[name].inc(gpsimd.collective_compute(
                        "AllGather", ALU.bypass, replica_groups=[list(range(C))],
                        ins=[ag[name][0].ap().opt()], outs=[ag[name][1].ap().opt()]))
            block.gpsimd(f_ag)

            if tap is not None and tap.startswith("prep"):
                wname = tap.split(":")[1] if ":" in tap else "w1"
                wcols = dict((n, c) for n, r, c in ag_shapes)[wname]
                wcols = min(wcols, J * D)
                with nc.sbuf_tensor("dmp", [P, wcols], BF16) as dmp, \
                     nc.sbuf_tensor("dmpf", [P, wcols], F32) as dmpf:
                    s_dmp = sem("dmp")

                    def d1(sync):
                        sync.wait_ge(ag_sem[wname].h, 1)
                        s_dmp.inc(sync.dma_start(out=dmp[:, :],
                                                 in_=ag[wname][1][0:P, 0:wcols]), 16)

                    def d2(vector):
                        vector.wait_ge(s_dmp.h, s_dmp.n)
                        s_v.inc(vector.tensor_copy(dmpf[:, :], dmp[:, :]))
                    block.sync(d1)
                    block.vector(d2)
                    tap_out(dmpf[:, :], wcols, [(s_v, s_v.n)])
                    return nc

        barrier()

        NS = max(1, L // 512)
        SW = min(512, L)

        def spans(lo, hi):
            out = []
            a = lo
            while a < hi:
                b = min(hi, (a // 512 + 1) * 512)
                out.append((a, b))
                a = b
            return out

        agk_sem = [sem(f"agk{l}") for l in range(depth)]
        agv_sem = [sem(f"agv{l}") for l in range(depth)]
        tapret = {"stop": False}

        def do_tap(name, src_ap, ncols, wait_sems):
            if tap == name:
                tap_out(src_ap, ncols, wait_sems)
                tapret["stop"] = True

        def ph_ln(l, xn, lnscr):
            a0, v0 = s_a.n, s_v.n

            def vsum(vector):
                vector.wait_ge(s_c0.h, consts_done)
                if l > 0:
                    vector.wait_ge(bar, barn[0])
                vector.drain()
                for t in range(J):
                    s_v.inc(vector.tensor_reduce(
                        sml[:, t:t + 1], x_res[:, t * D:(t + 1) * D],
                        AX.X, ALU.add))
            block.vector(vsum)

            def asq(scalar):
                scalar.wait_ge(s_c0.h, consts_done)
                if l > 0:
                    scalar.wait_ge(bar, barn[0])
                for t in range(J):
                    s_a.inc(scalar.activation(
                        lnscr[:, :], x_res[:, t * D:(t + 1) * D], AF.Square,
                        accum_out=sml[:, 8 + t:9 + t]))
                    scalar.drain()
            block.scalar(asq)

            def vstats(vector):
                vector.wait_ge(s_a.h, a0 + J)
                vector.drain()
                vector.tensor_scalar(sml[:, 16:16 + J], sml[:, 0:J],
                                     1.0 / D, None, ALU.mult)
                vector.drain()
                vector.tensor_scalar(sml[:, 24:24 + J], sml[:, 8:8 + J],
                                     1.0 / D, None, ALU.mult)
                vector.tensor_tensor(sml[:, 32:32 + J], sml[:, 16:16 + J],
                                     sml[:, 16:16 + J], ALU.mult)
                vector.drain()
                vector.tensor_tensor(sml[:, 8:8 + J], sml[:, 24:24 + J],
                                     sml[:, 32:32 + J], ALU.subtract)
                vector.drain()
                s_v.inc(vector.tensor_scalar(sml[:, 24:24 + J], sml[:, 8:8 + J],
                                             1e-5, None, ALU.add))
            block.vector(vstats)

            def asqrt(scalar):
                scalar.wait_ge(s_v.h, v0 + J + 1)
                s_a.inc(scalar.sqrt(sml[:, 32:32 + J], sml[:, 24:24 + J]))
            block.scalar(asqrt)

            def vrecip(vector):
                vector.wait_ge(s_a.h, a0 + J + 1)
                vector.reciprocal(sml[:, 0:J], sml[:, 32:32 + J])
                vector.drain()
                s_v.inc(vector.scalar_tensor_tensor(
                    sml[:, 8:8 + J], sml[:, 16:16 + J],
                    -1.0, sml[:, 0:J], ALU.mult, ALU.mult))
            block.vector(vrecip)

            def anorm(vector):
                vector.wait_ge(s_v.h, v0 + J + 2)
                li = None
                for t in range(J):
                    li = vector.tensor_scalar(
                        xn[:, t * D:(t + 1) * D], x_res[:, t * D:(t + 1) * D],
                        sml[:, t:t + 1], sml[:, 8 + t:9 + t],
                        ALU.mult, ALU.add)
                s_v.inc(li)
            block.vector(anorm)
            return s_v.n

        def ph_transpose(src, dst, ntile, ready, src_cols=D):
            nblk = src_cols // P
            t0, v0 = s_t.n, s_v.n
            with nc.psum_tensor(U("psTTa"), [P, P], BF16) as psTTa, \
                 nc.psum_tensor(U("psTTb"), [P, P], BF16) as psTTb:
                psTT = [psTTa, psTTb]
                def tt_(tensor):
                    tensor.wait_ge(ready[0].h, ready[1])
                    i = 0
                    for dk in range(nblk):
                        for t in range(ntile):
                            par = i % 2
                            if i >= 2:
                                tensor.wait_ge(s_v.h, v0 + i - 1)
                            s_t.inc(tensor.transpose(
                                psTT[par][:, :],
                                src[:, t * src_cols + dk * P:
                                    t * src_cols + (dk + 1) * P],
                                id_s[:, :]))
                            i += 1
                block.tensor(tt_)

                def tv_(vector):
                    i = 0
                    for dk in range(nblk):
                        for t in range(ntile):
                            par = i % 2
                            vector.wait_ge(s_t.h, t0 + i + 1)
                            s_v.inc(vector.tensor_copy(
                                dst[:, dk * L + t * P:dk * L + (t + 1) * P],
                                psTT[par][:, :]))
                            i += 1
                block.vector(tv_)
            return s_v.n


        def ph_qkv(l, xn, xnT, xnT_ready_v):
            wname = "wq0" if l == 0 else "wq1"
            wfull = ag[wname][1]
            rope = (l == 1)
            ki, ko, vi, vo = kvt[l]
            w0, st0 = s_w.n, s_st.n
            sp_base = [s_sp[0].n, s_sp[1].n]
            NOB = (3 * D) // 512
            tiles = []
            tt, vv = s_t.n, s_v.n
            for ob in range(NOB):
                for ot in range(4):
                    o = ob * 4 + ot
                    for sh in range(NS):
                        d = {"ob": ob, "o": o, "sh": sh}
                        tt += 1
                        d["t_grp"] = tt
                        vv += 1
                        d["v_evac"] = vv
                        if rope and o < 32:
                            tt += 1
                            d["t_rot"] = tt
                            vv += 1
                            d["v_rot"] = vv
                        elif o >= 32:
                            nss = SW // P
                            d["t_vt"] = []
                            d["v_vt"] = []
                            for ss in range(nss):
                                tt += 1
                                d["t_vt"].append(tt)
                                vv += 1
                                d["v_vt"].append(vv)
                        d["t_done"] = tt
                        d["v_done"] = vv
                        tiles.append(d)
            with nc.sbuf_tensor(U("ropes"), [P, (2 * L) if rope else 2], F32) \
                    as rope_s, \
                 nc.sbuf_tensor(U("wbufA"), [P, 16 * 512], BF16) as wbufA, \
                 nc.sbuf_tensor(U("wbufB"), [P, 16 * 512], BF16) as wbufB, \
                 nc.sbuf_tensor(U("qs0"), [P, 2 * SW], BF16) as qs0, \
                 nc.sbuf_tensor(U("qs1"), [P, 2 * SW], BF16) as qs1, \
                 nc.sbuf_tensor(U("vst"), [P, 2 * SW], BF16) as vst, \
                 nc.sbuf_tensor(U("rm1"), [P, SW], F32) as rm1, \
                 nc.sbuf_tensor(U("rm2"), [P, SW], F32) as rm2, \
                 nc.psum_tensor(U("psQ"), [P, 2 * 512], F32) as psQ, \
                 nc.psum_tensor(U("psR"), [P, 512], F32) as psR, \
                 nc.psum_tensor(U("psVTa"), [P, P], BF16) as psVTa, \
                 nc.psum_tensor(U("psVTb"), [P, P], BF16) as psVTb:
                psVT = [psVTa, psVTb]
                wbufs = [wbufA, wbufB]
                sin_s = rope_s[:, 0:L] if rope else None
                cos_s = rope_s[:, L:2 * L] if rope else None
                s_rp = sem(U("rp"))
                if rope:
                    def rload(sync):
                        s_rp.inc(sync.dma_start(out=sin_s,
                                                in_=sin_in[:, :]), 16)
                        s_rp.inc(sync.dma_start(out=cos_s,
                                                in_=cos_in[:, :]), 16)
                    block.sync(rload)

                wq_tgt = [s_wp[0].n + 256 * (ob // 2 + 1) if ob % 2 == 0
                          else s_wp[1].n + 256 * (ob // 2 + 1)
                          for ob in range(NOB)]

                def qkv_w(scalar):
                    scalar.wait_ge(ag_sem[wname].h, 1)
                    scalar.wait_ge(s_v.h, xnT_ready_v)
                    for ob in range(NOB):
                        if ob >= 2:
                            scalar.wait_ge(s_t.h,
                                           tiles[(ob - 1) * 4 * NS - 1]["t_done"])
                        for dk in range(16):
                            s_wp[ob % 2].inc(scalar.dma_start(
                                out=wbufs[ob % 2][:, dk * 512:(dk + 1) * 512],
                                in_=wfull[ob * 512:(ob + 1) * 512,
                                          dk * P:(dk + 1) * P],
                                transpose=True), 16)
                block.scalar(qkv_w)

                def qkv_t(tensor):
                    tensor.wait_ge(s_v.h, xnT_ready_v)
                    for i, d in enumerate(tiles):
                        ob, o, sh = d["ob"], d["o"], d["sh"]
                        pp = i % 2
                        if sh == 0 and d["o"] % 4 == 0:
                            tensor.wait_ge(s_wp[ob % 2].h, wq_tgt[ob])
                        if i >= 2:
                            tensor.wait_ge(s_v.h, tiles[i - 2]["v_done"])
                        for dk in range(16):
                            mm = tensor.matmul(
                                psQ[:, pp * 512:pp * 512 + SW],
                                wbufs[ob % 2][:, dk * 512 + ot_col(o):
                                              dk * 512 + ot_col(o) + P],
                                xnT[:, dk * L + sh * 512:dk * L + sh * 512 + SW],
                                start=(dk == 0), stop=(dk == 15))
                        s_t.inc(mm)
                        if "t_rot" in d:
                            tensor.wait_ge(s_v.h, d["v_evac"])
                            s_t.inc(tensor.matmul(
                                psR[:, 0:SW], rot_s[:, :],
                                qs0[:, pp * SW:pp * SW + SW],
                                start=True, stop=True))
                        elif "t_vt" in d:
                            tensor.wait_ge(s_v.h, d["v_evac"])
                            for ss in range(len(d["t_vt"])):
                                if ss >= 2:
                                    tensor.wait_ge(s_v.h, d["v_vt"][ss - 2])
                                s_t.inc(tensor.transpose(
                                    psVT[ss % 2][:, :],
                                    qs0[:, pp * SW + ss * P:pp * SW + (ss + 1) * P],
                                    id_s[:, :]))
                block.tensor(qkv_t)

                def qkv_v(vector):
                    if rope:
                        vector.wait_ge(s_rp.h, s_rp.n)
                    for i, d in enumerate(tiles):
                        o, sh = d["o"], d["sh"]
                        pp = i % 2
                        vector.wait_ge(s_t.h, d["t_grp"])
                        if i >= 2:
                            vector.wait_ge(s_sp[i % 2].h,
                                           sp_base[i % 2] + 16 * (i // 2))
                        s_v.inc(vector.tensor_copy(
                            qs0[:, pp * SW:pp * SW + SW],
                            psQ[:, pp * 512:pp * 512 + SW]))
                        if "t_rot" in d:
                            vector.drain()
                        if "t_rot" in d:
                            vector.wait_ge(s_t.h, d["t_rot"])
                            cs = sh * 512
                            vector.tensor_tensor(
                                rm1[:, 0:SW], qs0[:, pp * SW:pp * SW + SW],
                                cos_s[:, cs:cs + SW], ALU.mult)
                            vector.tensor_tensor(
                                rm2[:, 0:SW], psR[:, 0:SW],
                                sin_s[:, cs:cs + SW], ALU.mult)
                            vector.drain()
                            s_v.inc(vector.tensor_tensor(
                                qs1[:, pp * SW:pp * SW + SW],
                                rm1[:, 0:SW], rm2[:, 0:SW], ALU.add))
                        elif "t_vt" in d:
                            for ss in range(len(d["t_vt"])):
                                vector.wait_ge(s_t.h, d["t_vt"][ss])
                                s_v.inc(vector.tensor_copy(
                                    vst[:, pp * SW + ss * P:pp * SW + (ss + 1) * P],
                                    psVT[ss % 2][:, :]))
                block.vector(qkv_v)

                def qkv_st(gpsimd):
                    for i, d in enumerate(tiles):
                        o, sh = d["o"], d["sh"]
                        pp = i % 2
                        gpsimd.wait_ge(s_v.h, d["v_done"])
                        if o < 16:
                            srcb = qs1 if rope else qs0
                            s_sp[i % 2].inc(gpsimd.dma_start(
                                out=q_stash[:, o * L + sh * 512:
                                            o * L + sh * 512 + SW],
                                in_=srcb[:, pp * SW:pp * SW + SW]), 16)
                        elif o < 32:
                            srcb = qs1 if rope else qs0
                            s_sp[i % 2].inc(gpsimd.dma_start(
                                out=ki[(o - 16) * P:(o - 15) * P,
                                       sh * 512:sh * 512 + SW],
                                in_=srcb[:, pp * SW:pp * SW + SW]), 16)
                        else:
                            s_sp[i % 2].inc(gpsimd.dma_start(
                                out=vi.ap()[o - 32, :,
                                            sh * 512:sh * 512 + SW],
                                in_=vst[:, pp * SW:pp * SW + SW]), 16)
                    if l == 0:
                        xr = xn_stash.ap().rearrange("(t p) d -> p t d", p=P)
                        s_st.inc(gpsimd.dma_start(
                            out=xr,
                            in_=xn.ap().rearrange("p (t d) -> p t d", d=D)), 16)
                    else:
                        s_st.inc(gpsimd.dma_start(
                            out=xnT_stash[:, :], in_=xnT[:, :]), 16)
                    gpsimd.wait_ge(s_sp[0].h, s_sp[0].n)
                    gpsimd.wait_ge(s_sp[1].h, s_sp[1].n)
                    gpsimd.wait_ge(s_st.h, s_st.n)
                    agk_sem[l].inc(gpsimd.collective_compute(
                        "AllGather", ALU.bypass, replica_groups=[list(range(C))],
                        ins=[ki.ap().opt()], outs=[ko.ap().opt()]))
                    agv_sem[l].inc(gpsimd.collective_compute(
                        "AllGather", ALU.bypass, replica_groups=[list(range(C))],
                        ins=[vi.ap().opt()], outs=[vo.ap().opt()]))
                block.gpsimd(qkv_st)

        def ot_col(o):
            return (o % 4) * P

        def ph_attn(l):
            ki, ko, vi, vo = kvt[l]
            with nc.sbuf_tensor(U("QT"), [P, 2 * L], BF16) as QT, \
                 nc.sbuf_tensor(U("aoT"), [P, 16 * L], BF16) as aoT:
              with nc.sbuf_tensor(U("maskb"), [P, 8 * P], BF16) as mask_s, \
                 nc.sbuf_tensor(U("KTb"), [P, 2 * NKC * P], BF16) as KTb, \
                 nc.sbuf_tensor(U("Vb"), [P, 2 * NKC * P], BF16) as Vb, \
                 nc.sbuf_tensor(U("exb"), [P, 2 * L], BF16) as exb, \
                 nc.sbuf_tensor(U("dsc"), [P, 2 * P], BF16) as dsc, \
                 nc.sbuf_tensor(U("zsb"), [1, 2 * 512], F32) as zsb, \
                 nc.sbuf_tensor(U("rzb"), [1, 2 * 512], F32) as rzb, \
                 nc.sbuf_tensor(U("bbs"), [P, 2 * 512], F32) as bbs, \
                 nc.psum_tensor(U("Sps0"), [P, L], F32) as Sps0, \
                 nc.psum_tensor(U("Sps1"), [P, L], F32) as Sps1, \
                 nc.psum_tensor(U("Ops"), [P, L], F32) as Ops, \
                 nc.psum_tensor(U("Zps0"), [1, 512], F32) as Zps0, \
                 nc.psum_tensor(U("Zps1"), [1, 512], F32) as Zps1:
                Sps = [Sps0, Sps1]
                Zps = [Zps0, Zps1]
                kv0 = s_kv.n
                s_kp = [sem(U("kpa")), sem(U("kpb"))]
                kp0 = [s_kp[0].n, s_kp[1].n]
                s_mk = sem(U("mk"))

                def mload(sync):
                    for m in range(8):
                        s_mk.inc(sync.dma_start(
                            out=mask_s[:, m * P:(m + 1) * P],
                            in_=mask_in[m, :, :]), 16)
                block.sync(mload)

                heads_plan = []
                tt, aa, vv = s_t.n, s_a.n, s_v.n
                for h in range(H):
                    plan = {"qk": [0] * NKC, "zav": [0] * NKC,
                            "exp": [], "mask": []}
                    for kc in range(NKC):
                        tt += 1
                        plan["qk"][kc] = tt
                        if kc >= 1:
                            tt += 1
                            plan["zav"][kc - 1] = tt
                    tt += 1
                    plan["zav"][NKC - 1] = tt
                    for kc in range(NKC):
                        aa += 1
                        plan["exp"].append(aa)
                        vv += 1
                        plan["mask"].append(vv)
                    tt += NS
                    plan["bmm"] = tt
                    vv += 2 * NS
                    plan["vdone"] = vv
                    heads_plan.append(plan)

                def kvload(sync):
                    sync.wait_ge(agk_sem[l].h, agk_sem[l].n)
                    sync.wait_ge(agv_sem[l].h, agv_sem[l].n)
                    for h in range(H):
                        hb = (h % 2) * NKC * P
                        if h >= 2:
                            sync.wait_ge(s_t.h, heads_plan[h - 2]["zav"][-1])
                        s_kp[h % 2].inc(sync.dma_start(
                            out=QT[:, (h % 2) * L:(h % 2 + 1) * L],
                            in_=q_stash[:, h * L:(h + 1) * L]), 16)
                        for r in range(C):
                            s_kp[h % 2].inc(sync.dma_start(
                                out=KTb[:, hb + r * J * P:hb + (r + 1) * J * P],
                                in_=ko.ap()[r, h * P:(h + 1) * P, :]), 16)
                            s_kp[h % 2].inc(sync.dma_start(
                                out=Vb[:, hb + r * J * P:hb + (r + 1) * J * P],
                                in_=vo.ap()[r, h, :, :]), 16)
                block.sync(kvload)

                def attn_t(tensor):
                    def emit_qk(plan, h, kc):
                        jmin = kc // 8
                        qlo = jmin * P
                        slot = kc % 2
                        if kc >= 2:
                            tensor.wait_ge(s_a.h, plan["exp"][kc - 2])
                        kb = (h % 2) * NKC * P + ((kc % 8) * J + kc // 8) * P
                        qb = (h % 2) * L
                        for (pa, pb) in spans(qlo, L):
                            mm = tensor.matmul(
                                Sps[slot][:, pa:pb],
                                KTb[:, kb:kb + P],
                                QT[:, qb + pa:qb + pb],
                                start=True, stop=True)
                        s_t.inc(mm)

                    def emit_zav(plan, h, kc):
                        jmin = kc // 8
                        qlo = jmin * P
                        slot = kc % 2
                        tensor.wait_ge(s_v.h, plan["mask"][kc])
                        kb = (h % 2) * NKC * P + ((kc % 8) * J + kc // 8) * P
                        for (pa, pb) in spans(qlo, L):
                            half = pa // 512
                            kc_max = 8 * min(J, (half + 1) * 4)
                            tensor.matmul(
                                Zps[half][:, pa - half * 512:pb - half * 512],
                                onec_s[:, :],
                                exb[:, slot * L + pa:slot * L + pb],
                                start=(kc == 0), stop=(kc == kc_max - 1))
                            mm = tensor.matmul(
                                Ops[:, pa:pb],
                                Vb[:, kb:kb + P],
                                exb[:, slot * L + pa:slot * L + pb],
                                start=(kc == 0), stop=(kc == kc_max - 1))
                        s_t.inc(mm)

                    for h in range(H):
                        plan = heads_plan[h]
                        tensor.wait_ge(s_kp[h % 2].h,
                                       kp0[h % 2] + 272 * (h // 2 + 1))
                        if h >= 1:
                            tensor.wait_ge(s_v.h, heads_plan[h - 1]["vdone"])
                        for kc in range(NKC):
                            emit_qk(plan, h, kc)
                            if kc >= 1:
                                emit_zav(plan, h, kc - 1)
                        emit_zav(plan, h, NKC - 1)
                        tensor.wait_ge(s_a.h, plan["exp"][NKC - 1])
                        tensor.wait_ge(s_v.h, plan["vdone"] - NS)
                        for half in range(NS):
                            s_t.inc(tensor.matmul(
                                Sps[half][:, 0:SW], onerf_s[:, :],
                                rzb[0:1, half * 512:half * 512 + SW],
                                start=True, stop=True))
                block.tensor(attn_t)

                def attn_a(scalar):
                    for h in range(H):
                        plan = heads_plan[h]
                        for kc in range(NKC):
                            jmin = kc // 8
                            qlo = jmin * P
                            slot = kc % 2
                            scalar.wait_ge(s_t.h, plan["qk"][kc])
                            if kc >= 2:
                                scalar.wait_ge(s_t.h, plan["zav"][kc - 2])
                            ai = scalar.activation(
                                dsc[:, slot * P:(slot + 1) * P],
                                Sps[slot][:, qlo:qlo + P], AF.Exp,
                                scale=1.0 / math.sqrt(DH))
                            if qlo + P < L:
                                ai = scalar.activation(
                                    exb[:, slot * L + qlo + P:slot * L + L],
                                    Sps[slot][:, qlo + P:L], AF.Exp,
                                    scale=1.0 / math.sqrt(DH))
                            s_a.inc(ai)
                block.scalar(attn_a)

                def attn_v(vector):
                    vector.wait_ge(s_mk.h, s_mk.n)
                    for h in range(H):
                        plan = heads_plan[h]
                        for kc in range(NKC):
                            jmin = kc // 8
                            qlo = jmin * P
                            slot = kc % 2
                            m = kc % 8
                            vector.wait_ge(s_a.h, plan["exp"][kc])
                            s_v.inc(vector.tensor_tensor(
                                exb[:, slot * L + qlo:slot * L + qlo + P],
                                dsc[:, slot * P:(slot + 1) * P],
                                mask_s[:, m * P:(m + 1) * P], ALU.mult))
                        for half in range(NS):
                            kc_max = 8 * min(J, (half + 1) * 4)
                            vector.wait_ge(s_t.h, plan["zav"][kc_max - 1])
                            vector.tensor_copy(zsb[0:1, half * 512:half * 512 + SW],
                                               Zps[half][0:1, 0:SW])
                            vector.drain()
                            s_v.inc(vector.reciprocal(
                                rzb[0:1, half * 512:half * 512 + SW],
                                zsb[0:1, half * 512:half * 512 + SW]))
                        for half in range(NS):
                            vector.wait_ge(s_t.h, plan["bmm"] - NS + half + 1)
                            vector.tensor_copy(bbs[:, half * 512:half * 512 + SW],
                                               Sps[half][:, 0:SW])
                            vector.drain()
                            s_v.inc(vector.tensor_tensor(
                                aoT[:, h * L + half * 512:h * L + half * 512 + SW],
                                Ops[:, half * 512:half * 512 + SW],
                                bbs[:, half * 512:half * 512 + SW], ALU.mult))
                block.vector(attn_v)
              do_tap(f"attn{l}", aoT[:, 0:min(16 * L, J * D)],
                     min(16 * L, J * D), [(s_v, s_v.n)])
              if tapret["stop"]:
                  return
              barrier()

              if True:
                woname = "wo0" if l == 0 else "wo1"
                wofull = ag[woname][1]
                t0, v0 = s_t.n, s_v.n
                wo_tgt = [s_wp[ob % 2].n + 256 * (ob // 2 + 1)
                          for ob in range(D // 512)]
                with nc.sbuf_tensor(U("wbufO"), [P, 16 * 512], BF16) as wbufO, \
                     nc.sbuf_tensor(U("wbufO2"), [P, 16 * 512], BF16) as wbufO2, \
                     nc.sbuf_tensor(U("tmpO"), [P, 2 * 512], F32) as tmpO, \
                     nc.psum_tensor(U("psO"), [P, 2 * 512], F32) as psO:
                    wobufs = [wbufO, wbufO2]
                    NOB2 = D // 512

                    def wo_w(scalar):
                        scalar.wait_ge(ag_sem[woname].h, 1)
                        scalar.wait_ge(bar, barn[0])
                        for ob in range(NOB2):
                            if ob >= 2:
                                scalar.wait_ge(s_t.h, t0 + (ob - 1) * J)
                            for dk in range(16):
                                s_wp[ob % 2].inc(scalar.dma_start(
                                    out=wobufs[ob % 2][:, dk * 512:(dk + 1) * 512],
                                    in_=wofull[ob * 512:(ob + 1) * 512,
                                               dk * P:(dk + 1) * P],
                                    transpose=True), 16)
                    block.scalar(wo_w)

                    def wo_t(tensor):
                        i = 0
                        for ob in range(NOB2):
                            tensor.wait_ge(s_wp[ob % 2].h, wo_tgt[ob])
                            for st in range(J):
                                pp = i % 2
                                if i >= 2:
                                    tensor.wait_ge(s_v.h, v0 + i - 1)
                                for dk in range(16):
                                    mm = tensor.matmul(
                                        psO[:, pp * 512:(pp + 1) * 512],
                                        aoT[:, dk * L + st * P:dk * L + (st + 1) * P],
                                        wobufs[ob % 2][:, dk * 512:(dk + 1) * 512],
                                        start=(dk == 0), stop=(dk == 15))
                                s_t.inc(mm)
                                i += 1
                    block.tensor(wo_t)

                    def wo_v(vector):
                        vector.drain()
                        i = 0
                        for ob in range(NOB2):
                            for st in range(J):
                                pp = i % 2
                                vector.wait_ge(s_t.h, t0 + i + 1)
                                xsl = x_res[:, st * D + ob * 512:
                                            st * D + (ob + 1) * 512]
                                tsl = tmpO[:, pp * 512:(pp + 1) * 512]
                                vector.tensor_tensor(
                                    tsl, psO[:, pp * 512:(pp + 1) * 512],
                                    xsl, ALU.add)
                                vector.drain()
                                s_v.inc(vector.tensor_copy(xsl, tsl))
                                i += 1
                    block.vector(wo_v)

        def ph_ffn0():
            NH = HID // 512
            NHK = HID // P
            HF = min(4, J)
            nhalves = (J + HF - 1) // HF
            if True:
                s_xn = sem("ffn0xn")
                with nc.sbuf_tensor(U("xqT"), [P, 16 * L], BF16) as xqT:
                    with nc.sbuf_tensor(U("xnt_in"), [P, D], BF16) as xnt_in, \
                         nc.sbuf_tensor(U("xqt"), [P, D], BF16) as xqt, \
                         nc.sbuf_tensor(U("qsf"), [P, 512], F32) as qsf, \
                         nc.sbuf_tensor(U("qsi"), [P, 512], I32) as qsi, \
                         nc.psum_tensor(U("psTqa"), [P, P], BF16) as psTqa, \
                         nc.psum_tensor(U("psTqb"), [P, P], BF16) as psTqb:
                        psTq = [psTqa, psTqb]
                        tt, vv = s_t.n, s_v.n
                        q_plan = []
                        for t in range(J):
                            vv += 1
                            d = {"vq": vv, "t_vt": [], "v_vt": []}
                            for dk in range(16):
                                tt += 1
                                d["t_vt"].append(tt)
                                vv += 1
                                d["v_vt"].append(vv)
                            q_plan.append(d)

                        def xload(sync):
                            sync.wait_ge(bar, barn[0])
                            for t in range(J):
                                if t >= 1:
                                    sync.wait_ge(s_v.h, q_plan[t - 1]["vq"])
                                s_xn.inc(sync.dma_start(
                                    out=xnt_in[:, :],
                                    in_=xn_stash[t * P:(t + 1) * P, :]), 16)
                        block.sync(xload)

                        def q_v(vector):
                            for t in range(J):
                                vector.wait_ge(s_xn.h, 16 * (t + 1))
                                vector.tensor_reduce(
                                    sml[:, 48:49], xnt_in[:, :], AX.X, ALU.max,
                                    apply_absolute_value=True)
                                vector.drain()
                                vector.tensor_scalar(sml[:, 49:50], sml[:, 48:49],
                                                     1e-5, None, ALU.max)
                                vector.drain()
                                vector.reciprocal(sml[:, 50:51], sml[:, 49:50])
                                vector.drain()
                                vector.tensor_scalar(sml[:, 51:52], sml[:, 50:51],
                                                     127.0, None, ALU.mult)
                                vector.tensor_scalar(sml[:, 24 + t:25 + t],
                                                     sml[:, 49:50],
                                                     1.0 / 127.0, None, ALU.mult)
                                vector.drain()
                                vector.tensor_scalar(sml[:, 32 + t:33 + t],
                                                     sml[:, 24 + t:25 + t],
                                                     sml[:, 52:53], None, ALU.mult)
                                vector.tensor_scalar(sml[:, 40 + t:41 + t],
                                                     sml[:, 24 + t:25 + t],
                                                     sml[:, 53:54], None, ALU.mult)
                                vector.drain()
                                li = None
                                for ch in range(4):
                                    sl = slice(ch * 512, (ch + 1) * 512)
                                    vector.tensor_scalar(qsf[:, :], xnt_in[:, sl],
                                                         sml[:, 51:52], None,
                                                         ALU.mult)
                                    vector.drain()
                                    vector.tensor_copy(qsi[:, :], qsf[:, :])
                                    vector.drain()
                                    li = vector.tensor_scalar(xqt[:, sl], qsi[:, :],
                                                              127, -128,
                                                              ALU.min, ALU.max)
                                s_v.inc(li)
                                for dk in range(16):
                                    vector.wait_ge(s_t.h, q_plan[t]["t_vt"][dk])
                                    s_v.inc(vector.tensor_copy(
                                        xqT[:, dk * L + t * P:dk * L + (t + 1) * P],
                                        psTq[dk % 2][:, :]))
                        block.vector(q_v)

                        def q_t(tensor):
                            for t in range(J):
                                tensor.wait_ge(s_v.h, q_plan[t]["vq"])
                                for dk in range(16):
                                    if dk >= 2:
                                        tensor.wait_ge(s_v.h,
                                                       q_plan[t]["v_vt"][dk - 2])
                                    s_t.inc(tensor.transpose(
                                        psTq[dk % 2][:, :],
                                        xqt[:, dk * P:(dk + 1) * P],
                                        id_s[:, :]))
                        block.tensor(q_t)
                    xqT_ready_v = s_v.n

                    do_tap("xq0", xqT[:, 0:min(16 * L, J * D)],
                           min(16 * L, J * D), [(s_v, xqT_ready_v)])
                    if tapret["stop"]:
                        return

                    with nc.sbuf_tensor(U("wb1"), [P, 2 * 16 * 512], BF16) as wb1, \
                         nc.sbuf_tensor(U("wb3"), [P, 16 * 512], BF16) as wb3, \
                         nc.sbuf_tensor(U("hbuf"), [P, HF * HID], BF16) as hbuf, \
                         nc.sbuf_tensor(U("tm1"), [P, 512], F32) as tm1, \
                         nc.sbuf_tensor(U("hqf"), [P, 128], F32) as hqf, \
                         nc.sbuf_tensor(U("hqi"), [P, 128], I32) as hqi, \
                         nc.sbuf_tensor(U("hq"), [P, HID], BF16) as hq, \
                         nc.sbuf_tensor(U("hstg"), [P, 11 * P], BF16) as hstg, \
                         nc.psum_tensor(U("psB1"), [P, 512], F32) as psB1, \
                         nc.psum_tensor(U("psB3"), [P, 512], F32) as psB3, \
                         nc.psum_tensor(U("psHTa"), [P, P], BF16) as psHTa, \
                         nc.psum_tensor(U("psHTb"), [P, P], BF16) as psHTb:
                        psHT = [psHTa, psHTb]
                        w0, st0 = s_w.n, s_st.n
                        s_b3 = sem("fb3")
                        spb = [s_wp[0].n, s_wp[1].n]
                        tt, aa, vv = s_t.n, s_a.n, s_v.n
                        plan_mm = {}
                        plan_q = {}
                        for hf in range(nhalves):
                            sts = list(range(hf * HF, min(J, (hf + 1) * HF)))
                            for hg in range(NH):
                                for si, st in enumerate(sts):
                                    tt += 1
                                    aa += 1
                                    vv += 1
                                    plan_mm[(hf, hg, si)] = {
                                        "t": tt, "a": aa, "v": vv}
                            for si, st in enumerate(sts):
                                vv += 1
                                q = {"vq": vv, "t_vt": [], "v_vt": []}
                                for hk in range(NHK):
                                    tt += 1
                                    q["t_vt"].append(tt)
                                    vv += 1
                                    q["v_vt"].append(vv)
                                plan_q[(hf, si)] = q

                        groups_f = [(hf, hg) for hf in range(nhalves)
                                    for hg in range(NH)]
                        b3_0 = s_b3.n

                        def f_w(scalar):
                            scalar.wait_ge(ag_sem["w1"].h, 1)
                            scalar.wait_ge(ag_sem["w3"].h, 1)
                            scalar.wait_ge(s_v.h, xqT_ready_v)
                            for g, (hf, hg) in enumerate(groups_f):
                                pg = (g % 2) * 16 * 512
                                if g >= 2:
                                    phf, phg = groups_f[g - 2]
                                    nsi = len(list(range(
                                        phf * HF, min(J, (phf + 1) * HF))))
                                    scalar.wait_ge(
                                        s_t.h,
                                        plan_mm[(phf, phg, nsi - 1)]["t"])
                                for dk in range(16):
                                    s_wp[g % 2].inc(scalar.dma_start(
                                        out=wb1[:, pg + dk * 512:
                                                pg + (dk + 1) * 512],
                                        in_=ag["w1"][1][hg * 512:(hg + 1) * 512,
                                                        dk * P:(dk + 1) * P],
                                        transpose=True), 16)
                                if g >= 1:
                                    phf, phg = groups_f[g - 1]
                                    nsi = len(list(range(
                                        phf * HF, min(J, (phf + 1) * HF))))
                                    scalar.wait_ge(
                                        s_t.h,
                                        plan_mm[(phf, phg, nsi - 1)]["t"])
                                for dk in range(16):
                                    s_b3.inc(scalar.dma_start(
                                        out=wb3[:, dk * 512:(dk + 1) * 512],
                                        in_=ag["w3"][1][hg * 512:(hg + 1) * 512,
                                                        dk * P:(dk + 1) * P],
                                        transpose=True), 16)
                        block.sync(f_w)

                        def f_t(tensor):
                            i = 0
                            for hf in range(nhalves):
                                sts = list(range(hf * HF, min(J, (hf + 1) * HF)))
                                for hg in range(NH):
                                    pg = (i % 2) * 16 * 512
                                    tensor.wait_ge(
                                        s_wp[i % 2].h,
                                        spb[i % 2] + 256 * (i // 2 + 1))
                                    i += 1
                                    tensor.wait_ge(s_b3.h, b3_0 + 256 * i)
                                    for si, st in enumerate(sts):
                                        key = (hf, hg, si)
                                        if not (hf == 0 and hg == 0 and si == 0):
                                            tensor.wait_ge(
                                                s_v.h, plan_mm[key]["v"] - 1)
                                        for dk in range(16):
                                            tensor.matmul(
                                                psB1[:, :],
                                                xqT[:, dk * L + st * P:
                                                    dk * L + (st + 1) * P],
                                                wb1[:, pg + dk * 512:
                                                    pg + (dk + 1) * 512],
                                                start=(dk == 0), stop=(dk == 15))
                                        for dk in range(16):
                                            mm = tensor.matmul(
                                                psB3[:, :],
                                                xqT[:, dk * L + st * P:
                                                    dk * L + (st + 1) * P],
                                                wb3[:, dk * 512:(dk + 1) * 512],
                                                start=(dk == 0), stop=(dk == 15))
                                        s_t.inc(mm)
                                for si, st in enumerate(sts):
                                    q = plan_q[(hf, si)]
                                    tensor.wait_ge(s_v.h, q["vq"])
                                    for hk in range(NHK):
                                        if hk >= 2:
                                            tensor.wait_ge(s_v.h,
                                                           q["v_vt"][hk - 2])
                                        s_t.inc(tensor.transpose(
                                            psHT[hk % 2][:, :],
                                            hq[:, hk * P:(hk + 1) * P],
                                            id_s[:, :]))
                        block.tensor(f_t)

                        def f_a(scalar):
                            for hf in range(nhalves):
                                sts = list(range(hf * HF, min(J, (hf + 1) * HF)))
                                for hg in range(NH):
                                    for si, st in enumerate(sts):
                                        key = (hf, hg, si)
                                        scalar.wait_ge(s_t.h, plan_mm[key]["t"])
                                        ai = scalar.activation(
                                            tm1[:, :], psB1[:, :], AF.Silu,
                                            scale=sml[:, 32 + st:33 + st])
                                        s_a.inc(ai)
                        block.scalar(f_a)

                        def f_v(vector):
                            for hf in range(nhalves):
                                sts = list(range(hf * HF, min(J, (hf + 1) * HF)))
                                for hg in range(NH):
                                    for si, st in enumerate(sts):
                                        key = (hf, hg, si)
                                        vector.wait_ge(s_a.h, plan_mm[key]["a"])
                                        s_v.inc(vector.scalar_tensor_tensor(
                                            hbuf[:, si * HID + hg * 512:
                                                 si * HID + (hg + 1) * 512],
                                            psB3[:, :],
                                            sml[:, 40 + st:41 + st],
                                            tm1[:, :], ALU.mult, ALU.mult))
                                for si, st in enumerate(sts):
                                    q = plan_q[(hf, si)]
                                    nprev = hf * HF + si
                                    if nprev > 0:
                                        vector.wait_ge(s_st.h,
                                                       st0 + 64 * nprev)
                                    hrow = hbuf[:, si * HID:(si + 1) * HID]
                                    vector.drain()
                                    vector.tensor_reduce(
                                        sml[:, 48:49], hrow, AX.X, ALU.max,
                                        apply_absolute_value=True)
                                    vector.drain()
                                    vector.tensor_scalar(sml[:, 49:50],
                                                         sml[:, 48:49],
                                                         1e-5, None, ALU.max)
                                    vector.drain()
                                    vector.reciprocal(sml[:, 50:51],
                                                      sml[:, 49:50])
                                    vector.drain()
                                    vector.tensor_scalar(sml[:, 51:52],
                                                         sml[:, 50:51],
                                                         127.0, None, ALU.mult)
                                    vector.tensor_scalar(sml[:, 55:56],
                                                         sml[:, 49:50],
                                                         1.0 / 127.0, None,
                                                         ALU.mult)
                                    vector.drain()
                                    vector.tensor_scalar(sml[:, 56 + st:57 + st],
                                                         sml[:, 55:56],
                                                         sml[:, 54:55], None,
                                                         ALU.mult)
                                    li = None
                                    for hg2 in range(4 * NH):
                                        vector.tensor_scalar(
                                            hqf[:, :],
                                            hrow[:, hg2 * 128:(hg2 + 1) * 128],
                                            sml[:, 51:52], None, ALU.mult)
                                        vector.drain()
                                        vector.tensor_copy(hqi[:, :], hqf[:, :])
                                        vector.drain()
                                        li = vector.tensor_scalar(
                                            hq[:, hg2 * 128:(hg2 + 1) * 128],
                                            hqi[:, :], 127, -128,
                                            ALU.min, ALU.max)
                                    s_v.inc(li)
                                    for hk in range(NHK):
                                        if hk >= 11 and hk % 11 == 0:
                                            vector.wait_ge(
                                                s_st.h,
                                                st0 + 64 * nprev +
                                                16 * (hk // 11))
                                        vector.wait_ge(s_t.h, q["t_vt"][hk])
                                        s_v.inc(vector.tensor_copy(
                                            hstg[:, (hk % 11) * P:
                                                 (hk % 11 + 1) * P],
                                            psHT[hk % 2][:, :]))
                        block.vector(f_v)

                        def f_st(gpsimd):
                            for hf in range(nhalves):
                                sts = list(range(hf * HF, min(J, (hf + 1) * HF)))
                                for si, st in enumerate(sts):
                                    q = plan_q[(hf, si)]
                                    for qt in range(4):
                                        gpsimd.wait_ge(
                                            s_v.h,
                                            q["v_vt"][(qt + 1) * 11 - 1])
                                        s_st.inc(gpsimd.dma_start(
                                            out=hT_stash[:, st * NHK * P +
                                                         qt * 11 * P:
                                                         st * NHK * P +
                                                         (qt + 1) * 11 * P],
                                            in_=hstg[:, :]), 16)
                        block.gpsimd(f_st)
                        hstores = s_st.n

                with nc.sbuf_tensor(U("w2buf"), [P, 2 * NHK * 512], BF16) as w2buf, \
                     nc.sbuf_tensor(U("hq1"), [P, 2 * NHK * P], BF16) as hq1, \
                     nc.sbuf_tensor(U("tmp2b"), [P, 2 * 512], F32) as tmp2b, \
                     nc.psum_tensor(U("psW2"), [P, 2 * 512], F32) as psW2:
                    w0, t0, v0 = s_w.n, s_t.n, s_v.n
                    s_w2p = [sem("w2pa"), sem("w2pb")]
                    w2p0 = [s_w2p[0].n, s_w2p[1].n]
                    h2_tgt = [s_hp[i % 2].n + 16 * (i // 2 + 1)
                              for i in range(4 * J)]

                    def w2_w(scalar):
                        scalar.wait_ge(ag_sem["w2"].h, 1)
                        scalar.wait_ge(s_st.h, hstores)
                        for og in range(4):
                            po = (og % 2) * NHK * 512
                            if og >= 2:
                                scalar.wait_ge(s_t.h, t0 + (og - 1) * J)
                            for hk in range(NHK):
                                s_w2p[og % 2].inc(scalar.dma_start(
                                    out=w2buf[:, po + hk * 512:
                                              po + (hk + 1) * 512],
                                    in_=ag["w2"][1][og * 512:(og + 1) * 512,
                                                    hk * P:(hk + 1) * P],
                                    transpose=True), 16)
                    block.scalar(w2_w)

                    def w2_h(sync):
                        sync.wait_ge(s_st.h, hstores)
                        i = 0
                        for og in range(4):
                            for st in range(J):
                                if i >= 2:
                                    sync.wait_ge(s_t.h, t0 + i - 1)
                                s_hp[i % 2].inc(sync.dma_start(
                                    out=hq1[:, (i % 2) * NHK * P:
                                            (i % 2 + 1) * NHK * P],
                                    in_=hT_stash[:, st * NHK * P:
                                                 (st + 1) * NHK * P]), 16)
                                i += 1
                    block.sync(w2_h)

                    def w2_t(tensor):
                        i = 0
                        for og in range(4):
                            po = (og % 2) * NHK * 512
                            tensor.wait_ge(s_w2p[og % 2].h,
                                           w2p0[og % 2] + 16 * NHK * (og // 2 + 1))
                            for st in range(J):
                                pp = i % 2
                                tensor.wait_ge(s_hp[i % 2].h, h2_tgt[i])
                                if i >= 2:
                                    tensor.wait_ge(s_v.h, v0 + i - 1)
                                for hk in range(NHK):
                                    mm = tensor.matmul(
                                        psW2[:, pp * 512:(pp + 1) * 512],
                                        hq1[:, pp * NHK * P + hk * P:
                                            pp * NHK * P + (hk + 1) * P],
                                        w2buf[:, po + hk * 512:
                                              po + (hk + 1) * 512],
                                        start=(hk == 0), stop=(hk == NHK - 1))
                                s_t.inc(mm)
                                i += 1
                    block.tensor(w2_t)

                    def w2_v(vector):
                        vector.drain()
                        i = 0
                        for og in range(4):
                            for st in range(J):
                                pp = i % 2
                                vector.wait_ge(s_t.h, t0 + i + 1)
                                xsl = x_res[:, st * D + og * 512:
                                            st * D + (og + 1) * 512]
                                tsl = tmp2b[:, pp * 512:(pp + 1) * 512]
                                vector.scalar_tensor_tensor(
                                    tsl, psW2[:, pp * 512:(pp + 1) * 512],
                                    sml[:, 56 + st:57 + st], xsl,
                                    ALU.mult, ALU.add)
                                vector.drain()
                                s_v.inc(vector.tensor_copy(xsl, tsl))
                                i += 1
                    block.vector(w2_v)

        def ph_ffn1():
            NH = HID // 512
            NHK = HID // P
            HF = min(4, J)
            nhalves = (J + HF - 1) // HF
            s_x1 = sem("ffn1xn")
            with nc.sbuf_tensor(U("xnT1"), [P, 16 * L], BF16) as xnT1:
                def x1load(sync):
                    sync.wait_ge(bar, barn[0])
                    s_x1.inc(sync.dma_start(
                        out=xnT1[:, :], in_=xnT_stash[:, :]), 16)
                block.sync(x1load)

                with nc.sbuf_tensor(U("wbg"), [P, 2 * 16 * 512], BF16) as wbg, \
                     nc.sbuf_tensor(U("wbu"), [P, 16 * 512], BF16) as wbu, \
                     nc.sbuf_tensor(U("h1buf"), [P, HF * HID], BF16) as h1buf, \
                     nc.sbuf_tensor(U("tg1"), [P, 512], F32) as tg1, \
                     nc.sbuf_tensor(U("h1stg"), [P, (NHK // 2) * P], BF16) as h1stg, \
                     nc.psum_tensor(U("psG"), [P, 512], F32) as psG, \
                     nc.psum_tensor(U("psU"), [P, 512], F32) as psU, \
                     nc.psum_tensor(U("psHT1a"), [P, P], BF16) as psHT1a, \
                     nc.psum_tensor(U("psHT1b"), [P, P], BF16) as psHT1b:
                    psHT1 = [psHT1a, psHT1b]
                    w0, st0 = s_w.n, s_st.n
                    s_u1 = sem("gup")
                    spb = [s_wp[0].n, s_wp[1].n]
                    tt, aa, vv = s_t.n, s_a.n, s_v.n
                    plan_mm = {}
                    plan_q = {}
                    for hf in range(nhalves):
                        sts = list(range(hf * HF, min(J, (hf + 1) * HF)))
                        for hg in range(NH):
                            for si, st in enumerate(sts):
                                tt += 1
                                aa += 1
                                vv += 1
                                plan_mm[(hf, hg, si)] = {"t": tt, "a": aa, "v": vv}
                        for si, st in enumerate(sts):
                            q = {"t_vt": [], "v_vt": []}
                            for hk in range(NHK):
                                tt += 1
                                q["t_vt"].append(tt)
                                vv += 1
                                q["v_vt"].append(vv)
                            plan_q[(hf, si)] = q

                    groups_g = [(hf, hg) for hf in range(nhalves)
                                for hg in range(NH)]
                    u1_0 = s_u1.n

                    def g_w(scalar):
                        scalar.wait_ge(ag_sem["wgu"].h, 1)
                        scalar.wait_ge(s_x1.h, s_x1.n)
                        for g, (hf, hg) in enumerate(groups_g):
                            pg = (g % 2) * 16 * 512
                            if g >= 2:
                                phf, phg = groups_g[g - 2]
                                nsi = len(list(range(
                                    phf * HF, min(J, (phf + 1) * HF))))
                                scalar.wait_ge(
                                    s_t.h,
                                    plan_mm[(phf, phg, nsi - 1)]["t"])
                            for dk in range(16):
                                s_wp[g % 2].inc(scalar.dma_start(
                                    out=wbg[:, pg + dk * 512:pg + (dk + 1) * 512],
                                    in_=ag["wgu"][1][hg * 512:(hg + 1) * 512,
                                                     dk * P:(dk + 1) * P],
                                    transpose=True), 16)
                            if g >= 1:
                                phf, phg = groups_g[g - 1]
                                nsi = len(list(range(
                                    phf * HF, min(J, (phf + 1) * HF))))
                                scalar.wait_ge(
                                    s_t.h,
                                    plan_mm[(phf, phg, nsi - 1)]["t"])
                            for dk in range(16):
                                s_u1.inc(scalar.dma_start(
                                    out=wbu[:, dk * 512:(dk + 1) * 512],
                                    in_=ag["wgu"][1][HID + hg * 512:
                                                     HID + (hg + 1) * 512,
                                                     dk * P:(dk + 1) * P],
                                    transpose=True), 16)
                    block.sync(g_w)

                    def g_t(tensor):
                        tensor.wait_ge(s_x1.h, s_x1.n)
                        i = 0
                        for hf in range(nhalves):
                            sts = list(range(hf * HF, min(J, (hf + 1) * HF)))
                            for hg in range(NH):
                                pg = (i % 2) * 16 * 512
                                tensor.wait_ge(
                                    s_wp[i % 2].h,
                                    spb[i % 2] + 256 * (i // 2 + 1))
                                i += 1
                                tensor.wait_ge(s_u1.h, u1_0 + 256 * i)
                                for si, st in enumerate(sts):
                                    key = (hf, hg, si)
                                    if not (hf == 0 and hg == 0 and si == 0):
                                        tensor.wait_ge(s_v.h,
                                                       plan_mm[key]["v"] - 1)
                                    for dk in range(16):
                                        tensor.matmul(
                                            psG[:, :],
                                            xnT1[:, dk * L + st * P:
                                                 dk * L + (st + 1) * P],
                                            wbg[:, pg + dk * 512:
                                                pg + (dk + 1) * 512],
                                            start=(dk == 0), stop=(dk == 15))
                                    for dk in range(16):
                                        mm = tensor.matmul(
                                            psU[:, :],
                                            xnT1[:, dk * L + st * P:
                                                 dk * L + (st + 1) * P],
                                            wbu[:, dk * 512:(dk + 1) * 512],
                                            start=(dk == 0), stop=(dk == 15))
                                    s_t.inc(mm)
                            for si, st in enumerate(sts):
                                q = plan_q[(hf, si)]
                                tensor.wait_ge(s_v.h,
                                               plan_mm[(hf, NH - 1, si)]["v"])
                                for hk in range(NHK):
                                    if hk >= 2:
                                        tensor.wait_ge(s_v.h, q["v_vt"][hk - 2])
                                    elif si > 0:
                                        tensor.wait_ge(
                                            s_v.h,
                                            plan_q[(hf, si - 1)]["v_vt"][
                                                NHK - 2 + hk])
                                    s_t.inc(tensor.transpose(
                                        psHT1[hk % 2][:, :],
                                        h1buf[:, si * HID + hk * P:
                                              si * HID + (hk + 1) * P],
                                        id_s[:, :]))
                    block.tensor(g_t)

                    def g_a(scalar):
                        for hf in range(nhalves):
                            sts = list(range(hf * HF, min(J, (hf + 1) * HF)))
                            for hg in range(NH):
                                for si, st in enumerate(sts):
                                    key = (hf, hg, si)
                                    scalar.wait_ge(s_t.h, plan_mm[key]["t"])
                                    ai = scalar.activation(tg1[:, :], psG[:, :],
                                                           AF.Silu)
                                    s_a.inc(ai)
                    block.scalar(g_a)

                    def g_v(vector):
                        for hf in range(nhalves):
                            sts = list(range(hf * HF, min(J, (hf + 1) * HF)))
                            for hg in range(NH):
                                for si, st in enumerate(sts):
                                    key = (hf, hg, si)
                                    vector.wait_ge(s_a.h, plan_mm[key]["a"])
                                    s_v.inc(vector.scalar_tensor_tensor(
                                        h1buf[:, si * HID + hg * 512:
                                              si * HID + (hg + 1) * 512],
                                        psU[:, :], 1.0,
                                        tg1[:, :], ALU.mult, ALU.mult))
                            for si, st in enumerate(sts):
                                q = plan_q[(hf, si)]
                                nprev = hf * HF + si
                                if nprev > 0:
                                    vector.wait_ge(s_st.h, st0 + 32 * nprev)
                                for hk in range(NHK):
                                    if hk == NHK // 2:
                                        vector.wait_ge(s_st.h,
                                                       st0 + 32 * nprev + 16)
                                    vector.wait_ge(s_t.h, q["t_vt"][hk])
                                    s_v.inc(vector.tensor_copy(
                                        h1stg[:, (hk % (NHK // 2)) * P:
                                              (hk % (NHK // 2) + 1) * P],
                                        psHT1[hk % 2][:, :]))
                    block.vector(g_v)

                    def g_st(gpsimd):
                        HK2 = NHK // 2
                        for hf in range(nhalves):
                            sts = list(range(hf * HF, min(J, (hf + 1) * HF)))
                            for si, st in enumerate(sts):
                                q = plan_q[(hf, si)]
                                for half in range(2):
                                    gpsimd.wait_ge(
                                        s_v.h,
                                        q["v_vt"][(half + 1) * HK2 - 1])
                                    s_st.inc(gpsimd.dma_start(
                                        out=hT_stash[:, st * NHK * P +
                                                     half * HK2 * P:
                                                     st * NHK * P +
                                                     (half + 1) * HK2 * P],
                                        in_=h1stg[:, :]), 16)
                    block.gpsimd(g_st)
                    h1stores = s_st.n

            with nc.sbuf_tensor(U("wdnb"), [P, (HID // P) * 512], BF16) as wdnb, \
                 nc.sbuf_tensor(U("hq2"), [P, 2 * (HID // P) * P], BF16) as hq2, \
                 nc.sbuf_tensor(U("y_all"), [P, J * D], BF16) as y_all, \
                 nc.sbuf_tensor(U("rscr"), [P, D], F32) as rscr, \
                 nc.sbuf_tensor(U("tmp3"), [P, 2 * 512], F32) as tmp3, \
                 nc.psum_tensor(U("psDN"), [P, 2 * 512], F32) as psDN:
                NHK = HID // P
                w0, t0, v0, a0 = s_w.n, s_t.n, s_v.n, s_a.n
                dn_tgt = [s_hp[i % 2].n + 16 * (i // 2 + 1)
                          for i in range(4 * J)]

                def dn_w(scalar):
                    scalar.wait_ge(ag_sem["wdn"].h, 1)
                    scalar.wait_ge(s_st.h, h1stores)
                    for og in range(4):
                        if og >= 1:
                            scalar.wait_ge(s_t.h, t0 + og * J)
                        for hk in range(NHK):
                            s_w.inc(scalar.dma_start(
                                out=wdnb[:, hk * 512:(hk + 1) * 512],
                                in_=ag["wdn"][1][og * 512:(og + 1) * 512,
                                                 hk * P:(hk + 1) * P],
                                transpose=True), 16)
                block.scalar(dn_w)

                def dn_h(sync):
                    sync.wait_ge(s_st.h, h1stores)
                    i = 0
                    for og in range(4):
                        for st in range(J):
                            if i >= 2:
                                sync.wait_ge(s_t.h, t0 + i - 1)
                            s_hp[i % 2].inc(sync.dma_start(
                                out=hq2[:, (i % 2) * NHK * P:
                                        (i % 2 + 1) * NHK * P],
                                in_=hT_stash[:, st * NHK * P:
                                             (st + 1) * NHK * P]), 16)
                            i += 1
                block.sync(dn_h)

                def dn_t(tensor):
                    i = 0
                    for og in range(4):
                        tensor.wait_ge(s_w.h, w0 + 16 * NHK * (og + 1))
                        for st in range(J):
                            pp = i % 2
                            tensor.wait_ge(s_hp[i % 2].h, dn_tgt[i])
                            if i >= 2:
                                tensor.wait_ge(s_v.h, v0 + i - 1)
                            for hk in range(NHK):
                                mm = tensor.matmul(
                                    psDN[:, pp * 512:(pp + 1) * 512],
                                    hq2[:, pp * NHK * P + hk * P:
                                        pp * NHK * P + (hk + 1) * P],
                                    wdnb[:, hk * 512:(hk + 1) * 512],
                                    start=(hk == 0), stop=(hk == NHK - 1))
                            s_t.inc(mm)
                            i += 1
                block.tensor(dn_t)

                def dn_v(vector):
                    i = 0
                    for og in range(4):
                        for st in range(J):
                            pp = i % 2
                            vector.wait_ge(s_t.h, t0 + i + 1)
                            s_v.inc(vector.tensor_copy(
                                y_all[:, st * D + og * 512:st * D + (og + 1) * 512],
                                psDN[:, pp * 512:(pp + 1) * 512]))
                            i += 1
                block.vector(dn_v)
                yall_done_v = s_v.n

                def rms_a(scalar):
                    scalar.wait_ge(s_v.h, yall_done_v)
                    for st in range(J):
                        s_a.inc(scalar.activation(
                            rscr[:, :], y_all[:, st * D:(st + 1) * D], AF.Square,
                            accum_out=sml[:, st:st + 1]))
                        scalar.drain()
                block.scalar(rms_a)

                def rms_v(vector):
                    vector.wait_ge(s_a.h, a0 + J)
                    s_v.inc(vector.tensor_scalar(sml[:, 8:8 + J], sml[:, 0:J],
                                                 1.0 / D, 1e-6, ALU.mult, ALU.add))
                block.vector(rms_v)

                def rms_a2(scalar):
                    scalar.wait_ge(s_v.h, s_v.n)
                    s_a.inc(scalar.sqrt(sml[:, 16:16 + J], sml[:, 8:8 + J]))
                block.scalar(rms_a2)

                def rms_v2(vector):
                    vector.wait_ge(s_a.h, s_a.n)
                    vector.reciprocal(sml[:, 24:24 + J], sml[:, 16:16 + J])
                    vector.drain()
                    ii = 0
                    for st in range(J):
                        for og in range(4):
                            xsl = x_res[:, st * D + og * 512:
                                        st * D + (og + 1) * 512]
                            t3 = tmp3[:, (ii % 2) * 512:(ii % 2 + 1) * 512]
                            vector.scalar_tensor_tensor(
                                t3,
                                y_all[:, st * D + og * 512:st * D + (og + 1) * 512],
                                sml[:, 24 + st:25 + st], xsl, ALU.mult, ALU.add)
                            vector.drain()
                            li = vector.tensor_copy(xsl, t3)
                            ii += 1
                    s_v.inc(li)
                block.vector(rms_v2)

        if tap is None or not tap.startswith("prep"):
            for l in range(depth):
                with nc.sbuf_tensor(f"xn{l}", [P, J * D], BF16) as xn, \
                     nc.sbuf_tensor(f"lnscr{l}", [P, D], F32) as lnscr:
                    xn_ready = ph_ln(l, xn, lnscr)
                    with nc.sbuf_tensor(f"xnT{l}", [P, 16 * L], BF16) as xnT:
                        xnT_ready = ph_transpose(xn, xnT, J, (s_v, xn_ready))
                        ph_qkv(l, xn, xnT, xnT_ready)
                barrier()
                if l == 0:
                    ph_attn(0)
                    if tapret["stop"]:
                        return nc
                    barrier()
                    if tap != "L0":
                        ph_ffn0()
                        if tapret["stop"]:
                            return nc
                        barrier()
                else:
                    ph_ffn1()
                    barrier()
                    ph_attn(1)
                    if tapret["stop"]:
                        return nc
                    barrier()
                if tap == "L0" and l == 0:
                    break
                if tap == "L0a" and l == 0:
                    break

        def ph_out(sync):
            sync.wait_ge(bar, barn[0])
            dst = y_out.ap().rearrange("(t p) d -> p t d", p=P)
            s_o.inc(sync.dma_start(
                out=dst,
                in_=x_res.ap().rearrange("p (t d) -> p t d", d=D)), 16)
            sync.wait_ge(s_o.h, s_o.n)
        block.sync(ph_out)

    return nc



def make_inputs(inputs, J=8):
    L = J * P
    x = np.asarray(inputs["L0_attn_x"])[0]
    in_maps = []
    for c in range(C):
        tiles = [x[(c + 8 * j) * P:(c + 8 * j + 1) * P] for j in range(J)]
        xs = np.ascontiguousarray(np.concatenate(tiles, axis=0))
        pos = np.concatenate([np.arange((c + 8 * j) * P, (c + 8 * j + 1) * P)
                              for j in range(J)])
        sin2, cos2 = _rope_tables(pos)
        m = {
            "x_in": xs.astype(np.float32),
            "sin_in": sin2, "cos_in": cos2,
            "mask_in": _masks(c),
            "wq0_s": np.ascontiguousarray(inputs["L0_attn_w_project"][c * (3 * D // C):(c + 1) * (3 * D // C)]),
            "wo0_s": np.ascontiguousarray(inputs["L0_attn_w_out"][c * (D // C):(c + 1) * (D // C)]),
            "w1_s": np.ascontiguousarray(inputs["L0_ffn_w1"][c * (HID // C):(c + 1) * (HID // C)]),
            "w3_s": np.ascontiguousarray(inputs["L0_ffn_w3"][c * (HID // C):(c + 1) * (HID // C)]),
            "w2_s": np.ascontiguousarray(inputs["L0_ffn_w2"][c * (D // C):(c + 1) * (D // C)]),
            "wq1_s": np.ascontiguousarray(inputs["L1_attn_Wqkv"][c * (3 * D // C):(c + 1) * (3 * D // C)]),
            "wo1_s": np.ascontiguousarray(inputs["L1_attn_WO"][c * (D // C):(c + 1) * (D // C)]),
            "gup_s": np.ascontiguousarray(inputs["L1_ffn_gate_up_packed"][c * (GU // C):(c + 1) * (GU // C)]),
            "gus_s": np.ascontiguousarray(np.asarray(inputs["L1_ffn_gate_up_scales"])[:, c * (GU // C):(c + 1) * (GU // C)]),
            "dnp_s": np.ascontiguousarray(inputs["L1_ffn_down_packed"][c * (D // C):(c + 1) * (D // C)]),
            "dns_s": np.ascontiguousarray(np.asarray(inputs["L1_ffn_down_scales"])[:, c * (D // C):(c + 1) * (D // C)]),
        }
        in_maps.append(m)
    return in_maps


def assemble(results, J=8):
    S = C * J * P
    out = np.empty((1, S, D), dtype=np.float32)
    for c in range(C):
        y = results[c]["y_out"]
        for j in range(J):
            out[0, (c + 8 * j) * P:(c + 8 * j + 1) * P] = y[j * P:(j + 1) * P]
    return out


def kernel(**inputs):
    nc = build(J=8, depth=2)
    in_maps = make_inputs(inputs, J=8)
    res = run_bass_kernel_spmd(nc, in_maps, core_ids=list(range(C)))
    return assemble(res.results, J=8)
